# version 02
# baseline (speedup 1.0000x reference)
import contextlib
import numpy as np
import ml_dtypes
import concourse.bass as bass
import concourse.mybir as mybir
from concourse.bass_utils import run_bass_kernel_spmd
from concourse.bass_core_rust import IndirectOffsetOnAxis

F32 = mybir.dt.float32
BF16 = mybir.dt.bfloat16
I32 = mybir.dt.int32
AF = mybir.ActivationFunctionType
ALU = mybir.AluOpType
AX = mybir.AxisListType
BF = ml_dtypes.bfloat16

S = 16384
D = 1024
NCORES = 8
TPC = S // NCORES
DC = D // 128
NEG = -30000.0


class Prog:

    def __init__(self, nc):
        self.nc = nc
        self.ops = []
        self.last_writer = {}
        self.readers = {}
        self.barrier_deps = set()

    NDMASEM = 12

    @staticmethod
    def semkey(o):
        if o["dma"]:
            import zlib
            h = zlib.adler32(repr(o["writes0"]).encode()) % Prog.NDMASEM
            return ("D", o["engine"], h)
        return ("C", o["engine"])

    def barrier(self):
        last = {}
        for o in self.ops:
            last[Prog.semkey(o)] = o["i"]
        self.barrier_deps = set(last.values())

    def op(self, engine, fn, reads=(), writes=(), dma=False, tag=""):
        deps = set(self.barrier_deps)
        for k in reads:
            w = self.last_writer.get(k)
            if w is not None:
                deps.add(w)
        for k in writes:
            w = self.last_writer.get(k)
            if w is not None:
                deps.add(w)
            for r in self.readers.get(k, ()):
                deps.add(r)
        i = len(self.ops)
        wl = list(writes)
        self.ops.append(dict(i=i, engine=engine, fn=fn, deps=deps, dma=dma,
                             tag=tag, signals=False,
                             writes0=(wl[0] if wl else tag)))
        for k in reads:
            self.readers.setdefault(k, []).append(i)
        for k in writes:
            self.last_writer[k] = i
            self.readers[k] = [i]
        return i

    def dma(self, engine, out, in_, tag="", reads=(), writes=(), **kw):
        def fn(eng, out=out, in_=in_, kw=kw):
            return eng.dma_start(out=out, in_=in_, **kw)
        return self.op(engine, fn, reads=reads, writes=writes, dma=True, tag=tag)

    def dma_tr(self, engine, out, in_, tag="", reads=(), writes=()):
        def fn(eng, out=out, in_=in_):
            return eng.dma_start_transpose(out, in_)
        return self.op(engine, fn, reads=reads, writes=writes, dma=True, tag=tag)

    def emit(self, stack):
        nc = self.nc

        for o in self.ops:
            o["sem"] = Prog.semkey(o)

        def needs_edge(dep, o):
            if dep["dma"]:
                return True
            if dep["engine"] != o["engine"] or o["dma"]:
                return True
            return dep["engine"] != "tensor"

        for o in self.ops:
            if o["dma"]:
                o["signals"] = True
            for d in o["deps"]:
                dep = self.ops[d]
                if needs_edge(dep, o):
                    dep["signals"] = True
        cum = {}
        for o in self.ops:
            s = o["sem"]
            if o["signals"]:
                cum[s] = cum.get(s, 0) + (16 if o["dma"] else 1)
            o["cum"] = cum.get(s, 0)
        sems = {}
        for s in cum:
            sems[s] = stack.enter_context(
                nc.semaphore(f"sem_{'_'.join(str(x) for x in s)}"))
        est = {}
        for o in self.ops:
            est.setdefault(o["engine"], []).append(o)
        for eng_name, ops in est.items():
            inherited = {}
            for o in ops:
                waits = {}

                def add_wait(s, v):
                    if v > inherited.get(s, 0):
                        waits[s] = max(waits.get(s, 0), v)

                for d in o["deps"]:
                    dep = self.ops[d]
                    if not needs_edge(dep, o):
                        continue
                    add_wait(dep["sem"], dep["cum"])
                if o["dma"]:
                    prev = o["cum"] - 16
                    add_wait(o["sem"], prev)
                o["waits"] = waits
                for s, v in waits.items():
                    inherited[s] = max(inherited.get(s, 0), v)

        totals = dict(cum)

        if getattr(self, "debug_dump", None):
            with open(self.debug_dump, "w") as f:
                for eng_name, ops in est.items():
                    f.write(f"==== engine {eng_name} ====\n")
                    for o in ops:
                        w = " ".join(f"wait({s},{v})" for s, v in
                                     sorted(o["waits"].items(), key=str))
                        sg = f" inc[{o['sem']}]->{o['cum']}" if o["signals"] else ""
                        f.write(f"  [{o['i']:5d}] {o['tag']:<16} {w}{sg}\n")

        with nc.Block() as block:
            def make_body(engine):
                def body(eng):
                    for o in est.get(engine, []):
                        for s, v in sorted(o["waits"].items(), key=str):
                            eng.wait_ge(sems[s], v)
                        inst = o["fn"](eng)
                        if o["signals"]:
                            if inst is None:
                                raise RuntimeError(f"op '{o['tag']}' returned no instruction")
                            inst.then_inc(sems[o["sem"]], 16 if o["dma"] else 1)
                    for s, t in totals.items():
                        if s[0] == "D" and s[1] == engine:
                            eng.wait_ge(sems[s], t)
                return body
            block.tensor(make_body("tensor"))
            block.vector(make_body("vector"))
            block.scalar(make_body("scalar"))
            block.gpsimd(make_body("gpsimd"))
            block.sync(make_body("sync"))


def ceil_div(a, b):
    return (a + b - 1) // b


def t_matmul(ps_ap, lhsT_ap, rhs_ap, start, stop, tile_position=None):
    def fn(eng, a=ps_ap, l=lhsT_ap, r=rhs_ap, st=start, sp=stop, tp=tile_position):
        return eng.matmul(a, l, r, start=st, stop=sp, tile_position=tp)
    return fn


def t_transpose(ps_ap, in_ap, ident_ap):
    def fn(eng, a=ps_ap, i=in_ap, d=ident_ap):
        return eng.transpose(a, i, d)
    return fn


def v_tt(out_ap, a_ap, b_ap, op):
    def fn(eng, o=out_ap, a=a_ap, b=b_ap, op=op):
        return eng.tensor_tensor(o, a, b, op)
    return fn


def v_ts(out_ap, a_ap, s1, s2=None, op0=ALU.mult, op1=ALU.add):
    def fn(eng, o=out_ap, a=a_ap, s1=s1, s2=s2, op0=op0, op1=op1):
        if s2 is None:
            return eng.tensor_single_scalar(o, a, s1, op0)
        return eng.tensor_scalar(o, a, s1, s2, op0, op1)
    return fn


def v_copy(out_ap, in_ap):
    def fn(eng, o=out_ap, a=in_ap):
        return eng.tensor_copy(o, a)
    return fn


def v_red(out_ap, in_ap, op="max"):
    def fn(eng, o=out_ap, a=in_ap, op=op):
        if op == "max":
            return eng.reduce_max(o, a, axis=AX.X)
        return eng.reduce_sum(o, a, axis=AX.X)
    return fn


def v_recip(out_ap, in_ap):
    def fn(eng, o=out_ap, a=in_ap):
        return eng.reciprocal(o, a)
    return fn


def v_memset(ap, val):
    def fn(eng, a=ap, v=val):
        return eng.memset(a, v)
    return fn


def v_select(out_ap, mask_ap, t_ap, f_ap):
    def fn(eng, o=out_ap, m=mask_ap, t=t_ap, f=f_ap):
        return eng.select(o, m, t, f)
    return fn


def s_copy(out_ap, in_ap):
    def fn(eng, o=out_ap, a=in_ap):
        return eng.copy(o, a)
    return fn


def s_act(out_ap, in_ap, func, bias=0.0, scale=1.0, accum=None):
    def fn(eng, o=out_ap, a=in_ap, f=func, b=bias, s=scale, acc=accum):
        return eng.activation(o, a, f, bias=b, scale=s, accum_out=acc)
    return fn


class Core:
    def __init__(self, name="prog"):
        self.nc = bass.Bass("TRN2", target_bir_lowering=False, debug=False)
        self.P = Prog(self.nc)
        self.stack = contextlib.ExitStack()
        self.stack_root = self.stack
        self.inputs = []

    def inp(self, name, shape, dtype=F32):
        self.inputs.append(name)
        return self.nc.dram_tensor(name, list(shape), dtype, kind="ExternalInput").ap()

    def out(self, name, shape, dtype=F32):
        return self.nc.dram_tensor(name, list(shape), dtype, kind="ExternalOutput").ap()

    def _nm(self, name):
        self._uid = getattr(self, "_uid", 0) + 1
        return f"{name}_u{self._uid}"

    def sb(self, name, shape, dtype=F32):
        return self.stack.enter_context(
            self.nc.sbuf_tensor(self._nm(name), list(shape), dtype))

    def sb_right(self, name, shape, dtype=F32):
        return self.stack_root.enter_context(
            self.nc.sbuf_tensor(self._nm(name), list(shape), dtype, side="right"))

    def psum(self, name, cols=512, dtype=F32):
        nbank = ceil_div(cols * (4 if dtype == F32 else 2), 2048)
        alloc_cols = nbank * (512 if dtype == F32 else 1024)
        t = self.stack.enter_context(
            self.nc.psum_tensor(self._nm(name), [128, alloc_cols], dtype))
        return t

    @contextlib.contextmanager
    def phase(self):
        st = contextlib.ExitStack()
        outer = self.stack
        self.stack = st
        try:
            yield
        finally:
            self.stack = outer
            st.close()
            self.P.barrier()

    def finish(self):
        self.P.emit(self.stack)
        self.stack.close()
        return self.nc


SCVALS = [1e-5, -0.5, 0.125, -0.125, 0.5, 1.0, -1.0, 2.0, -2.0, 8.0, -8.0,
          -30000.0, 0.03125, -0.03125, 1e-30, 1e-12]


def host_consts():
    c = {}
    c["scvals"] = np.tile(np.asarray(SCVALS, np.float32)[None, :], (128, 1))
    c["ones1"] = np.ones((1, 128), np.float32)
    c["negokm"] = np.full((128, 1), -1.0 / D, np.float32)
    c["onesokm"] = np.full((128, 1), 1.0 / D, np.float32)
    c["negokmb"] = np.full((128, 1), -1.0 / D, BF)
    c["onesokmb"] = np.full((128, 1), 1.0 / D, BF)
    c["identb"] = np.eye(128, dtype=BF)
    c["identf8"] = np.eye(8, dtype=np.float32)
    return c


_CONST_SPECS = {
    "scvals": ([128, len(SCVALS)], F32),
    "ones1": ([1, 128], F32),
    "negokm": ([128, 1], F32),
    "onesokm": ([128, 1], F32),
    "negokmb": ([128, 1], BF16),
    "onesokmb": ([128, 1], BF16),
    "identb": ([128, 128], BF16),
    "identf8": ([8, 8], F32),
}


def dev_consts(core: Core, extra=None):
    sb = {}
    specs = dict(_CONST_SPECS)
    if extra:
        specs.update(extra)
    for k, (shape, dt) in specs.items():
        d = core.inp(k, shape, dt)
        t = core.sb(f"c_{k}", shape, dt)
        sb[k] = t[:]
        core.P.dma("sync", sb[k], d, writes=[("c", k)], tag=f"c_{k}")
    for i, v in enumerate(SCVALS):
        core.nc.const_aps.register(F32, v, sb["scvals"][:, i:i + 1])
    core.P.barrier()
    return sb


class WeightLoader:

    def __init__(self, core: Core, stage_cols=5504, conv_engines=("vector", "scalar"),
                 consts=None, nbuf=2):
        self.c = core
        self.conv_engines = conv_engines
        self.consts = consts
        self._n = 0
        self.stage = [core.sb(f"wstage{i}", [128, stage_cols], F32) for i in range(nbuf)]

    def load(self, dram_ap, name, rows, cols, dma_engine="sync", dst=None):
        c = self.c
        assert rows % 128 == 0
        RC = rows // 128
        if dst is None:
            dst = c.sb(f"w_{name}", [128, RC, cols], BF16)
        for rc in range(RC):
            j = self._n % len(self.stage)
            st = self.stage[j]
            stkey = ("wstage", j)
            self._n += 1
            c.P.dma(dma_engine, st[:, 0:cols], dram_ap[rc * 128:(rc + 1) * 128, :],
                    reads=[stkey], writes=[stkey], tag=f"ldw_{name}_{rc}")
            eng = self.conv_engines[self._n % len(self.conv_engines)]
            mk = s_copy if eng == "scalar" else v_copy
            c.P.op(eng, mk(dst[:, rc, :], st[:, 0:cols]),
                   reads=[stkey], writes=[(name, rc)], tag=f"cvt_{name}_{rc}")
        return dst

    def load_bias_cols(self, dram_ap, name, n, dma_engine="sync"):
        c = self.c
        NCOL = ceil_div(n, 128)
        dst = c.sb(f"b_{name}", [128, NCOL], F32)
        c.P.dma(dma_engine, dst[:], dram_ap, writes=[(name, "bias")],
                tag=f"ldb_{name}")
        return dst


def emit_row_bcast(core, ps_ap, row_ap, consts, reads, writes, mult=False):
    core.P.op("tensor", t_matmul(ps_ap, consts["ones1"], row_ap, start=True, stop=True),
              reads=list(reads) + [("c", "ones1")], writes=list(writes), tag="bcast")


class LNState:
    def __init__(self, negmu, rstd, mukey, rkey, var=None, varkey=None):
        self.negmu = negmu
        self.rstd = rstd
        self.mukey = mukey
        self.rkey = rkey
        self.var = var
        self.varkey = varkey


def emit_ln_stats(core, x_ap_fn, xkeys, T, consts, name, xdt=BF16, tg=512,
                  root=False):
    P = core.P
    rsb = core.sb_right if root else core.sb
    negmu = rsb(f"ln_negmu_{name}", [1, T])
    msq = core.sb(f"ln_msq_{name}", [1, T], F32)
    sq = [core.sb(f"ln_sq_{name}{i}", [128, tg], xdt) for i in range(2)]
    ps = core.psum(f"ln_ps_{name}")
    pskey = f"ln_ps_{name}"
    negokm = consts["negokm"] if xdt == F32 else consts["negokmb"]
    onesokm = consts["onesokm"] if xdt == F32 else consts["onesokmb"]
    for g in range(ceil_div(T, tg)):
        t0, t1 = g * tg, min((g + 1) * tg, T)
        n = t1 - t0
        for dc in range(DC):
            P.op("tensor", t_matmul(ps[0:1, 0:n], negokm, x_ap_fn(dc, t0, t1),
                                    start=dc == 0, stop=dc == DC - 1),
                 reads=[xkeys(dc), ("c", "negokm")], writes=[pskey], tag=f"ln_mu_{name}")
        for dc in range(DC):
            j = dc % 2
            P.op("scalar", s_act(sq[j][:, 0:n], x_ap_fn(dc, t0, t1), AF.Square),
                 reads=[xkeys(dc), (name, "sq", j)], writes=[(name, "sq", j)],
                 tag=f"ln_sq_{name}")
            P.op("tensor", t_matmul(ps[32:33, 0:n], onesokm, sq[j][:, 0:n],
                                    start=dc == 0, stop=dc == DC - 1),
                 reads=[(name, "sq", j), ("c", "onesokm")], writes=[pskey],
                 tag=f"ln_ms_{name}")
        P.op("vector", v_copy(negmu[0:1, t0:t1], ps[0:1, 0:n]),
             reads=[pskey], writes=[(name, "negmu")], tag="ln_ev")
        P.op("vector", v_copy(msq[0:1, t0:t1], ps[32:33, 0:n]),
             reads=[pskey], writes=[(name, "msq")], tag="ln_ev2")
    var = rsb(f"ln_var_{name}", [1, T])
    P.op("vector", v_tt(var[:], negmu[:], negmu[:], ALU.mult),
         reads=[(name, "negmu")], writes=[(name, "var")], tag="ln_mu2")
    P.op("vector", v_tt(var[:], msq[:], var[:], ALU.subtract),
         reads=[(name, "msq"), (name, "var")], writes=[(name, "var")], tag="ln_var")
    lnv = core.sb(f"ln_lnv_{name}", [1, T], F32)
    P.op("scalar", s_act(lnv[:], var[:], AF.Ln, bias=1e-5),
         reads=[(name, "var")], writes=[(name, "lnv")], tag="ln_ln")
    rstd = rsb(f"ln_rstd_{name}", [1, T])
    P.op("scalar", s_act(rstd[:], lnv[:], AF.Exp, scale=-0.5),
         reads=[(name, "lnv")], writes=[(name, "rstd")], tag="ln_exp")
    return LNState(negmu[:], rstd[:], (name, "negmu"), (name, "rstd"),
                   var=var[:], varkey=(name, "var"))


def emit_ln_compose(core, lnst, name, T, root=False):
    P = core.P
    rsb = core.sb_right if root else core.sb
    vz = core.sb(f"lnc_vz_{name}", [1, T], F32)
    P.op("vector", v_tt(vz[:], lnst.rstd, lnst.rstd, ALU.mult),
         reads=[lnst.rkey], writes=[(name, "vz")], tag="lnc_r2")
    P.op("vector", v_tt(vz[:], vz[:], lnst.var, ALU.mult),
         reads=[(name, "vz"), lnst.varkey], writes=[(name, "vz")], tag="lnc_vz")
    P.op("scalar", s_act(vz[:], vz[:], AF.Ln, bias=1e-5),
         reads=[(name, "vz")], writes=[(name, "vz")], tag="lnc_ln")
    P.op("scalar", s_act(vz[:], vz[:], AF.Exp, scale=-0.5),
         reads=[(name, "vz")], writes=[(name, "vz")], tag="lnc_exp")
    R = rsb(f"lnc_R_{name}", [1, T])
    P.op("vector", v_tt(R[:], vz[:], lnst.rstd, ALU.mult),
         reads=[(name, "vz"), lnst.rkey], writes=[(name, "R")], tag="lnc_R")
    return LNState(lnst.negmu, R[:], lnst.mukey, (name, "R"))


class RowsDram:

    def __init__(self, core, name, T, nrows=3):
        self.core = core
        self.name = name
        self.T = T
        self.d = core.nc.dram_tensor(f"rows_{name}", [nrows, T], F32,
                                     kind="Internal").ap()
        self.tmp = core.sb_right(f"rowtmp_{name}", [1, 512], F32)
        self._key = (name, "rowtmp")

    def bcast(self, ps_ap, pskey, idx, t0, t1, consts):
        c = self.core
        n = t1 - t0
        c.P.dma("sync", self.tmp[0:1, 0:n], self.d[idx:idx + 1, t0:t1],
                reads=[self._key, (self.name, "row", idx)], writes=[self._key],
                tag=f"rowld_{self.name}")
        emit_row_bcast(c, ps_ap[:, 0:n], self.tmp[0:1, 0:n], consts,
                       reads=[self._key], writes=[pskey])


def emit_ln_stats_dram(core, x_ap_fn, xkeys, T, consts, name, xdt=BF16, tg=512,
                       compose=True):
    P = core.P
    rows = RowsDram(core, name, T, nrows=3 if compose else 2)
    ra = core.sb(f"ln_ra_{name}", [1, tg], F32)
    rb = core.sb(f"ln_rb_{name}", [1, tg], F32)
    rc = core.sb(f"ln_rc_{name}", [1, tg], F32)
    sq = [core.sb(f"ln_sq_{name}{i}", [128, tg], xdt) for i in range(2)]
    ps = core.psum(f"ln_ps_{name}")
    pskey = f"ln_ps_{name}"
    negokm = consts["negokm"] if xdt == F32 else consts["negokmb"]
    onesokm = consts["onesokm"] if xdt == F32 else consts["onesokmb"]
    ka, kb, kc2 = (name, "ra"), (name, "rb"), (name, "rc")
    for g in range(ceil_div(T, tg)):
        t0, t1 = g * tg, min((g + 1) * tg, T)
        n = t1 - t0
        for dc in range(DC):
            P.op("tensor", t_matmul(ps[0:1, 0:n], negokm, x_ap_fn(dc, t0, t1),
                                    start=dc == 0, stop=dc == DC - 1),
                 reads=[xkeys(dc), ("c", "negokm")], writes=[pskey], tag=f"ln_mu_{name}")
        for dc in range(DC):
            j = dc % 2
            P.op("scalar", s_act(sq[j][:, 0:n], x_ap_fn(dc, t0, t1), AF.Square),
                 reads=[xkeys(dc), (name, "sq", j)], writes=[(name, "sq", j)],
                 tag=f"ln_sq_{name}")
            P.op("tensor", t_matmul(ps[32:33, 0:n], onesokm, sq[j][:, 0:n],
                                    start=dc == 0, stop=dc == DC - 1),
                 reads=[(name, "sq", j), ("c", "onesokm")], writes=[pskey],
                 tag=f"ln_ms_{name}")
        P.op("vector", v_copy(ra[0:1, 0:n], ps[0:1, 0:n]),
             reads=[pskey, ka], writes=[ka], tag="ln_ev")
        P.op("vector", v_copy(rb[0:1, 0:n], ps[32:33, 0:n]),
             reads=[pskey, kb], writes=[kb], tag="ln_ev2")
        P.dma("gpsimd", rows.d[0:1, t0:t1], ra[0:1, 0:n],
              reads=[ka], writes=[(name, "row", 0)], tag="ln_st0")
        P.op("vector", v_tt(rc[0:1, 0:n], ra[0:1, 0:n], ra[0:1, 0:n], ALU.mult),
             reads=[ka, kc2], writes=[kc2], tag="ln_mu2")
        P.op("vector", v_tt(rb[0:1, 0:n], rb[0:1, 0:n], rc[0:1, 0:n], ALU.subtract),
             reads=[kb, kc2], writes=[kb], tag="ln_var")
        P.op("scalar", s_act(rc[0:1, 0:n], rb[0:1, 0:n], AF.Ln, bias=1e-5),
             reads=[kb, kc2], writes=[kc2], tag="ln_ln")
        P.op("scalar", s_act(rc[0:1, 0:n], rc[0:1, 0:n], AF.Exp, scale=-0.5),
             reads=[kc2], writes=[kc2], tag="ln_exp")
        P.dma("gpsimd", rows.d[1:2, t0:t1], rc[0:1, 0:n],
              reads=[kc2], writes=[(name, "row", 1)], tag="ln_st1")
        if compose:
            P.op("vector", v_tt(rb[0:1, 0:n], rb[0:1, 0:n], rc[0:1, 0:n], ALU.mult),
                 reads=[kb, kc2], writes=[kb], tag="lnc_1")
            P.op("vector", v_tt(rb[0:1, 0:n], rb[0:1, 0:n], rc[0:1, 0:n], ALU.mult),
                 reads=[kb, kc2], writes=[kb], tag="lnc_2")
            P.op("scalar", s_act(rb[0:1, 0:n], rb[0:1, 0:n], AF.Ln, bias=1e-5),
                 reads=[kb], writes=[kb], tag="lnc_3")
            P.op("scalar", s_act(rb[0:1, 0:n], rb[0:1, 0:n], AF.Exp, scale=-0.5),
                 reads=[kb], writes=[kb], tag="lnc_4")
            P.op("vector", v_tt(rb[0:1, 0:n], rb[0:1, 0:n], rc[0:1, 0:n], ALU.mult),
                 reads=[kb, kc2], writes=[kb], tag="lnc_5")
            P.dma("gpsimd", rows.d[2:3, t0:t1], rb[0:1, 0:n],
                  reads=[kb], writes=[(name, "row", 2)], tag="ln_st2")
    return rows


def emit_ln_apply_dram(core, x_ap_fn, xkeys, rows, muidx, ridx, consts,
                       out_ap_fn, outkey_fn, t0, t1, psA, psAkey, psB, psBkey,
                       gcol_fn=None, bcol_fn=None, gbkeys=()):
    P = core.P
    n = t1 - t0
    rows.bcast(psA, psAkey, muidx, t0, t1, consts)
    rows.bcast(psB, psBkey, ridx, t0, t1, consts)
    for dc in range(DC):
        dst = out_ap_fn(dc)
        ok = outkey_fn(dc)
        P.op("vector", v_tt(dst, x_ap_fn(dc, t0, t1), psA[:, 0:n], ALU.add),
             reads=[xkeys(dc), psAkey], writes=[ok], tag="ln_sub")
        P.op("vector", v_tt(dst, dst, psB[:, 0:n], ALU.mult),
             reads=[ok, psBkey], writes=[ok], tag="ln_mul")
        if gcol_fn is not None:
            P.op("vector", v_ts(dst, dst, gcol_fn(dc), bcol_fn(dc),
                                op0=ALU.mult, op1=ALU.add),
                 reads=[ok] + list(gbkeys), writes=[ok], tag="ln_gb")


def emit_ln_apply_group(core, x_ap_fn, xkeys, lnst, consts, out_ap_fn, outkey_fn,
                        t0, t1, psA, psAkey, psB, psBkey,
                        gcol_fn=None, bcol_fn=None, gbkeys=()):
    P = core.P
    n = t1 - t0
    emit_row_bcast(core, psA[:, 0:n], lnst.negmu[0:1, t0:t1], consts,
                   reads=[lnst.mukey], writes=[psAkey])
    emit_row_bcast(core, psB[:, 0:n], lnst.rstd[0:1, t0:t1], consts,
                   reads=[lnst.rkey], writes=[psBkey])
    for dc in range(DC):
        dst = out_ap_fn(dc)
        ok = outkey_fn(dc)
        P.op("vector", v_tt(dst, x_ap_fn(dc, t0, t1), psA[:, 0:n], ALU.add),
             reads=[xkeys(dc), psAkey], writes=[ok], tag="ln_sub")
        P.op("vector", v_tt(dst, dst, psB[:, 0:n], ALU.mult),
             reads=[ok, psBkey], writes=[ok], tag="ln_mul")
        if gcol_fn is not None:
            P.op("vector", v_ts(dst, dst, gcol_fn(dc), bcol_fn(dc),
                                op0=ALU.mult, op1=ALU.add),
                 reads=[ok] + list(gbkeys), writes=[ok], tag="ln_gb")


L0A_HALO = 512
L0A_TEXT = TPC + 2 * L0A_HALO

def l0a_slot_qtiles(k):
    i_off = -256 + 256 * k
    out = []
    for j in range(4):
        q = i_off - 128 + 128 * j
        if 0 <= q < TPC:
            out.append((q, j))
    return out


def build_l0attn(dbg=False):
    core = Core("l0attn")
    P = core.P
    nc = core.nc

    x0f = core.inp("x0f", [D, L0A_TEXT], F32)
    wq_d = core.inp("wq", [D, D], F32)
    wk_d = core.inp("wk", [D, D], F32)
    wv_d = core.inp("wv", [D, D], F32)
    wproj_d = core.inp("wproj", [D, D], F32)
    w1_d = core.inp("w1", [D, 4 * D], F32)
    w2_d = core.inp("w2", [4 * D, D], F32)
    bq_d = core.inp("bq", [128, DC], F32)
    bk_d = core.inp("bk", [128, DC], F32)
    bv_d = core.inp("bv", [1, D], F32)
    bproj_d = core.inp("bproj", [128, DC], F32)
    b1_d = core.inp("b1", [128, 32], F32)
    b2_d = core.inp("b2", [128, DC], F32)
    ln1g_d = core.inp("ln1g", [128, DC], F32)
    ln1b_d = core.inp("ln1b", [128, DC], F32)
    ln2g_d = core.inp("ln2g", [128, DC], F32)
    ln2b_d = core.inp("ln2b", [128, DC], F32)
    tents_d = core.inp("tents", [128, 40], F32)
    wmask_d = core.inp("wmask", [10, 512], F32)

    x1f = core.out("x1f", [128, DC, TPC], F32)
    if dbg:
        dbg_ctx = core.out("dbg_ctx", [128, DC, TPC], BF16)
        dbg_out = core.out("dbg_outf", [128, DC, TPC], F32)
        dbg_qf = core.out("dbg_qf", [128, DC, 512], BF16)
        dbg_kf = core.out("dbg_kf", [128, DC, 512], BF16)
        dbg_p = core.out("dbg_p", [128, 512], BF16)
        dbg_xn = core.out("dbg_xn", [128, DC, 512], BF16)
        dbg_rows = core.out("dbg_rows", [3, L0A_TEXT], F32)

    consts = dev_consts(core)
    tents = core.sb("tents_sb", [128, 40], F32)
    P.dma("sync", tents[:], tents_d, writes=[("tents",)], tag="ld_tents")
    wmtmp = core.sb("wmtmp", [1, 512], F32)

    ctxb = core.sb("ctxb", [128, DC, TPC], BF16)
    with core.phase():
        wl = WeightLoader(core, stage_cols=1024, consts=consts)
        wq = wl.load(wq_d, "wq", D, D)
        wk = wl.load(wk_d, "wk", D, D)
        wv = wl.load(wv_d, "wv", D, D)
        bqc = wl.load_bias_cols(bq_d, "bq", D)
        bkc = wl.load_bias_cols(bk_d, "bk", D)
        ln1gc = wl.load_bias_cols(ln1g_d, "ln1g", D)
        ln1bc = wl.load_bias_cols(ln1b_d, "ln1b", D)
        bvrow = core.sb("bvrow", [1, D], F32)
        P.dma("sync", bvrow[:], bv_d, writes=[("bvrow",)], tag="ld_bv")

        xb = core.sb("xb", [128, DC, L0A_TEXT], BF16)
        for dc in range(DC):
            for cc in range(3):
                j = (dc * 3 + cc) % 2
                st = wl.stage[j]
                P.dma("sync" if cc % 2 == 0 else "scalar", st[:, 0:1024],
                      x0f[dc * 128:(dc + 1) * 128, cc * 1024:(cc + 1) * 1024],
                      reads=[("wstage", j)], writes=[("wstage", j)], tag="ldx")
                P.op("vector", v_copy(xb[:, dc, cc * 1024:(cc + 1) * 1024],
                                      st[:, 0:1024]),
                     reads=[("wstage", j)], writes=[("xb", dc)], tag="cvt_x")

        with core.phase():
            rows1 = emit_ln_stats_dram(core, lambda d, a, b: xb[:, d, a:b],
                                       lambda d: ("xb", d), L0A_TEXT, consts, "ln1",
                                       xdt=BF16, compose=True)

        bvb = core.sb("bvb", [128, D], F32)
        psA = core.psum("psA")
        psB = core.psum("psB")
        for half in range(2):
            emit_row_bcast(core, psA[:, 0:512], bvrow[0:1, half * 512:(half + 1) * 512],
                           consts, reads=[("bvrow",)], writes=[("psA",)])
            P.op("vector", v_copy(bvb[:, half * 512:(half + 1) * 512], psA[:, 0:512]),
                 reads=[("psA",)], writes=[("bvb",)], tag="bvb")

        for dc in range(DC):
            P.op("vector", v_memset(ctxb[:, dc, :], 0.0),
                 writes=[("ctx", dc)], tag="ctx0")

        xnk = [core.sb(f"xnk{j}", [128, DC, 512], BF16) for j in range(2)]
        kfb = core.sb("kfb", [128, DC, 512], BF16)
        vtb = core.sb("vtb", [128, 4, D], BF16)
        qfb = core.sb("qfb", [128, DC, 512], BF16)
        pb = core.sb("pb", [128, 512], BF16)
        ptb = core.sb("ptb", [128, 2, 4, 512], BF16)
        mcol = core.sb("mcol", [128, 4], F32)
        psS = [core.psum(f"psS{j}") for j in range(2)]
        psK = core.psum("psK")
        psV = core.psum("psV", cols=1024)
        psO = core.psum("psO")

        for k in range(10):
            ke0 = 256 * k + 128
            qts = l0a_slot_qtiles(k)
            q0 = qts[0][0] + L0A_HALO
            qn = len(qts)
            wj = k % 2
            emit_ln_apply_dram(core, lambda d, a, b: xb[:, d, a:b], lambda d: ("xb", d),
                               rows1, 0, 2, consts,
                               lambda d, w=wj: xnk[w][:, d, 0:512],
                               lambda d, w=wj: ("xnk", w, d),
                               ke0, ke0 + 512, psA, ("psA",), psB, ("psB",),
                               gcol_fn=lambda d: ln1gc[:, d:d + 1],
                               bcol_fn=lambda d: ln1bc[:, d:d + 1],
                               gbkeys=[("ln1g", "bias"), ("ln1b", "bias")])
            P.dma("sync", wmtmp[0:1, :], wmask_d[k:k + 1, :],
                  reads=[("wm",)], writes=[("wm",)], tag="ld_wm")
            qrel = q0 - ke0
            for (w, x0_, dstb, dstkey, bc, bckey, wname, width) in (
                    (wq, qrel, qfb, ("qfb",), bqc, ("bq", "bias"), "wq", qn * 128),
                    (wk, 0, kfb, ("kfb",), bkc, ("bk", "bias"), "wk", 512)):
                for oc in range(DC):
                    for ki in range(DC):
                        P.op("tensor", t_matmul(psK[:, 0:width], w[:, ki, oc * 128:(oc + 1) * 128],
                                                xnk[wj][:, ki, x0_:x0_ + width],
                                                start=ki == 0, stop=ki == DC - 1),
                             reads=[(wname, ki), ("xnk", wj, ki)], writes=[("psK",)], tag="qk_mm")
                    P.op("vector", v_ts(dstb[:, oc, 0:width], psK[:, 0:width],
                                        bc[:, oc:oc + 1], None, op0=ALU.add),
                         reads=[("psK",), bckey], writes=[dstkey + (oc,)], tag="qk_ev")
            for tt in range(4):
                for half in range(2):
                    for ki in range(DC):
                        P.op("tensor", t_matmul(psV[:, half * 512:(half + 1) * 512],
                                                xnk[wj][:, ki, tt * 128:(tt + 1) * 128],
                                                wv[:, ki, half * 512:(half + 1) * 512],
                                                start=ki == 0, stop=ki == DC - 1),
                             reads=[("wv", ki), ("xnk", wj, ki)], writes=[("psV",)], tag="v_mm")
                for half in range(2):
                    P.op("vector", v_tt(vtb[:, tt, half * 512:(half + 1) * 512],
                                        psV[:, half * 512:(half + 1) * 512],
                                        bvb[:, half * 512:(half + 1) * 512], ALU.add),
                         reads=[("psV",), ("bvb",)], writes=[("vtb", tt)], tag="v_ev")
            for h in range(16):
                hp, hh = h // 2, (h % 2) * 64
                pt_j = h % 2
                for qi, (qoff, tcls) in enumerate(qts):
                    ps = psS[(h * 4 + qi) % 2]
                    pskey = ("psS", (h * 4 + qi) % 2)
                    P.op("tensor", t_matmul(ps[:, 0:512], consts["ones1"],
                                            wmtmp[0:1, :], start=True, stop=False),
                         reads=[("wm",), ("c", "ones1")], writes=[pskey], tag="s_mask")
                    P.op("tensor", t_matmul(ps[:, 0:512],
                                            qfb[hh:hh + 64, hp, qi * 128:(qi + 1) * 128],
                                            kfb[hh:hh + 64, hp, 0:512],
                                            start=False, stop=True),
                         reads=[("qfb", hp), ("kfb", hp)], writes=[pskey], tag="s_mm")
                    P.op("vector", v_red(mcol[:, 0:1], ps[:, 0:512], "max"),
                         reads=[pskey], writes=[("mcol",)], tag="s_max")
                    P.op("vector", v_ts(mcol[:, 1:2], mcol[:, 0:1], -0.125, None, op0=ALU.mult),
                         reads=[("mcol",)], writes=[("mcol",)], tag="s_neg")
                    P.op("scalar", s_act(pb[:, 0:512], ps[:, 0:512], AF.Exp,
                                         bias=mcol[:, 1:2], scale=0.125,
                                         accum=mcol[:, 2:3]),
                         reads=[pskey, ("mcol",)], writes=[("pb",), ("mcol",)], tag="s_exp")
                    P.op("vector", v_recip(mcol[:, 3:4], mcol[:, 2:3]),
                         reads=[("mcol",)], writes=[("mcol",)], tag="s_rec")
                    P.op("vector", v_tt(mcol[:, 3:4], mcol[:, 3:4],
                                        tents[:, k * 4 + tcls:k * 4 + tcls + 1], ALU.mult),
                         reads=[("mcol",), ("tents",)], writes=[("mcol",)], tag="s_tent")
                    P.op("vector", v_ts(pb[:, 0:512], pb[:, 0:512], mcol[:, 3:4], None,
                                        op0=ALU.mult),
                         reads=[("pb",), ("mcol",)], writes=[("pb",)], tag="s_scl")
                    for kc in range(4):
                        P.dma_tr("scalar" if kc % 2 else "sync",
                                 ptb[:, pt_j, kc, qi * 128:(qi + 1) * 128],
                                 pb[:, kc * 128:(kc + 1) * 128],
                                 reads=[("pb",)], writes=[("ptb", pt_j, kc)], tag="s_tr")
                for kc in range(4):
                    P.op("tensor", t_matmul(psO[hh:hh + 64, 0:qn * 128],
                                            vtb[:, kc, 64 * h:64 * h + 64],
                                            ptb[:, pt_j, kc, 0:qn * 128],
                                            start=kc == 0, stop=kc == 3,
                                            tile_position=(0, hh)),
                         reads=[("vtb", kc), ("ptb", pt_j, kc)], writes=[("psO",)], tag="pv_mm")
                if h % 2 == 1:
                    for qi, (qoff, tcls) in enumerate(qts):
                        P.op("vector", v_tt(ctxb[:, hp, qoff:qoff + 128],
                                            ctxb[:, hp, qoff:qoff + 128],
                                            psO[:, qi * 128:(qi + 1) * 128], ALU.add),
                             reads=[("ctx", hp), ("psO",)], writes=[("ctx", hp)], tag="pv_ev")

        if dbg:
            for dc in range(DC):
                P.dma("gpsimd", dbg_ctx[:, dc, :], ctxb[:, dc, :],
                      reads=[("ctx", dc)], writes=[("dbgc", dc)], tag="dbg")
            for dc in range(DC):
                P.dma("gpsimd", dbg_qf[:, dc, :], qfb[:, dc, :],
                      reads=[("qfb", dc)], writes=[("dbgq", dc)], tag="dbg")
                P.dma("gpsimd", dbg_kf[:, dc, :], kfb[:, dc, :],
                      reads=[("kfb", dc)], writes=[("dbgk", dc)], tag="dbg")
                P.dma("gpsimd", dbg_xn[:, dc, :], xnk[1][:, dc, :],
                      reads=[("xnk", 1, dc)], writes=[("dbgxn", dc)], tag="dbg")
            P.dma("gpsimd", dbg_p[:], pb[:], reads=[("pb",)], writes=[("dbgp",)], tag="dbg")
            P.dma("gpsimd", dbg_rows, rows1.d[:],
                  reads=[("ln1", "row", 0), ("ln1", "row", 1), ("ln1", "row", 2)],
                  writes=[("dbgr",)], tag="dbg")

    outf = core.sb("outf", [128, DC, TPC], F32)
    with core.phase():
        wl = WeightLoader(core, stage_cols=1024, consts=consts)
        wproj = wl.load(wproj_d, "wproj", D, D)
        bprojc = wl.load_bias_cols(bproj_d, "bproj", D)
        rst = [core.sb(f"rst{j}", [128, 512], F32) for j in range(2)]
        ost = [core.sb(f"ost{j}", [128, 512], F32) for j in range(2)]
        psP = [core.psum(f"psP{j}") for j in range(2)]
        psA = core.psum("psAr")
        psB = core.psum("psBr")
        for tg in range(4):
            e0 = L0A_HALO + tg * 512
            rows1.bcast(psA, ("psAr",), 0, e0, e0 + 512, consts)
            rows1.bcast(psB, ("psBr",), 1, e0, e0 + 512, consts)
            for oc in range(DC):
                j = (oc + tg) % 2
                P.dma("scalar", rst[j][:], x0f[oc * 128:(oc + 1) * 128, e0:e0 + 512],
                      reads=[("rst", j)], writes=[("rst", j)], tag="ld_res")
                P.op("vector", v_tt(rst[j][:], rst[j][:], psA[:, 0:512], ALU.add),
                     reads=[("rst", j), ("psAr",)], writes=[("rst", j)],
                     tag="z_sub")
                P.op("vector", v_tt(rst[j][:], rst[j][:], psB[:, 0:512], ALU.mult),
                     reads=[("rst", j), ("psBr",)], writes=[("rst", j)], tag="z_mul")
                for ki in range(DC):
                    P.op("tensor", t_matmul(psP[j][:, 0:512], wproj[:, ki, oc * 128:(oc + 1) * 128],
                                            ctxb[:, ki, tg * 512:(tg + 1) * 512],
                                            start=ki == 0, stop=ki == DC - 1),
                         reads=[("wproj", ki), ("ctx", ki)], writes=[("psP", j)], tag="pr_mm")
                P.op("scalar", s_act(ost[j][:], psP[j][:, 0:512], AF.Gelu_apprx_sigmoid
                                     if False else AF.Copy),
                     reads=[("psP", j), ("ost", j)],
                     writes=[("ost", j)], tag="pr_cp")
                P.op("vector", v_ts(ost[j][:], ost[j][:],
                                    bprojc[:, oc:oc + 1], None, op0=ALU.add),
                     reads=[("ost", j), ("bproj", "bias")],
                     writes=[("ost", j)], tag="pr_b")
                P.op("vector", v_tt(outf[:, oc, tg * 512:(tg + 1) * 512], ost[j][:],
                                    rst[j][:], ALU.add),
                     reads=[("ost", j), ("rst", j)], writes=[("outf", oc, tg)], tag="pr_res")

    if dbg:
        for dc in range(DC):
            core.P.dma("gpsimd", dbg_out[:, dc, :], outf[:, dc, :],
                       reads=[("outf", dc, 0), ("outf", dc, 1), ("outf", dc, 2),
                              ("outf", dc, 3)], writes=[("dbgo", dc)], tag="dbg")
    with core.phase():
        wl = WeightLoader(core, stage_cols=1024, consts=consts)
        b1c = wl.load_bias_cols(b1_d, "b1", 4 * D)
        b2c = wl.load_bias_cols(b2_d, "b2", D)
        ln2gc = wl.load_bias_cols(ln2g_d, "ln2g", D)
        ln2bc = wl.load_bias_cols(ln2b_d, "ln2b", D)
        outb = core.sb("outb", [128, DC, TPC], BF16)
        for dc in range(DC):
            P.op("vector", v_copy(outb[:, dc, :], outf[:, dc, :]),
                 reads=[("outf", dc, 0), ("outf", dc, 1), ("outf", dc, 2), ("outf", dc, 3)],
                 writes=[("outb", dc)], tag="cp_outb")
        with core.phase():
            rows2 = emit_ln_stats_dram(core, lambda d, a, b: outb[:, d, a:b],
                                       lambda d: ("outb", d), TPC, consts, "ln2",
                                       xdt=BF16, compose=False)
        psA = core.psum("psA2")
        psB = core.psum("psB2")
        for tg in range(4):
            emit_ln_apply_dram(core, lambda d, a, b: outb[:, d, a:b],
                               lambda d: ("outb", d), rows2, 0, 1, consts,
                               lambda d, tg=tg: outb[:, d, tg * 512:(tg + 1) * 512],
                               lambda d: ("outb", d),
                               tg * 512, (tg + 1) * 512, psA, ("psA2",), psB, ("psB2",),
                               gcol_fn=lambda d: ln2gc[:, d:d + 1],
                               bcol_fn=lambda d: ln2bc[:, d:d + 1],
                               gbkeys=[("ln2g", "bias"), ("ln2b", "bias")])
        h2 = core.sb("h2", [128, 32, 512], BF16)
        w1sl = [core.sb(f"w1sl{j}", [128, DC, 128], BF16) for j in range(2)]
        w2sl = [core.sb(f"w2sl{j}", [128, 32, 128], BF16) for j in range(2)]
        ost = [core.sb(f"ost{j}", [128, 512], F32) for j in range(2)]
        xst = [core.sb(f"xst{j}", [128, 512], F32) for j in range(2)]
        psM = [core.psum(f"psM{j}") for j in range(2)]
        for tg in range(4):
            for oc in range(32):
                j = oc % 2
                st = wl.stage[j]
                for rc in range(DC):
                    P.dma("sync" if rc % 2 == 0 else "scalar",
                          st[:, rc * 128:(rc + 1) * 128],
                          w1_d[rc * 128:(rc + 1) * 128, oc * 128:(oc + 1) * 128],
                          reads=[("wstage", j)], writes=[("wstage", j)], tag="ld_w1")
                P.op("scalar", s_copy(w1sl[j][:].rearrange("p c k -> p (c k)"), st[:, 0:1024]),
                     reads=[("wstage", j)], writes=[("w1sl", j)], tag="cvt_w1")
                for ki in range(DC):
                    P.op("tensor", t_matmul(psM[j][:, 0:512], w1sl[j][:, ki, :],
                                            outb[:, ki, tg * 512:(tg + 1) * 512],
                                            start=ki == 0, stop=ki == DC - 1),
                         reads=[("w1sl", j), ("outb", ki)], writes=[("psM", j)], tag="m1_mm")
                P.op("scalar", s_act(h2[:, oc, :], psM[j][:, 0:512], AF.Gelu,
                                     bias=b1c[:, oc:oc + 1]),
                     reads=[("psM", j), ("b1", "bias")], writes=[("h2", oc)], tag="m1_gelu")
            for do in range(DC):
                j = do % 2
                st2 = wl.stage[j]
                for rc in range(4):
                    P.dma("sync" if rc % 2 == 0 else "scalar",
                          st2[:, 0:1024].rearrange("p (a b) -> p a b", a=8)[:, 0:8, 0:128],
                          w2_d[rc * 1024:(rc + 1) * 1024, do * 128:(do + 1) * 128]
                          .rearrange("(a p) b -> p a b", p=128),
                          reads=[("wstage", j), ("w2sl", j)], writes=[("wstage", j)],
                          tag="ld_w2")
                    P.op("scalar", s_copy(w2sl[j][:, rc * 8:(rc + 1) * 8, :],
                                          st2[:, 0:1024].rearrange("p (a b) -> p a b", a=8)),
                         reads=[("wstage", j)], writes=[("w2sl", j)], tag="cvt_w2")
                for hg in range(32):
                    P.op("tensor", t_matmul(psM[j][:, 0:512], w2sl[j][:, hg, :],
                                            h2[:, hg, :], start=hg == 0, stop=hg == 31),
                         reads=[("w2sl", j), ("h2", hg)], writes=[("psM", j)], tag="m2_mm")
                P.op("vector", v_ts(ost[j][:], psM[j][:, 0:512],
                                    b2c[:, do:do + 1], None, op0=ALU.add),
                     reads=[("psM", j), ("b2", "bias"), ("ost", j)],
                     writes=[("ost", j)], tag="m2_b")
                P.op("vector", v_tt(outf[:, do, tg * 512:(tg + 1) * 512], ost[j][:],
                                    outf[:, do, tg * 512:(tg + 1) * 512], ALU.add),
                     reads=[("ost", j), ("outf", do, tg)],
                     writes=[("outf", do, tg)], tag="m2_res")

    with core.phase():
        outb = core.sb("outbC", [128, DC, TPC], BF16)
        for dc in range(DC):
            P.op("vector", v_copy(outb[:, dc, :], outf[:, dc, :]),
                 reads=[("outf", dc, 0), ("outf", dc, 1), ("outf", dc, 2),
                        ("outf", dc, 3)],
                 writes=[("outbC", dc)], tag="cp_outC")
        lnst3 = emit_ln_stats(core, lambda d, a, b: outb[:, d, a:b],
                              lambda d: ("outbC", d), TPC, consts, "ln3", xdt=BF16)
        psA = core.psum("psAc")
        psB = core.psum("psBc")
        rst = [core.sb(f"rstC{j}", [128, 512], F32) for j in range(2)]
        xst = [core.sb(f"xstC{j}", [128, 512], F32) for j in range(2)]
        for tg in range(4):
            emit_row_bcast(core, psA[:, 0:512], lnst3.negmu[0:1, tg * 512:(tg + 1) * 512],
                           consts, reads=[lnst3.mukey], writes=[("psAc",)])
            emit_row_bcast(core, psB[:, 0:512], lnst3.rstd[0:1, tg * 512:(tg + 1) * 512],
                           consts, reads=[lnst3.rkey], writes=[("psBc",)])
            for oc in range(DC):
                j = (oc * 4 + tg) % 2
                P.dma("sync", rst[j][:],
                      x0f[oc * 128:(oc + 1) * 128, L0A_HALO + tg * 512:L0A_HALO + (tg + 1) * 512],
                      reads=[("rstC", j)], writes=[("rstC", j)], tag="ld_x0")
                P.op("vector", v_tt(xst[j][:], outf[:, oc, tg * 512:(tg + 1) * 512],
                                    psA[:, 0:512], ALU.add),
                     reads=[("outf", oc, tg), ("psAc",), ("xstC", j)],
                     writes=[("xstC", j)], tag="c_sub")
                P.op("vector", v_tt(xst[j][:], xst[j][:], psB[:, 0:512], ALU.mult),
                     reads=[("xstC", j), ("psBc",)], writes=[("xstC", j)], tag="c_mul")
                P.op("vector", v_tt(xst[j][:], xst[j][:], rst[j][:], ALU.add),
                     reads=[("xstC", j), ("rstC", j)], writes=[("xstC", j)], tag="c_res")
                P.dma("gpsimd", x1f[:, oc, tg * 512:(tg + 1) * 512], xst[j][:],
                      reads=[("xstC", j)], writes=[("x1out", oc, tg)], tag="st_x1")
    return core


def l0a_host_inputs(inputs, consts_h):
    x0 = np.asarray(inputs["L0_attn_hidden_states"]).reshape(S, D)
    maps = []
    for c in range(NCORES):
        lo = 2048 * c - L0A_HALO
        xe = np.zeros((L0A_TEXT, D), np.float32)
        a, b = max(0, lo), min(S, lo + L0A_TEXT)
        xe[a - lo:b - lo] = x0[a:b]
        m = dict(consts_h)
        m["x0f"] = np.ascontiguousarray(xe.T)
        for nm in ("wq", "wk", "wv", "wproj", "w1", "w2"):
            m[nm] = np.asarray(inputs[f"L0_attn_{nm}"])
        def cols(v):
            return np.ascontiguousarray(np.asarray(v).reshape(-1, 128).T)
        for nm in ("bq", "bk", "bproj", "b1", "b2"):
            m[nm] = cols(inputs[f"L0_attn_{nm}"])
        m["bv"] = np.asarray(inputs["L0_attn_bv"]).reshape(1, -1)
        m["ln1g"] = cols(inputs["L0_attn_ln1_g"])
        m["ln1b"] = cols(inputs["L0_attn_ln1_b"])
        m["ln2g"] = cols(inputs["L0_attn_ln2_g"])
        m["ln2b"] = cols(inputs["L0_attn_ln2_b"])
        r = np.arange(128, dtype=np.float32)
        up_lo = r / 255.0
        up_hi = (r + 128.0) / 255.0
        dn_hi = 1.0 - r / 255.0
        dn_lo = 1.0 - (r + 128.0) / 255.0
        tents = np.zeros((128, 40), np.float32)
        for k in range(10):
            i_global = 2048 * c - 256 + 256 * k
            cls = [up_lo, up_hi, dn_hi, dn_lo]
            if i_global < 0 or i_global > S - 256:
                cls = [np.zeros(128, np.float32)] * 4
            else:
                if i_global == 0:
                    cls = [cls[0], np.ones(128, np.float32), cls[2], cls[3]]
                if i_global == S - 256:
                    cls = [cls[0], cls[1], np.ones(128, np.float32), cls[3]]
            for t in range(4):
                tents[:, k * 4 + t] = cls[t]
        m["tents"] = tents
        wm = np.zeros((10, 512), np.float32)
        for k in range(10):
            i_global = 2048 * c - 256 + 256 * k
            ks = i_global - 128
            for col in range(0, 512, 128):
                if ks + col < 0 or ks + col >= S:
                    wm[k, col:col + 128] = NEG
        m["wmask"] = wm
        maps.append(m)
    return maps


F0_HALO = 512
F0_TEXT = TPC + F0_HALO
F0_QE0 = 384
F0_OCW1 = 44
F0_HGW2 = 22


def build_f0pair(first, dbg=False):
    core = Core("f0pair")
    P = core.P

    yf = core.inp("yf", [D, F0_TEXT], F32)
    wqkv_d = core.inp("wqkv", [D, 3 * D], F32)
    wo_d = core.inp("wo", [D, D], F32)
    w1_d = core.inp("w1p", [D, F0_OCW1 * 128], F32)
    w2_d = core.inp("w2p", [F0_HGW2 * 128, D], F32)
    b1_d = core.inp("b1p", [128, F0_OCW1], F32)
    b2_d = core.inp("b2c", [128, DC], F32)
    lng_d = core.inp("lngc", [128, DC], F32)
    lnb_d = core.inp("lnbc", [128, DC], F32)
    qs_d = core.inp("qscol", [128, 1], F32)
    ks_d = core.inp("kscol", [128, 1], F32)
    qsr_d = core.inp("qsrcol", [128, 1], F32)
    ksr_d = core.inp("ksrcol", [128, 1], F32)
    rotq_d = core.inp("rotq", [128, 2, 5, 512], BF16)
    rotk_d = core.inp("rotk", [128, 2, 5, 768], BF16)
    rotm_d = core.inp("rotm", [128, 128], BF16)
    e2_d = core.inp("expand2", [128, 128], F32)
    o64_d = core.inp("ones64", [128, 2], BF16)
    mdiag_d = core.inp("mdiag", [128, 128], BF16)
    mfar_d = core.inp("mfar", [128, 128], BF16)
    kmask_d = core.inp("kmask", [1, F0_TEXT], F32)
    edge_d = core.inp("edgecol", [128, 1], F32)

    yout = core.out("yout", [D, TPC], F32)
    if dbg:
        dbg_y2 = core.out("dbg_y2", [D, TPC], F32)
        dbg_yb2 = core.out("dbg_yb2", [128, DC, TPC + 1], BF16)
        dbg_p = core.out("dbg_pf", [128, 384], BF16)
        dbg_q = core.out("dbg_qf", [128, DC, 512], BF16)
        dbg_k = core.out("dbg_kf", [128, DC, 768], BF16)

    consts = dev_consts(core)
    ydram = core.nc.dram_tensor("ydram", [D, TPC], F32, kind="Internal").ap()

    small = [("qsc", qs_d, [128, 1], F32), ("ksc", ks_d, [128, 1], F32),
             ("qsr", qsr_d, [128, 1], F32), ("ksr", ksr_d, [128, 1], F32),
             ("edg", edge_d, [128, 1], F32), ("rotm", rotm_d, [128, 128], BF16),
             ("e2", e2_d, [128, 128], F32), ("o64", o64_d, [128, 2], BF16),
             ("mdiag", mdiag_d, [128, 128], BF16), ("mfar", mfar_d, [128, 128], BF16),
             ("kmask", kmask_d, [1, F0_TEXT], F32)]
    sc = {}
    for k, dap, shape, dt in small:
        sc[k] = core.sb_right(f"sc_{k}", shape, dt)
        P.dma("sync", sc[k][:], dap, writes=[("c", k)], tag=f"ld_{k}")
    P.barrier()

    yb2 = core.sb_right("yb2", [128, DC, TPC + 1], BF16)

    with core.phase():
        wl = WeightLoader(core, stage_cols=3072, consts=consts, nbuf=1)
        wqkv = wl.load(wqkv_d, "wqkv", D, 3 * D)
        rows_state = None
        if first:
            with core.phase():
                xpan = core.sb("xpan", [128, DC, 512], BF16)
                xld = [core.sb(f"xld{j}", [128, 512], F32) for j in range(2)]
                rows_state = RowsDram(core, "lnt", F0_TEXT, nrows=2)
                ra = core.sb("st_ra", [1, 512], F32)
                rb = core.sb("st_rb", [1, 512], F32)
                rc_ = core.sb("st_rc", [1, 512], F32)
                sq0 = core.sb("st_sq", [128, 512], BF16)
                ps0 = core.psum("st_ps")
                ka, kb, kc2 = ("lnt", "ra"), ("lnt", "rb"), ("lnt", "rc")
                for g in range(5):
                    t0 = g * 512
                    for dc in range(DC):
                        j = dc % 2
                        P.dma("scalar", xld[j][:], yf[dc * 128:(dc + 1) * 128, t0:t0 + 512],
                              reads=[("xld", j)], writes=[("xld", j)], tag="ldst")
                        P.op("vector", v_copy(xpan[:, dc, :], xld[j][:]),
                             reads=[("xld", j), ("xpan", dc)], writes=[("xpan", dc)],
                             tag="cvst")
                    for dc in range(DC):
                        P.op("tensor", t_matmul(ps0[0:1, 0:512], consts["negokmb"],
                                                xpan[:, dc, :], start=dc == 0, stop=dc == DC - 1),
                             reads=[("xpan", dc), ("c", "negokmb")], writes=[("st_ps",)],
                             tag="st_mu")
                    for dc in range(DC):
                        P.op("scalar", s_act(sq0[:], xpan[:, dc, :], AF.Square),
                             reads=[("xpan", dc), ("st_sq",)], writes=[("st_sq",)],
                             tag="st_sq")
                        P.op("tensor", t_matmul(ps0[32:33, 0:512], consts["onesokmb"],
                                                sq0[:], start=dc == 0, stop=dc == DC - 1),
                             reads=[("st_sq",), ("c", "onesokmb")], writes=[("st_ps",)],
                             tag="st_ms")
                    P.op("vector", v_copy(ra[:], ps0[0:1, 0:512]),
                         reads=[("st_ps",), ka], writes=[ka], tag="st_ev")
                    P.op("vector", v_copy(rb[:], ps0[32:33, 0:512]),
                         reads=[("st_ps",), kb], writes=[kb], tag="st_ev2")
                    P.dma("gpsimd", rows_state.d[0:1, t0:t0 + 512], ra[:],
                          reads=[ka], writes=[("lnt", "row", 0)], tag="st_r0")
                    P.op("vector", v_tt(rc_[:], ra[:], ra[:], ALU.mult),
                         reads=[ka, kc2], writes=[kc2], tag="st_mu2")
                    P.op("vector", v_tt(rb[:], rb[:], rc_[:], ALU.subtract),
                         reads=[kb, kc2], writes=[kb], tag="st_var")
                    P.op("scalar", s_act(rb[:], rb[:], AF.Ln, bias=1e-5),
                         reads=[kb], writes=[kb], tag="st_ln")
                    P.op("scalar", s_act(rb[:], rb[:], AF.Exp, scale=-0.5),
                         reads=[kb], writes=[kb], tag="st_exp")
                    P.dma("gpsimd", rows_state.d[1:2, t0:t0 + 512], rb[:],
                          reads=[kb], writes=[("lnt", "row", 1)], tag="st_r1")

        KB = core.sb("kbuf", [128, DC, 768], BF16)
        VR = core.sb("vring", [128, 2, 4, D], BF16)
        QB = core.sb("qbuf", [128, DC, 512], BF16)
        ybg = [core.sb("ybg0", [128, DC, 769], BF16)]
        tabg = core.sb("tabg", [128, 2, 768], BF16)
        sqb = core.sb("sqs", [128, 512], BF16)
        f1b = core.sb("f1b", [128, 512], F32)
        f2b = core.sb("f2b", [128, 512], F32)
        rn2 = core.sb("rn2", [2, 512], F32)
        pb = core.sb("pbuf", [128, 384], BF16)
        ptb = core.sb("ptbuf", [128, 2, 3, 128], BF16)
        mcol = core.sb("mcolf", [128, 4], F32)
        ohb = core.sb("ohb", [128, DC, 512], BF16)
        yst = [core.sb(f"yst{j}", [128, 770], F32) for j in range(2)]
        wosl = [core.sb(f"wosl{j}", [128, DC, 128], BF16) for j in range(2)]
        psQK = core.psum("psQK")
        psS = [core.psum(f"psSf{j}") for j in range(2)]
        psO = core.psum("psOf")
        psR = core.psum("psR")
        psN = core.psum("psN")

        for p in range(5):
            g0 = p * 512
            gslot = p % 2
            gj = 0
            lo = max(g0 - 257, 0)
            off = lo - (g0 - 257)
            for dc in range(DC):
                j = dc % 2
                if off:
                    P.op("vector", v_memset(ybg[gj][:, dc, 0:off], 0.0),
                         reads=[("ybg", gj, dc)], writes=[("ybg", gj, dc)], tag="zb_pan")
                P.dma("scalar", yst[j][:, 0:769 - off],
                      yf[dc * 128:(dc + 1) * 128, lo:g0 + 512],
                      reads=[("yst", j)], writes=[("yst", j)], tag="ld_pan")
                P.op("vector", v_copy(ybg[gj][:, dc, off:769], yst[j][:, 0:769 - off]),
                     reads=[("yst", j), ("ybg", gj, dc)], writes=[("ybg", gj, dc)],
                     tag="cv_pan")
            if first:
                for (a0, a1) in ((off, off + 385), (off + 385, 769)):
                    if a1 <= a0:
                        continue
                    e0 = g0 - 257 + a0
                    n_ = a1 - a0
                    rows_state.bcast(psR, ("psR",), 0, e0, e0 + n_, consts)
                    for dc in range(DC):
                        P.op("vector", v_tt(ybg[gj][:, dc, a0:a1], ybg[gj][:, dc, a0:a1],
                                            psR[:, 0:n_], ALU.add),
                             reads=[("ybg", gj, dc), ("psR",)],
                             writes=[("ybg", gj, dc)], tag="pl_mu")
                    rows_state.bcast(psR, ("psR",), 1, e0, e0 + n_, consts)
                    for dc in range(DC):
                        P.op("vector", v_tt(ybg[gj][:, dc, a0:a1], ybg[gj][:, dc, a0:a1],
                                            psR[:, 0:n_], ALU.mult),
                             reads=[("ybg", gj, dc), ("psR",)],
                             writes=[("ybg", gj, dc)], tag="pl_rs")

            for kind, oc0 in (("q", 0), ("k", DC)):
                segs = ((257, 0, 512),) if kind == "q" else ((1, 0, 512), (513, 512, 256))
                for oc in range(DC):
                    for (ybase, dbase, wid) in segs:
                        for ki in range(DC):
                            soff = ybase if ki < 4 else ybase - 1
                            P.op("tensor", t_matmul(psQK[:, 0:wid],
                                                    wqkv[:, ki, (oc0 + oc) * 128:(oc0 + oc + 1) * 128],
                                                    ybg[gj][:, ki, soff:soff + wid],
                                                    start=ki == 0, stop=ki == DC - 1),
                                 reads=[("wqkv", ki), ("ybg", gj, ki)], writes=[("psQK",)],
                                 tag="qkv_mm")
                        if kind == "q":
                            P.op("vector", v_copy(QB[:, oc, dbase:dbase + wid], psQK[:, 0:wid]),
                                 reads=[("psQK",), ("qb0", oc)], writes=[("qb0", oc)],
                                 tag="qkv_ev")
                        else:
                            P.op("vector", v_copy(KB[:, oc, dbase:dbase + wid], psQK[:, 0:wid]),
                                 reads=[("psQK",), ("kr0", oc)],
                                 writes=[("kr0", oc)], tag="qkv_ev")
            for tt in range(4):
                for half in range(2):
                    for ki in range(DC):
                        soff = 257 if ki < 4 else 256
                        P.op("tensor", t_matmul(psQK[:, 0:512],
                                                ybg[gj][:, ki, soff + tt * 128:soff + (tt + 1) * 128],
                                                wqkv[:, ki, 2 * D + half * 512:2 * D + (half + 1) * 512],
                                                start=ki == 0, stop=ki == DC - 1),
                             reads=[("wqkv", ki), ("ybg", gj, ki)], writes=[("psQK",)],
                             tag="v_mm")
                    P.op("vector", v_copy(VR[:, gslot, tt, half * 512:(half + 1) * 512],
                                          psQK[:, 0:512]),
                         reads=[("psQK",), ("vr0", gslot, tt)],
                         writes=[("vr0", gslot, tt)], tag="v_ev")
            for kind in ("q", "k"):
                wid = 512 if kind == "q" else 768
                tabd = rotq_d if kind == "q" else rotk_d
                scol = "qsc" if kind == "q" else "ksc"
                srol = "qsr" if kind == "q" else "ksr"
                P.dma("sync", tabg[:, :, 0:wid], tabd[:, :, p, :],
                      reads=[("tabg",)], writes=[("tabg",)], tag="ld_tab")
                P.op("vector", v_ts(tabg[:, 0, 0:wid], tabg[:, 0, 0:wid],
                                    sc[scol][:], None, op0=ALU.mult),
                     reads=[("tabg",), ("c", scol)], writes=[("tabg",)], tag="tabq")
                P.op("vector", v_ts(tabg[:, 1, 0:wid], tabg[:, 1, 0:wid],
                                    sc[srol][:], None, op0=ALU.mult),
                     reads=[("tabg",), ("c", srol)], writes=[("tabg",)], tag="tabr")
                for hp in range(DC):
                    src = QB[:, hp, :] if kind == "q" else KB[:, hp, :]
                    skey = ("qb0", hp) if kind == "q" else ("kr0", hp)
                    for w0 in range(0, wid, 512):
                        w1 = min(w0 + 512, wid)
                        n_ = w1 - w0
                        sv = src[:, w0:w1] if kind == "q" else KB[:, hp, w0:w1]
                        P.op("scalar", s_act(sqb[:, 0:n_], sv, AF.Square),
                             reads=[skey, ("sqs",)], writes=[("sqs",)], tag="nsq")
                        P.op("tensor", t_matmul(psN[0:2, 0:n_], sc["o64"][:], sqb[:, 0:n_],
                                                start=True, stop=True),
                             reads=[("sqs",), ("c", "o64")], writes=[("psN",)], tag="n_mm")
                        P.op("vector", v_copy(rn2[:, 0:n_], psN[0:2, 0:n_]),
                             reads=[("psN",), ("rn2",)], writes=[("rn2",)], tag="n_ev")
                        P.op("scalar", s_act(rn2[:, 0:n_], rn2[:, 0:n_], AF.Ln, bias=1e-30),
                             reads=[("rn2",)], writes=[("rn2",)], tag="n_ln")
                        P.op("scalar", s_act(rn2[:, 0:n_], rn2[:, 0:n_], AF.Exp, scale=0.5),
                             reads=[("rn2",)], writes=[("rn2",)], tag="n_exp")
                        P.op("vector", v_ts(rn2[:, 0:n_], rn2[:, 0:n_], 1e-12, None,
                                            op0=ALU.max),
                             reads=[("rn2",)], writes=[("rn2",)], tag="n_max")
                        P.op("vector", v_recip(rn2[:, 0:n_], rn2[:, 0:n_]),
                             reads=[("rn2",)], writes=[("rn2",)], tag="n_rec")
                        P.op("tensor", t_matmul(psN[:, 0:n_], sc["e2"][0:2, :], rn2[:, 0:n_],
                                                start=True, stop=True),
                             reads=[("rn2",), ("c", "e2")], writes=[("psN",)], tag="nb_mm")
                        P.op("tensor", t_matmul(psR[:, 0:n_], sc["rotm"][:], sv,
                                                start=True, stop=True),
                             reads=[skey, ("c", "rotm")], writes=[("psR",)], tag="rot_mm")
                        P.op("vector", v_tt(f1b[:, 0:n_], sv, tabg[:, 0, w0:w1], ALU.mult),
                             reads=[skey, ("tabg",), ("f1b",)], writes=[("f1b",)],
                             tag="r_a")
                        P.op("vector", v_tt(f2b[:, 0:n_], psR[:, 0:n_], tabg[:, 1, w0:w1],
                                            ALU.mult),
                             reads=[("psR",), ("tabg",), ("f2b",)], writes=[("f2b",)],
                             tag="r_b")
                        P.op("vector", v_tt(f1b[:, 0:n_], f1b[:, 0:n_], f2b[:, 0:n_],
                                            ALU.add),
                             reads=[("f1b",), ("f2b",)], writes=[("f1b",)], tag="r_add")
                        P.op("vector", v_tt(sv, f1b[:, 0:n_], psN[:, 0:n_], ALU.mult),
                             reads=[("f1b",), ("psN",), skey], writes=[skey], tag="r_rn")
            if dbg and p == 1:
                for dc in range(DC):
                    P.dma("gpsimd", dbg_q[:, dc, :], QB[:, dc, :],
                          reads=[("qb0", dc)], writes=[("dbq", dc)], tag="dbgq")
                    P.dma("gpsimd", dbg_k[:, dc, :], KB[:, dc, :],
                          reads=[("kr0", dc)], writes=[("dbk", dc)], tag="dbgk")
            qts = [q for q in range(g0, g0 + 512, 128) if q >= F0_QE0]
            for h in range(16):
                hp, hh = h // 2, (h % 2) * 64
                ptj = h % 2
                for q0 in qts:
                    qi = (q0 - g0) // 128
                    ps = psS[(h + qi) % 2]
                    pskey = ("psSf", (h + qi) % 2)
                    P.op("tensor", t_matmul(ps[:, 0:384], consts["ones1"],
                                            sc["kmask"][0:1, q0 - 256:q0 + 128],
                                            start=True, stop=False),
                         reads=[("c", "kmask"), ("c", "ones1")], writes=[pskey],
                         tag="s_km")
                    P.op("tensor", t_matmul(ps[:, 0:384],
                                            QB[hh:hh + 64, hp, qi * 128:(qi + 1) * 128],
                                            KB[hh:hh + 64, hp, qi * 128:qi * 128 + 384],
                                            start=False, stop=False),
                         reads=[("qb0", hp), ("kr0", hp)], writes=[pskey],
                         tag="s_mm")
                    P.op("tensor", t_matmul(ps[:, 256:384], sc["mdiag"][:],
                                            consts["identb"], start=False, stop=False),
                         reads=[("c", "mdiag"), ("c", "identb")], writes=[pskey],
                         tag="s_md")
                    P.op("tensor", t_matmul(ps[:, 0:128], sc["mfar"][:],
                                            consts["identb"], start=False, stop=True),
                         reads=[("c", "mfar"), ("c", "identb")], writes=[pskey],
                         tag="s_mf")
                    P.op("vector", v_red(mcol[:, 0:1], ps[:, 0:384], "max"),
                         reads=[pskey, ("mcolf",)], writes=[("mcolf",)], tag="s_max")
                    P.op("vector", v_ts(mcol[:, 1:2], mcol[:, 0:1], -8.0, None,
                                        op0=ALU.mult),
                         reads=[("mcolf",)], writes=[("mcolf",)], tag="s_neg")
                    P.op("scalar", s_act(pb[:], ps[:, 0:384], AF.Exp,
                                         bias=mcol[:, 1:2], scale=8.0,
                                         accum=mcol[:, 2:3]),
                         reads=[pskey, ("mcolf",), ("pbuf",)],
                         writes=[("pbuf",), ("mcolf",)], tag="s_exp")
                    P.op("vector", v_recip(mcol[:, 3:4], mcol[:, 2:3]),
                         reads=[("mcolf",)], writes=[("mcolf",)], tag="s_rec")
                    P.op("vector", v_ts(pb[:], pb[:], mcol[:, 3:4], None, op0=ALU.mult),
                         reads=[("pbuf",), ("mcolf",)], writes=[("pbuf",)], tag="s_nrm")
                    if dbg and p == 1 and h == 15 and qi == 0:
                        P.dma("gpsimd", dbg_p, pb[:], reads=[("pbuf",)],
                              writes=[("dbp",)], tag="dbgp")
                    for kc in range(3):
                        P.dma_tr("scalar" if kc % 2 else "sync",
                                 ptb[:, ptj, kc, :], pb[:, kc * 128:(kc + 1) * 128],
                                 reads=[("pbuf",)], writes=[("ptbuf", ptj, kc)],
                                 tag="s_tr")
                    for kc in range(3):
                        kcol = q0 - 256 + kc * 128
                        slot = (kcol // 512) % 2
                        tt = (kcol % 512) // 128
                        P.op("tensor", t_matmul(psO[hh:hh + 64, qi * 128:(qi + 1) * 128],
                                                VR[:, slot, tt, 64 * h:64 * h + 64],
                                                ptb[:, ptj, kc, :],
                                                start=kc == 0, stop=kc == 2,
                                                tile_position=(0, hh)),
                             reads=[("vr0", slot, tt), ("ptbuf", ptj, kc)],
                             writes=[("psOf",)], tag="pv_mm")
                if h % 2 == 1:
                    P.op("vector", v_copy(ohb[:, hp, :], psO[:, 0:512]),
                         reads=[("psOf",), ("ohb", hp)], writes=[("ohb", hp)],
                         tag="pv_ev")
            c0 = g0 - 511
            s_ = max(0, -c0)
            for oc in range(DC):
                j = oc % 2
                st = wl.stage[0]
                P.dma("sync", st[:, 0:1024].rearrange("p (a b) -> p a b", b=128),
                      wo_d[:, oc * 128:(oc + 1) * 128].rearrange("(a p) b -> p a b", p=128),
                      reads=[("wstage", 0)], writes=[("wstage", 0)], tag="ld_wo")
                P.op("scalar", s_copy(wosl[j][:].rearrange("p c k -> p (c k)"),
                                      st[:, 0:1024]),
                     reads=[("wstage", 0)], writes=[("wosl", j)], tag="cvt_wo")
                for ki in range(DC):
                    P.op("tensor", t_matmul(psS[0][:, 0:512], wosl[j][:, ki, :],
                                            ohb[:, ki, :], start=ki == 0, stop=ki == DC - 1),
                         reads=[("wosl", j), ("ohb", ki)], writes=[("psSf", 0)],
                         tag="wo_mm")
                P.dma("scalar", yst[j][:, 0:512], yf[oc * 128:(oc + 1) * 128, g0:g0 + 512],
                      reads=[("yst", j)], writes=[("yst", j)], tag="ld_yold")
                if first:
                    rows_state.bcast(psR, ("psR",), 0, g0, g0 + 512, consts)
                    P.op("vector", v_tt(yst[j][:, 0:512], yst[j][:, 0:512], psR[:, 0:512], ALU.add),
                         reads=[("yst", j), ("psR",)], writes=[("yst", j)], tag="y_mu")
                    rows_state.bcast(psR, ("psR",), 1, g0, g0 + 512, consts)
                    P.op("vector", v_tt(yst[j][:, 0:512], yst[j][:, 0:512], psR[:, 0:512], ALU.mult),
                         reads=[("yst", j), ("psR",)], writes=[("yst", j)], tag="y_rs")
                P.op("vector", v_tt(yst[j][:, 0:512], yst[j][:, 0:512], psS[0][:, 0:512], ALU.add),
                     reads=[("yst", j), ("psSf", 0)], writes=[("yst", j)], tag="y_res")
                if p >= 1:
                    P.dma("gpsimd", ydram[oc * 128:(oc + 1) * 128, g0 - 512:g0],
                          yst[j][:, 0:512], reads=[("yst", j)],
                          writes=[("ydr", oc, p)], tag="st_y2")
                P.op("vector", v_copy(yb2[:, oc, c0 + s_:c0 + 512], yst[j][:, s_:512]),
                     reads=[("yst", j), ("yb2", oc)], writes=[("yb2", oc)], tag="yb2")
        for dc in range(DC):
            P.op("vector", v_ts(yb2[:, dc, 0:1], yb2[:, dc, 0:1], sc["edg"][:], None,
                                op0=ALU.mult),
                 reads=[("yb2", dc), ("c", "edg")], writes=[("yb2", dc)], tag="edge")

    if dbg:
        for dc in range(DC):
            P.dma("gpsimd", dbg_y2[dc * 128:(dc + 1) * 128, :], ydram[dc * 128:(dc + 1) * 128, :],
                  reads=[("ydr", dc, 1), ("ydr", dc, 2), ("ydr", dc, 3), ("ydr", dc, 4)],
                  writes=[("dby", dc)], tag="dbg")
            P.dma("gpsimd", dbg_yb2[:, dc, :], yb2[:, dc, :],
                  reads=[("yb2", dc)], writes=[("dbyb", dc)], tag="dbg")
    with core.phase():
        wl = WeightLoader(core, stage_cols=2048, consts=consts)
        b1c = core.sb("b1c", [128, F0_OCW1], F32)
        P.dma("sync", b1c[:], b1_d, writes=[("b1", "bias")], tag="ld_b1")
        b2c = core.sb("b2c_sb", [128, DC], F32)
        P.dma("sync", b2c[:], b2_d, writes=[("b2", "bias")], tag="ld_b2")
        lngc = core.sb("lngc_sb", [128, DC], F32)
        P.dma("sync", lngc[:], lng_d, writes=[("lng", "bias")], tag="ld_lng")
        lnbc = core.sb("lnbc_sb", [128, DC], F32)
        P.dma("sync", lnbc[:], lnb_d, writes=[("lnb", "bias")], tag="ld_lnb")
        P.barrier()
        w2f = wl.load(w2_d, "w2p", F0_HGW2 * 128, D)

        def xsh(dc, a, b):
            off = 1 if dc < 4 else 0
            return yb2[:, dc, a + off:b + off]

        def xshkeys(d_):
            return ("yb2", d_)

        with core.phase():
            rows2 = emit_ln_stats_dram(core, xsh, xshkeys, TPC, consts,
                                       "lnf", xdt=BF16, compose=False)
        xnb = core.sb("xnb", [128, DC, 512], BF16)
        tb = core.sb("tbuf", [128, F0_HGW2, 512], BF16)
        w1sl = [core.sb(f"w1slf{j}", [128, DC, 128], BF16) for j in range(2)]
        ast_ = [core.sb(f"astf{j}", [128, 512], F32) for j in range(2)]
        yst = [core.sb(f"ystf{j}", [128, 512], F32) for j in range(2)]
        psA = core.psum("psAf")
        psB = core.psum("psBf")
        psM = [core.psum(f"psMf{j}") for j in range(2)]
        for g in range(4):
            t0 = g * 512
            emit_ln_apply_dram(core, xsh, xshkeys, rows2, 0, 1, consts,
                               lambda d_: xnb[:, d_, :], lambda d_: ("xnb", d_),
                               t0, t0 + 512, psA, ("psAf",), psB, ("psBf",),
                               gcol_fn=lambda d_: lngc[:, d_:d_ + 1],
                               bcol_fn=lambda d_: lnbc[:, d_:d_ + 1],
                               gbkeys=[("lng", "bias"), ("lnb", "bias")])
            for oc in range(F0_OCW1):
                j = oc % 2
                st = wl.stage[j]
                P.dma("sync", st[:, 0:1024].rearrange("p (a b) -> p a b", b=128),
                      w1_d[:, oc * 128:(oc + 1) * 128].rearrange("(a p) b -> p a b", p=128),
                      reads=[("wstage", j)], writes=[("wstage", j)], tag="ld_w1f")
                P.op("scalar", s_copy(w1sl[j][:].rearrange("p c k -> p (c k)"),
                                      st[:, 0:1024]),
                     reads=[("wstage", j)], writes=[("w1slf", j)], tag="cvt_w1f")
                for ki in range(DC):
                    P.op("tensor", t_matmul(psM[j][:, 0:512], w1sl[j][:, ki, :],
                                            xnb[:, ki, :], start=ki == 0, stop=ki == DC - 1),
                         reads=[("w1slf", j), ("xnb", ki)], writes=[("psMf", j)],
                         tag="m1_mm")
                hh = 64 * (oc % 2)
                hgc = oc // 2
                P.op("scalar", s_act(tb[hh:hh + 64, hgc, :], psM[j][64:128, 0:512],
                                     AF.Gelu, bias=b1c[64:128, oc:oc + 1]),
                     reads=[("psMf", j), ("b1", "bias"), ("tbuf", hgc)],
                     writes=[("tbuf", hgc)], tag="m1_gelu")
                P.op("vector", v_ts(ast_[j][hh:hh + 64, :], psM[j][0:64, 0:512],
                                    b1c[0:64, oc:oc + 1], None, op0=ALU.add),
                     reads=[("psMf", j), ("b1", "bias"), ("astf", j)],
                     writes=[("astf", j)], tag="m1_a")
                P.op("vector", v_tt(tb[hh:hh + 64, hgc, :], tb[hh:hh + 64, hgc, :],
                                    ast_[j][hh:hh + 64, :], ALU.mult),
                     reads=[("tbuf", hgc), ("astf", j)],
                     writes=[("tbuf", hgc)], tag="m1_t")
            for do in range(DC):
                j = do % 2
                for hg in range(F0_HGW2):
                    P.op("tensor", t_matmul(psM[j][:, 0:512], w2f[:, hg, do * 128:(do + 1) * 128],
                                            tb[:, hg, :], start=hg == 0, stop=hg == F0_HGW2 - 1),
                         reads=[("w2p", hg), ("tbuf", hg)], writes=[("psMf", j)],
                         tag="m2_mm")
                P.dma("scalar", yst[j][:], ydram[do * 128:(do + 1) * 128, t0:t0 + 512],
                      reads=[("ystf", j)] + [("ydr", do, pp) for pp in range(1, 5)],
                      writes=[("ystf", j)], tag="ld_y2")
                P.op("vector", v_ts(yst[j][:], yst[j][:], b2c[:, do:do + 1], None,
                                    op0=ALU.add),
                     reads=[("ystf", j), ("b2", "bias")], writes=[("ystf", j)],
                     tag="m2_b")
                P.op("vector", v_tt(yst[j][:], yst[j][:], psM[j][:, 0:512], ALU.add),
                     reads=[("ystf", j), ("psMf", j)], writes=[("ystf", j)], tag="m2_res")
                P.dma("gpsimd", yout[do * 128:(do + 1) * 128, t0:t0 + 512], yst[j][:],
                      reads=[("ystf", j)], writes=[("yo", do, g)], tag="st_y3")
    return core


def f0_host_inputs(inputs, consts_h, layer, state_f, first):
    DH = 64
    freqs, scale = None, None
    half = np.arange(0, DH, 2, dtype=np.float32)
    inv_freq = 1.0 / 10000.0 ** (half / DH)
    t = np.arange(S, dtype=np.float32)
    fr = t[:, None] * inv_freq[None, :]
    fr = np.concatenate([fr, fr], -1)
    base_scale = (half + 0.4 * DH) / (1.4 * DH)
    power = (t - S // 2) / 128.0
    sc_ = base_scale[None, :] ** power[:, None]
    sc_ = np.concatenate([sc_, sc_], -1)
    cosf = np.cos(fr)
    sinf = np.sin(fr)
    logb = np.log(np.concatenate([base_scale, base_scale]))

    W1 = np.asarray(inputs["L0_ffn_W1"][layer])
    b1 = np.asarray(inputs["L0_ffn_b1"][layer])
    W2 = np.asarray(inputs["L0_ffn_W2"][layer])
    HALF = 2730
    W1p = np.zeros((D, F0_OCW1 * 128), np.float32)
    b1p = np.zeros((128, F0_OCW1), np.float32)
    for jj in range(F0_OCW1):
        a0, a1 = 64 * jj, min(64 * jj + 64, HALF)
        n = a1 - a0
        if n > 0:
            W1p[:, 128 * jj:128 * jj + n] = W1[:, a0:a1]
            W1p[:, 128 * jj + 64:128 * jj + 64 + n] = W1[:, HALF + a0:HALF + a1]
            b1p[0:n, jj] = b1[a0:a1]
            b1p[64:64 + n, jj] = b1[HALF + a0:HALF + a1]
    W2p = np.zeros((F0_HGW2 * 128, D), np.float32)
    W2p[0:HALF] = W2

    rotm = np.zeros((128, 128), np.float32)
    for base in (0, 64):
        for mm_ in range(32):
            rotm[base + mm_ + 32, base + mm_] = -1.0
            rotm[base + mm_, base + mm_ + 32] = 1.0
    e2 = np.zeros((128, 128), np.float32)
    e2[0, 0:64] = 1.0
    e2[1, 64:128] = 1.0
    o64 = np.zeros((128, 2), np.float32)
    o64[0:64, 0] = 1.0
    o64[64:128, 1] = 1.0
    ii, jj_ = np.mgrid[0:128, 0:128]
    mdiag = np.where(ii > jj_, NEG, 0.0).astype(np.float32)
    mfar = np.where(ii < jj_, NEG, 0.0).astype(np.float32)

    def cols(v):
        return np.ascontiguousarray(np.asarray(v).reshape(-1, 128).T)

    maps = []
    for c in range(NCORES):
        lo = 2048 * c - F0_HALO
        ye = np.zeros((D, F0_TEXT), np.float32)
        a, b = max(0, lo), min(S, lo + F0_TEXT)
        ye[:, a - lo:b - lo] = state_f[:, a:b]
        rotq = np.zeros((64, 2, 5, 512), np.float32)
        rotk = np.zeros((64, 2, 5, 768), np.float32)
        for pp in range(5):
            tref = lo + 512 * pp
            tq = np.arange(tref, tref + 512)
            tk = np.arange(tref - 256, tref + 512)
            for arr, tt_ in ((rotq, tq), (rotk, tk)):
                tcl = np.clip(tt_, 0, S - 1)
                e = (tt_ - tref)[:, None] / 128.0
                scl = np.exp(e * logb[None, :]) if arr is rotq else np.exp(-e * logb[None, :])
                arr[:, 0, pp, :] = (cosf[tcl] * scl).T
                arr[:, 1, pp, :] = (sinf[tcl] * scl).T
        rotq = np.concatenate([rotq, rotq], axis=0).astype(BF)
        rotk = np.concatenate([rotk, rotk], axis=0).astype(BF)
        km = np.zeros((1, F0_TEXT), np.float32)
        if a - lo > 0:
            km[0, 0:a - lo] = NEG
        edge = np.ones((128, 1), np.float32)
        if c == 0:
            edge[:] = 0.0
        m = dict(consts_h)
        m.update({
            "yf": ye, "wqkv": np.asarray(inputs["L0_ffn_Wqkv"][layer]),
            "wo": np.asarray(inputs["L0_ffn_Wo"][layer]),
            "w1p": W1p, "w2p": W2p, "b1p": b1p,
            "b2c": cols(inputs["L0_ffn_b2"][layer]),
            "lngc": cols(inputs["L0_ffn_ln_g"][layer]),
            "lnbc": cols(inputs["L0_ffn_ln_b"][layer]),
            "qscol": np.tile(np.asarray(inputs["L0_ffn_q_scale"][layer]), 2).reshape(128, 1).astype(np.float32),
            "kscol": np.tile(np.asarray(inputs["L0_ffn_k_scale"][layer]), 2).reshape(128, 1).astype(np.float32),
            "qsrcol": np.tile(np.roll(np.asarray(inputs["L0_ffn_q_scale"][layer]), 32), 2).reshape(128, 1).astype(np.float32),
            "ksrcol": np.tile(np.roll(np.asarray(inputs["L0_ffn_k_scale"][layer]), 32), 2).reshape(128, 1).astype(np.float32),
            "rotq": rotq, "rotk": rotk, "rotm": rotm.astype(BF), "expand2": e2,
            "ones64": o64.astype(BF), "mdiag": mdiag.astype(BF),
            "mfar": mfar.astype(BF), "kmask": km, "edgecol": edge,
        })
        maps.append(m)
    return maps


def build_f0fin():
    core = Core("f0fin")
    P = core.P
    x1p = core.inp("x1p", [D, TPC], F32)
    y6p = core.inp("y6p", [D, TPC], F32)
    wqT_d = core.inp("wqT", [D, D], F32)
    wkT_d = core.inp("wkT", [D, D], F32)
    wvT_d = core.inp("wvT", [D, D], F32)
    bqc_d = core.inp("bqc", [128, DC], F32)
    bkc_d = core.inp("bkc", [128, DC], F32)
    bvr_d = core.inp("bvr", [1, D], F32)
    x2f = core.out("x2f", [D, TPC], F32)
    qf_o = core.out("qf", [128, DC, TPC], BF16)
    kf_o = core.out("kf", [128, DC, TPC], BF16)
    vt_o = core.out("vt", [TPC, D], BF16)
    qn_o = core.out("qn", [1, TPC], F32)
    kn_o = core.out("kn", [1, TPC], F32)
    consts = dev_consts(core)

    xb2 = core.sb_right("xb2", [128, DC, TPC], BF16)
    with core.phase():
        yb6 = core.sb("yb6", [128, DC, TPC], BF16)
        st2 = [core.sb(f"st{j}", [128, 512], F32) for j in range(2)]
        for dc in range(DC):
            for g in range(4):
                j = (dc * 4 + g) % 2
                P.dma("scalar", st2[j][:], y6p[dc * 128:(dc + 1) * 128, g * 512:(g + 1) * 512],
                      reads=[("st", j)], writes=[("st", j)], tag="ldy6")
                P.op("vector", v_copy(yb6[:, dc, g * 512:(g + 1) * 512], st2[j][:]),
                     reads=[("st", j), ("yb6", dc)], writes=[("yb6", dc)], tag="cvy6")
        with core.phase():
            rows6 = emit_ln_stats_dram(core, lambda d_, a, b: yb6[:, d_, a:b],
                                       lambda d_: ("yb6", d_), TPC, consts, "ln6",
                                       xdt=BF16, compose=False)
        psA = core.psum("psA6")
        psB = core.psum("psB6")
        xst = [core.sb(f"xst{j}", [128, 512], F32) for j in range(2)]
        for g in range(4):
            t0 = g * 512
            rows6.bcast(psA, ("psA6",), 0, t0, t0 + 512, consts)
            rows6.bcast(psB, ("psB6",), 1, t0, t0 + 512, consts)
            for dc in range(DC):
                j = (dc + g) % 2
                P.dma("sync", xst[j][:], x1p[dc * 128:(dc + 1) * 128, t0:t0 + 512],
                      reads=[("xst", j)], writes=[("xst", j)], tag="ldx1")
                P.op("vector", v_tt(st2[j][:], yb6[:, dc, t0:t0 + 512], psA[:, 0:512],
                                    ALU.add),
                     reads=[("yb6", dc), ("psA6",), ("st", j)], writes=[("st", j)],
                     tag="f_mu")
                P.op("vector", v_tt(st2[j][:], st2[j][:], psB[:, 0:512], ALU.mult),
                     reads=[("st", j), ("psB6",)], writes=[("st", j)], tag="f_rs")
                P.op("vector", v_tt(xst[j][:], xst[j][:], st2[j][:], ALU.add),
                     reads=[("xst", j), ("st", j)], writes=[("xst", j)], tag="f_res")
                P.dma("gpsimd", x2f[dc * 128:(dc + 1) * 128, t0:t0 + 512], xst[j][:],
                      reads=[("xst", j)], writes=[("x2o", dc, g)], tag="st_x2")
                P.op("vector", v_copy(xb2[:, dc, t0:t0 + 512], xst[j][:]),
                     reads=[("xst", j), ("xb2", dc)], writes=[("xb2", dc)], tag="cvx2")
    with core.phase():
        wl = WeightLoader(core, stage_cols=1024, consts=consts)
        wq = wl.load(wqT_d, "wq", D, D)
        wk = wl.load(wkT_d, "wk", D, D)
        wv = wl.load(wvT_d, "wv", D, D)
        bqc = core.sb("bqc_sb", [128, DC], F32)
        P.dma("sync", bqc[:], bqc_d, writes=[("bq", "bias")], tag="ldbq")
        bkc = core.sb("bkc_sb", [128, DC], F32)
        P.dma("sync", bkc[:], bkc_d, writes=[("bk", "bias")], tag="ldbk")
        bvr = core.sb("bvr_sb", [1, D], F32)
        P.dma("sync", bvr[:], bvr_d, writes=[("bvr",)], tag="ldbv")
        with core.phase():
            rows2 = emit_ln_stats_dram(core, lambda d_, a, b: xb2[:, d_, a:b],
                                       lambda d_: ("xb2", d_), TPC, consts, "lnx2",
                                       xdt=BF16, compose=False)
        bvb = core.sb("bvb", [128, D], F32)
        psA = core.psum("psA7")
        psB = core.psum("psB7")
        for half in range(2):
            emit_row_bcast(core, psA[:, 0:512], bvr[0:1, half * 512:(half + 1) * 512],
                           consts, reads=[("bvr",)], writes=[("psA7",)])
            P.op("vector", v_copy(bvb[:, half * 512:(half + 1) * 512], psA[:, 0:512]),
                 reads=[("psA7",)], writes=[("bvb",)], tag="bvb")
        xnb = core.sb("xnb2", [128, DC, 512], BF16)
        sqb = core.sb("sqb2", [128, 512], BF16)
        rowb = core.sb("rowb2", [1, 512], F32)
        vst = [core.sb(f"vst{j}", [128, D], BF16) for j in range(2)]
        psQ = [core.psum(f"psQ7{j}") for j in range(2)]
        psV = core.psum("psV7", cols=1024)
        psN = core.psum("psN7")
        for g in range(4):
            t0 = g * 512
            emit_ln_apply_dram(core, lambda d_, a, b: xb2[:, d_, a:b],
                               lambda d_: ("xb2", d_), rows2, 0, 1, consts,
                               lambda d_: xnb[:, d_, :], lambda d_: ("xnb", d_),
                               t0, t0 + 512, psA, ("psA7",), psB, ("psB7",))
            for (w, wname, bc, bkey, dst, dkey, nout) in (
                    (wq, "wq", bqc, ("bq", "bias"), qf_o, "qn", qn_o),
                    (wk, "wk", bkc, ("bk", "bias"), kf_o, "kn", kn_o)):
                for oc in range(DC):
                    j = oc % 2
                    for ki in range(DC):
                        P.op("tensor", t_matmul(psQ[j][:, 0:512], w[:, ki, oc * 128:(oc + 1) * 128],
                                                xnb[:, ki, :], start=ki == 0, stop=ki == DC - 1),
                             reads=[(wname, ki), ("xnb", ki)], writes=[("psQ7", j)],
                             tag="qk_mm")
                    P.op("vector", v_ts(sqb[:] if False else xnb[:, 0, 0:512] if False else sqb[:],
                                        psQ[j][:, 0:512], bc[:, oc:oc + 1], None,
                                        op0=ALU.add),
                         reads=[("psQ7", j), bkey, ("sqb2",)], writes=[("sqb2",)],
                         tag="qk_b")
                    P.dma("gpsimd", dst[:, oc, t0:t0 + 512], sqb[:],
                          reads=[("sqb2",)], writes=[(wname + "o", oc, g)], tag="st_qk")
                    P.op("scalar", s_act(sqb[:], sqb[:], AF.Square),
                         reads=[("sqb2",)], writes=[("sqb2",)], tag="qk_sq")
                    P.op("tensor", t_matmul(psN[0:1, 0:512], consts["onesokmb"], sqb[:],
                                            start=oc == 0, stop=oc == DC - 1),
                         reads=[("sqb2",), ("c", "onesokm")], writes=[("psN7",)],
                         tag="qk_nm")
                P.op("vector", v_copy(rowb[:], psN[0:1, 0:512]),
                     reads=[("psN7",), ("rowb2",)], writes=[("rowb2",)], tag="qk_nev")
                P.dma("gpsimd", nout[0:1, t0:t0 + 512], rowb[:],
                      reads=[("rowb2",)], writes=[(wname + "n", g)], tag="st_n")
            for tt in range(4):
                j = tt % 2
                for half in range(2):
                    for ki in range(DC):
                        P.op("tensor", t_matmul(psV[:, half * 512:(half + 1) * 512],
                                                xnb[:, ki, tt * 128:(tt + 1) * 128],
                                                wv[:, ki, half * 512:(half + 1) * 512],
                                                start=ki == 0, stop=ki == DC - 1),
                             reads=[("wv", ki), ("xnb", ki)], writes=[("psV7",)],
                             tag="v_mm")
                for half in range(2):
                    P.op("vector", v_tt(vst[j][:, half * 512:(half + 1) * 512],
                                        psV[:, half * 512:(half + 1) * 512],
                                        bvb[:, half * 512:(half + 1) * 512], ALU.add),
                         reads=[("psV7",), ("bvb",), ("vst", j)], writes=[("vst", j)],
                         tag="v_ev")
                P.dma("gpsimd", vt_o[t0 + tt * 128:t0 + (tt + 1) * 128, :], vst[j][:],
                      reads=[("vst", j)], writes=[("vto", g, tt)], tag="st_v")
    return core


def f0fin_host_inputs(inputs, consts_h, x1_f, y6_f):
    maps = []
    for c in range(NCORES):
        m = dict(consts_h)
        m.update({
            "x1p": np.ascontiguousarray(x1_f[:, c * TPC:(c + 1) * TPC]),
            "y6p": np.ascontiguousarray(y6_f[:, c * TPC:(c + 1) * TPC]),
            "wqT": np.ascontiguousarray(np.asarray(inputs["L1_attn_Wq"]).T),
            "wkT": np.ascontiguousarray(np.asarray(inputs["L1_attn_Wk"]).T),
            "wvT": np.ascontiguousarray(np.asarray(inputs["L1_attn_Wv"]).T),
            "bqc": np.ascontiguousarray(np.asarray(inputs["L1_attn_bq"]).reshape(-1, 128).T),
            "bkc": np.ascontiguousarray(np.asarray(inputs["L1_attn_bk"]).reshape(-1, 128).T),
            "bvr": np.asarray(inputs["L1_attn_bv"]).reshape(1, -1),
        })
        maps.append(m)
    return maps


def build_a1attn():
    core = Core("a1attn")
    P = core.P
    qf = core.inp("qfi", [128, DC, TPC], BF16)
    kfull_d = core.inp("kfull", [128, DC, S], BF16)
    vfull_d = core.inp("vfull", [S, D], BF16)
    qn_d = core.inp("qni", [1, TPC], F32)
    knfull_d = core.inp("knfull", [1, S], F32)
    x2p = core.inp("x2p", [D, TPC], F32)
    identf_d = core.inp("identf", [128, 128], F32)
    mdiag2_d = core.inp("mdiag2", [128, DC, 128], BF16)
    x3f = core.out("x3f", [D, TPC], F32)
    consts = dev_consts(core)
    sc = {}
    for k, dd, shape, dt in (("identf", identf_d, [128, 128], F32),
                             ("mdiag2", mdiag2_d, [128, DC, 128], BF16)):
        sc[k] = core.sb_right(f"sc_{k}", shape, dt)
        P.dma("sync", sc[k][:], dd, writes=[("c", k)], tag=f"ld_{k}")
    P.barrier()

    with core.phase():
        qb = core.sb("qbl", [128, DC, TPC], BF16)
        for dc in range(DC):
            P.dma("sync", qb[:, dc, :], qf[:, dc, :], writes=[("qb", dc)], tag="ldq")
        qnrow = core.sb("qnrow", [1, TPC], F32)
        P.dma("sync", qnrow[:], qn_d, writes=[("qnr",)], tag="ldqn")
        knrow = core.sb("knrow", [1, 1024], F32)
        kmax = core.sb("kmax", [1, 4], F32)
        brow = core.sb("brow", [1, 128], F32)
        bcol = core.sb("bcol", [128, 1], F32)
        lacc = core.sb("lacc", [128, 2], F32)
        pb = core.sb("pbA", [128, 512], BF16)
        ptb = core.sb("ptA", [128, 4, 128], BF16)
        ktb = [core.sb(f"ktb{j}", [128, DC, 512], BF16) for j in range(2)]
        vtb = [core.sb(f"vtb{j}", [128, 4, D], BF16) for j in range(2)]
        ot = core.sb("otA", [128, D], F32)
        sqA = core.sb("sqA", [128, 512], F32)
        mc = core.sb("mcA", [128, 6], F32)
        x2st = [core.sb(f"x2stA{j}", [128, 128], F32) for j in range(2)]
        x3st = [core.sb(f"x3stA{j}", [128, 128], F32) for j in range(2)]
        psS = [core.psum(f"psSA{j}") for j in range(2)]
        psO = core.psum("psOA", cols=1024)
        psB1 = core.psum("psB1")
        P.op("vector", v_memset(kmax[:], 1e-30), writes=[("kmax",)], tag="km0")
        for i in range(16):
            nkb = 2 * i + 2
            P.dma("sync", knrow[:], knfull_d[0:1, (nkb - 2) * 512:nkb * 512],
                  reads=[("knrow",)], writes=[("knrow",)], tag="ldkn")
            P.op("vector", v_red(kmax[:, 1:2], knrow[:], "max"),
                 reads=[("knrow",), ("kmax",)], writes=[("kmax",)], tag="km1")
            P.op("vector", v_tt(kmax[:, 0:1], kmax[:, 0:1], kmax[:, 1:2], ALU.max),
                 reads=[("kmax",)], writes=[("kmax",)], tag="km2")
            P.op("vector", v_ts(brow[:], qnrow[0:1, i * 128:(i + 1) * 128],
                                kmax[0:1, 0:1], None, op0=ALU.mult),
                 reads=[("qnr",), ("kmax",), ("brow",)], writes=[("brow",)], tag="b1")
            P.op("scalar", s_act(brow[:], brow[:], AF.Ln, bias=1e-30),
                 reads=[("brow",)], writes=[("brow",)], tag="b2")
            P.op("scalar", s_act(brow[:], brow[:], AF.Exp, scale=0.5),
                 reads=[("brow",)], writes=[("brow",)], tag="b3")
            P.op("vector", v_ts(brow[:], brow[:], -32.0, None, op0=ALU.mult),
                 reads=[("brow",)], writes=[("brow",)], tag="b4")
            P.op("tensor", t_transpose(psB1[:, 0:1], brow[:],
                                       consts["ones1"][0:1, 0:1]),
                 reads=[("brow",), ("c", "ones1")], writes=[("psB1",)], tag="b5")
            P.op("vector", v_copy(bcol[:], psB1[:, 0:1]),
                 reads=[("psB1",), ("bcol",)], writes=[("bcol",)], tag="b6")
            P.op("vector", v_memset(lacc[:, 0:1], 0.0),
                 reads=[("lacc",)], writes=[("lacc",)], tag="l0")
            for kt in range(nkb):
                j = kt % 2
                P.dma("sync", ktb[j][:], kfull_d[:, :, kt * 512:(kt + 1) * 512],
                      reads=[("ktb", j)], writes=[("ktb", j)], tag="ldk")
                for tt in range(4):
                    P.dma("scalar", vtb[j][:, tt, :],
                          vfull_d[kt * 512 + tt * 128:kt * 512 + (tt + 1) * 128, :],
                          reads=[("vtb", j)], writes=[("vtb", j)], tag="ldv")
                ps = psS[j]
                pskey = ("psSA", j)
                for dc in range(DC):
                    P.op("tensor", t_matmul(ps[:, 0:512], qb[:, dc, i * 128:(i + 1) * 128],
                                            ktb[j][:, dc, :], start=dc == 0,
                                            stop=dc == DC - 1 and kt < nkb - 2),
                         reads=[("qb", dc), ("ktb", j)], writes=[pskey], tag="s_mm")
                if kt >= nkb - 2:
                    moff = 4 * (kt - (nkb - 2))
                    for mm_ in range(4):
                        P.op("tensor", t_matmul(ps[:, mm_ * 128:(mm_ + 1) * 128],
                                                sc["mdiag2"][:, moff + mm_, :],
                                                consts["identb"],
                                                start=False,
                                                stop=mm_ == 3),
                             reads=[("c", "mdiag2"), ("c", "identb")], writes=[pskey],
                             tag="s_md")
                P.op("scalar", s_act(pb[:], ps[:, 0:512], AF.Exp,
                                     bias=bcol[:], scale=0.03125,
                                     accum=mc[:, 0:1]),
                     reads=[pskey, ("bcol",), ("pbA",), ("mcA",)],
                     writes=[("pbA",), ("mcA",)], tag="s_exp")
                P.op("vector", v_tt(lacc[:, 0:1], lacc[:, 0:1], mc[:, 0:1], ALU.add),
                     reads=[("lacc",), ("mcA",)], writes=[("lacc",)], tag="s_l")
                for kc in range(4):
                    P.dma_tr("scalar" if kc % 2 else "sync",
                             ptb[:, kc, :], pb[:, kc * 128:(kc + 1) * 128],
                             reads=[("pbA",)], writes=[("ptA", kc)], tag="s_tr")
                for kc in range(4):
                    for half in range(2):
                        P.op("tensor", t_matmul(psO[:, half * 512:(half + 1) * 512],
                                                ptb[:, kc, :],
                                                vtb[j][:, kc, half * 512:(half + 1) * 512],
                                                start=(kt == 0 and kc == 0),
                                                stop=(kt == nkb - 1 and kc == 3)),
                             reads=[("ptA", kc), ("vtb", j)], writes=[("psOA",)],
                             tag="pv_mm")
            P.op("vector", v_recip(lacc[:, 1:2], lacc[:, 0:1]),
                 reads=[("lacc",)], writes=[("lacc",)], tag="o_rec")
            for half in range(2):
                P.op("vector", v_ts(ot[:, half * 512:(half + 1) * 512],
                                    psO[:, half * 512:(half + 1) * 512],
                                    lacc[:, 1:2], None, op0=ALU.mult),
                     reads=[("psOA",), ("lacc",), ("ot",)], writes=[("ot",)],
                     tag="o_nrm")
            P.op("vector", v_red(mc[:, 1:2], ot[:], "sum"),
                 reads=[("ot",), ("mcA",)], writes=[("mcA",)], tag="ln_s")
            P.op("vector", v_ts(mc[:, 2:3], mc[:, 1:2], -1.0 / D, None, op0=ALU.mult),
                 reads=[("mcA",)], writes=[("mcA",)], tag="ln_m")
            P.op("scalar", s_act(pb.bitcast(F32)[:, 0:256] if False else ot[:, 0:1],
                                 ot[:, 0:1], AF.Square) if False else
                 s_act(mc[:, 3:4], mc[:, 3:4], AF.Square) if False else
                 s_act(mc[:, 3:4], mc[:, 2:3], AF.Square),
                 reads=[("mcA",)], writes=[("mcA",)], tag="ln_m2")
            P.op("scalar", s_act(sqA[:], ot[:, 0:512], AF.Square,
                                 accum=mc[:, 4:5]),
                 reads=[("ot",), ("sqA",), ("mcA",)], writes=[("sqA",), ("mcA",)],
                 tag="ln_q1")
            P.op("scalar", s_act(sqA[:], ot[:, 512:1024], AF.Square,
                                 accum=mc[:, 5:6]),
                 reads=[("ot",), ("sqA",), ("mcA",)], writes=[("sqA",), ("mcA",)],
                 tag="ln_q2")
            P.op("vector", v_tt(mc[:, 4:5], mc[:, 4:5], mc[:, 5:6], ALU.add),
                 reads=[("mcA",)], writes=[("mcA",)], tag="ln_q3")
            P.op("vector", v_ts(mc[:, 4:5], mc[:, 4:5], 1.0 / D, None, op0=ALU.mult),
                 reads=[("mcA",)], writes=[("mcA",)], tag="ln_q4")
            P.op("vector", v_tt(mc[:, 4:5], mc[:, 4:5], mc[:, 3:4], ALU.subtract),
                 reads=[("mcA",)], writes=[("mcA",)], tag="ln_v")
            P.op("scalar", s_act(mc[:, 4:5], mc[:, 4:5], AF.Ln, bias=1e-5),
                 reads=[("mcA",)], writes=[("mcA",)], tag="ln_l")
            P.op("scalar", s_act(mc[:, 4:5], mc[:, 4:5], AF.Exp, scale=-0.5),
                 reads=[("mcA",)], writes=[("mcA",)], tag="ln_e")
            P.op("vector", v_ts(ot[:], ot[:], mc[:, 2:3], None, op0=ALU.add),
                 reads=[("ot",), ("mcA",)], writes=[("ot",)], tag="ln_a1")
            P.op("vector", v_ts(ot[:], ot[:], mc[:, 4:5], None, op0=ALU.mult),
                 reads=[("ot",), ("mcA",)], writes=[("ot",)], tag="ln_a2")
            for dc in range(DC):
                j = dc % 2
                P.op("tensor", t_transpose(psB1[:, 0:128], ot[:, dc * 128:(dc + 1) * 128],
                                           sc["identf"][:]),
                     reads=[("ot",), ("c", "identf")], writes=[("psB1",)], tag="tr")
                P.dma("sync", x2st[j][:], x2p[dc * 128:(dc + 1) * 128, i * 128:(i + 1) * 128],
                      reads=[("x2stA", j)], writes=[("x2stA", j)], tag="ldx2")
                P.op("vector", v_tt(x3st[j][:], x2st[j][:], psB1[:, 0:128], ALU.add),
                     reads=[("x2stA", j), ("psB1",), ("x3stA", j)],
                     writes=[("x3stA", j)], tag="x3")
                P.dma("gpsimd", x3f[dc * 128:(dc + 1) * 128, i * 128:(i + 1) * 128],
                      x3st[j][:], reads=[("x3stA", j)], writes=[("x3o", dc, i)],
                      tag="st_x3")
    return core


def a1_host_inputs(inputs, consts_h, x2_f, qf_c, kf_c, vt_c, qn_c, kn_c):
    kfull = np.concatenate(kf_c, axis=2)
    vfull = np.concatenate(vt_c, axis=0)
    knfull = np.concatenate(kn_c, axis=1)
    qfull = np.concatenate(qf_c, axis=2)
    qnfull = np.concatenate(qn_c, axis=1)
    maps = []
    for c in range(NCORES):
        tiles = [8 * i + c for i in range(16)]
        cols = np.concatenate([np.arange(t * 128, (t + 1) * 128) for t in tiles])
        md = np.zeros((128, D, 1), np.float32)
        md2 = np.zeros((128, DC, 128), np.float32)
        for m in range(DC):
            for pp in range(128):
                for q in []:
                    pass
        pidx = np.arange(128)
        for m in range(DC):
            mask = (128 * m + pidx[:, None]) > (128 * c + pidx[None, :])
            md2[:, m, :] = np.where(mask, NEG, 0.0)
        m = dict(consts_h)
        m.update({
            "qfi": np.ascontiguousarray(qfull[:, :, cols]),
            "kfull": kfull, "vfull": vfull,
            "qni": np.ascontiguousarray(qnfull[:, cols]),
            "knfull": knfull,
            "x2p": np.ascontiguousarray(x2_f[:, cols]),
            "identf": np.eye(128, dtype=np.float32),
            "mdiag2": md2.astype(BF),
        })
        maps.append(m)
    return maps


def a1_reassemble(results):
    x3 = np.zeros((D, S), np.float32)
    for c in range(NCORES):
        tiles = [8 * i + c for i in range(16)]
        cols = np.concatenate([np.arange(t * 128, (t + 1) * 128) for t in tiles])
        x3[:, cols] = results[c]["x3f"]
    return x3


ECAP = 2926
ECAPP = 2944
ESLOTS = 7 * ECAPP
ESLOTS_PAD = 24 * 128 * 8


def build_moe_route():
    core = Core("moeroute")
    P = core.P
    x3p = core.inp("x3p", [D, TPC], F32)
    wr_d = core.inp("wr", [D, 8], F32)
    lng_d = core.inp("lngc", [128, DC], F32)
    lnb_d = core.inp("lnbc", [128, DC], F32)
    iota8_d = core.inp("iota8", [128, 8], F32)
    iotaInv_d = core.inp("iotaInv", [128, 8], F32)
    zero8_d = core.inp("zero8", [128, 8], F32)
    cneg8_d = core.inp("cneg8", [128, 8], F32)
    ht_o = core.out("ht", [TPC, D], BF16)
    hb_o = core.out("hb", [128, DC, TPC], BF16)
    route_o = core.out("route", [128, 16, 4], F32)
    consts = dev_consts(core)
    sc = {}
    for k, dd in (("iota8", iota8_d), ("iotaInv", iotaInv_d), ("zero8", zero8_d),
                  ("cneg8", cneg8_d)):
        sc[k] = core.sb_right(f"sc_{k}", [128, 8], F32)
        P.dma("sync", sc[k][:], dd, writes=[("c", k)], tag=f"ld_{k}")
    lngc = core.sb_right("lngc_sb", [128, DC], F32)
    P.dma("sync", lngc[:], lng_d, writes=[("lng",)], tag="ld_lng")
    lnbc = core.sb_right("lnbc_sb", [128, DC], F32)
    P.dma("sync", lnbc[:], lnb_d, writes=[("lnb",)], tag="ld_lnb")
    wrs = core.sb_right("wr_sb", [128, DC, 8], F32)
    P.dma("sync", wrs[:], wr_d.rearrange("(c p) e -> p c e", p=128),
          writes=[("wr",)], tag="ld_wr")
    P.barrier()
    with core.phase():
        xf3 = core.sb("xf3", [128, DC, TPC], F32)
        xb3 = core.sb("xb3", [128, DC, TPC], BF16)
        for dc in range(DC):
            for g in range(4):
                P.dma("scalar", xf3[:, dc, g * 512:(g + 1) * 512],
                      x3p[dc * 128:(dc + 1) * 128, g * 512:(g + 1) * 512],
                      reads=[("xf3", dc)], writes=[("xf3", dc)], tag="ldx3")
            P.op("vector", v_copy(xb3[:, dc, :], xf3[:, dc, :]),
                 reads=[("xf3", dc), ("xb3", dc)], writes=[("xb3", dc)], tag="cvx3")
        with core.phase():
            rows = emit_ln_stats_dram(core, lambda d_, a, b: xb3[:, d_, a:b],
                                      lambda d_: ("xb3", d_), TPC, consts, "lnr",
                                      xdt=BF16, compose=True)
        psA = core.psum("psAr")
        psB = core.psum("psBr")
        psL = core.psum("psLr")
        psT = core.psum("psTr")
        psH = core.psum("psHr", dtype=BF16, cols=1024)
        h32 = core.sb("h32", [128, DC, 512], F32)
        lgs = core.sb("lgs", [8, 512], F32)
        lgt = core.sb("lgt", [128, 8], F32)
        colA = core.sb("colA", [128, 8], F32)
        colB = core.sb("colB", [128, 8], F32)
        rsb = core.sb("rsb", [128, 16, 4], F32)
        hts = [core.sb(f"hts{j}", [128, D], BF16) for j in range(2)]
        for g in range(4):
            t0 = g * 512
            emit_ln_apply_dram(core, lambda d_, a, b: xf3[:, d_, a:b],
                               lambda d_: ("xf3", d_), rows, 0, 2, consts,
                               lambda d_: h32[:, d_, :], lambda d_: ("h32", d_),
                               t0, t0 + 512, psA, ("psAr",), psB, ("psBr",),
                               gcol_fn=lambda d_: lngc[:, d_:d_ + 1],
                               bcol_fn=lambda d_: lnbc[:, d_:d_ + 1],
                               gbkeys=[("lng",), ("lnb",)])
            for dc in range(DC):
                P.op("scalar", s_copy(xb3[:, dc, t0:t0 + 512], h32[:, dc, :]),
                     reads=[("h32", dc), ("xb3", dc)], writes=[("xb3", dc)], tag="hb_cv")
                P.dma("gpsimd", hb_o[:, dc, t0:t0 + 512], xb3[:, dc, t0:t0 + 512],
                      reads=[("xb3", dc)], writes=[("hbo", dc, g)], tag="st_hb")
                P.op("tensor", t_matmul(psL[0:8, 0:512], wrs[:, dc, :], h32[:, dc, :],
                                        start=dc == 0, stop=dc == DC - 1),
                     reads=[("wr",), ("h32", dc)], writes=[("psLr",)], tag="lg_mm")
            P.op("scalar", s_copy(lgs[:], psL[0:8, 0:512]),
                 reads=[("psLr",), ("lgs",)], writes=[("lgs",)], tag="lg_ev")
            for tt in range(4):
                j = tt % 2
                for dc in range(DC):
                    P.op("tensor", t_transpose(psH[:, dc * 128:(dc + 1) * 128],
                                               xb3[:, dc, t0 + tt * 128:t0 + (tt + 1) * 128],
                                               consts["identb"]),
                         reads=[("xb3", dc), ("c", "identb")], writes=[("psHr",)],
                         tag="ht_tr")
                P.op("vector", v_copy(hts[j][:], psH[:, 0:1024]),
                     reads=[("psHr",), ("hts", j)], writes=[("hts", j)], tag="ht_ev")
                P.dma("gpsimd", ht_o[t0 + tt * 128:t0 + (tt + 1) * 128, :], hts[j][:],
                      reads=[("hts", j)], writes=[("hto", g, tt)], tag="st_ht")
            for tt in range(4):
                qt = g * 4 + tt
                P.op("tensor", t_transpose(psT[:, 0:8], lgs[:, tt * 128:(tt + 1) * 128],
                                           consts["identf8"]),
                     reads=[("lgs",), ("c", "identf8")], writes=[("psTr",)], tag="tp_tr")
                P.op("vector", v_copy(lgt[:], psT[:, 0:8]),
                     reads=[("psTr",), ("lgt",)], writes=[("lgt",)], tag="tp_ev")
                P.op("vector", v_memset(lgt[:, 7:8], -3e38),
                     reads=[("lgt",)], writes=[("lgt",)], tag="tp_p7")
                P.op("vector", v_red(rsb[:, qt, 0:1] if False else colA[:, 0:1], lgt[:], "max"),
                     reads=[("lgt",), ("colA",)], writes=[("colA",)], tag="t_m1")
                P.op("vector", v_ts(colB[:], lgt[:], colA[:, 0:1], None, op0=ALU.is_equal),
                     reads=[("lgt",), ("colA",), ("colB",)], writes=[("colB",)], tag="t_e1")
                P.op("vector", v_tt(colB[:], colB[:], sc["iotaInv"][:], ALU.mult),
                     reads=[("colB",), ("c", "iotaInv")],
                     writes=[("colB",)], tag="t_c1")
                P.op("vector", v_red(colA[:, 1:2], colB[:], "max"),
                     reads=[("colB",), ("colA",)], writes=[("colA",)], tag="t_i1m")
                P.op("vector", v_ts(colA[:, 2:3], colA[:, 1:2], -1.0, 999.0,
                                    op0=ALU.mult, op1=ALU.add),
                     reads=[("colA",)], writes=[("colA",)], tag="t_i1")
                P.op("vector", v_ts(colB[:], sc["iota8"][:], colA[:, 2:3], None,
                                    op0=ALU.is_equal),
                     reads=[("c", "iota8"), ("colA",), ("colB",)], writes=[("colB",)],
                     tag="t_e2")
                P.op("vector", v_tt(colB[:], colB[:], sc["cneg8"][:], ALU.mult),
                     reads=[("colB",), ("c", "cneg8")], writes=[("colB",)],
                     tag="t_mskm")
                P.op("vector", v_tt(lgt[:], lgt[:], colB[:], ALU.add),
                     reads=[("colB",), ("lgt",)], writes=[("lgt",)],
                     tag="t_msk")
                P.op("vector", v_red(colA[:, 3:4], lgt[:], "max"),
                     reads=[("lgt",), ("colA",)], writes=[("colA",)], tag="t_m2")
                P.op("vector", v_ts(colB[:], lgt[:], colA[:, 3:4], None, op0=ALU.is_equal),
                     reads=[("lgt",), ("colA",), ("colB",)], writes=[("colB",)], tag="t_e3")
                P.op("vector", v_tt(colB[:], colB[:], sc["iotaInv"][:], ALU.mult),
                     reads=[("colB",), ("c", "iotaInv")],
                     writes=[("colB",)], tag="t_c2")
                P.op("vector", v_red(colA[:, 4:5], colB[:], "max"),
                     reads=[("colB",), ("colA",)], writes=[("colA",)], tag="t_i2m")
                P.op("vector", v_ts(colA[:, 5:6], colA[:, 4:5], -1.0, 999.0,
                                    op0=ALU.mult, op1=ALU.add),
                     reads=[("colA",)], writes=[("colA",)], tag="t_i2")
                P.op("vector", v_tt(colA[:, 6:7], colA[:, 3:4], colA[:, 0:1],
                                    ALU.subtract),
                     reads=[("colA",)], writes=[("colA",)], tag="t_d")
                P.op("scalar", s_act(colA[:, 6:7], colA[:, 6:7], AF.Exp),
                     reads=[("colA",)], writes=[("colA",)], tag="t_ex")
                P.op("vector", v_ts(colA[:, 7:8], colA[:, 6:7], 1.0, None, op0=ALU.add),
                     reads=[("colA",)], writes=[("colA",)], tag="t_d1")
                P.op("vector", v_recip(colA[:, 7:8], colA[:, 7:8]),
                     reads=[("colA",)], writes=[("colA",)], tag="t_g1")
                P.op("vector", v_tt(colA[:, 6:7], colA[:, 6:7], colA[:, 7:8], ALU.mult),
                     reads=[("colA",)], writes=[("colA",)], tag="t_g2")
                for (srci, dsti) in ((2, 0), (5, 1), (7, 2), (6, 3)):
                    P.op("vector", v_copy(rsb[:, qt, dsti:dsti + 1],
                                          colA[:, srci:srci + 1]),
                         reads=[("colA",), ("rsb",)], writes=[("rsb",)], tag="t_st")
        P.dma("gpsimd", route_o, rsb[:], reads=[("rsb",)], writes=[("rto",)], tag="st_rt")
    return core


def build_moe_plan():
    core = Core("moeplan")
    P = core.P
    i1t_d = core.inp("i1t", [128, 128], F32)
    i2t_d = core.inp("i2t", [128, 128], F32)
    g1t_d = core.inp("g1t", [128, 128], F32)
    g2t_d = core.inp("g2t", [128, 128], F32)
    triu_d = core.inp("triu", [128, 128], BF16)
    iota8_d = core.inp("iota8", [128, 8], F32)
    ones8b_d = core.inp("ones8b", [128, 8], BF16)
    iota128_d = core.inp("iota128", [128, 1], F32)
    cbig_d = core.inp("cbig", [128, 1], F32)
    eslot_o = core.out("eslot", [128, 128, 2], I32)
    geff_o = core.out("geff", [128, 128, 2], F32)
    consts = dev_consts(core)
    sc = {}
    for k, dd, shape, dt in (("i1t", i1t_d, [128, 128], F32), ("i2t", i2t_d, [128, 128], F32),
                             ("g1t", g1t_d, [128, 128], F32), ("g2t", g2t_d, [128, 128], F32),
                             ("triu", triu_d, [128, 128], BF16),
                             ("iota8", iota8_d, [128, 8], F32),
                             ("ones8b", ones8b_d, [128, 8], BF16),
                             ("iota128", iota128_d, [128, 1], F32),
                             ("cbig", cbig_d, [128, 1], F32)):
        sc[k] = core.sb_right(f"sc_{k}", shape, dt)
        P.dma("sync", sc[k][:], dd, writes=[("c", k)], tag=f"ld_{k}")
    P.barrier()
    with core.phase():
        running = core.sb("running", [1, 8], F32)
        P.op("vector", v_memset(running[:], 0.0), writes=[("run",)], tag="r0")
        oh = core.sb("oh", [128, 8], F32)
        ohb = core.sb("ohb", [128, 8], BF16)
        e1 = core.sb("e1", [128, 8], F32)
        cA = core.sb("cA", [128, 8], F32)
        st8 = core.sb("st8", [8, 8], F32)
        esl = core.sb("esl", [128, 128, 2], I32)
        gef = core.sb("gef", [128, 128, 2], F32)
        tki = core.sb("tki", [128, 2], I32)
        sli = core.sb("sli", [128, 1], I32)
        tkf = core.sb("tkf", [128, 1], F32)
        psC = core.psum("psC")
        psB2 = core.psum("psB2")
        psT = core.psum("psTp")
        for k in range(2):
            it = sc["i1t"] if k == 0 else sc["i2t"]
            gt = sc["g1t"] if k == 0 else sc["g2t"]
            ikey = ("c", "i1t" if k == 0 else "i2t")
            gkey = ("c", "g1t" if k == 0 else "g2t")
            for tau in range(128):
                icol = it[:, tau:tau + 1]
                P.op("vector", v_ts(oh[:], sc["iota8"][:], icol, None, op0=ALU.is_equal),
                     reads=[("c", "iota8"), ikey, ("oh",)], writes=[("oh",)], tag="p_oh")
                P.op("vector", v_copy(ohb[:], oh[:]),
                     reads=[("oh",), ("ohb",)], writes=[("ohb",)], tag="p_ohb")
                P.op("tensor", t_matmul(psC[:, 0:8], sc["triu"][:], ohb[:],
                                        start=True, stop=True),
                     reads=[("ohb",), ("c", "triu")], writes=[("psC",)], tag="p_cum")
                P.op("tensor", t_matmul(psB2[:, 0:8], consts["ones1"], running[:],
                                        start=True, stop=True),
                     reads=[("run",), ("c", "ones1")], writes=[("psB2",)], tag="p_bc")
                P.op("vector", v_tt(e1[:], psC[:, 0:8], oh[:], ALU.mult),
                     reads=[("psC",), ("oh",), ("e1",)], writes=[("e1",)], tag="p_x1")
                P.op("vector", v_red(cA[:, 0:1], e1[:], "sum"),
                     reads=[("e1",), ("cA",)], writes=[("cA",)], tag="p_cm")
                P.op("vector", v_tt(e1[:], psB2[:, 0:8], oh[:], ALU.mult),
                     reads=[("psB2",), ("oh",), ("e1",)], writes=[("e1",)], tag="p_x2")
                P.op("vector", v_red(cA[:, 1:2], e1[:], "sum"),
                     reads=[("e1",), ("cA",)], writes=[("cA",)], tag="p_bs")
                P.op("vector", v_tt(cA[:, 2:3], cA[:, 0:1], cA[:, 1:2], ALU.add),
                     reads=[("cA",)], writes=[("cA",)], tag="p_p1")
                P.op("vector", v_ts(cA[:, 2:3], cA[:, 2:3], -1.0, None, op0=ALU.add),
                     reads=[("cA",)], writes=[("cA",)], tag="p_p2")
                P.op("vector", v_ts(cA[:, 3:4], cA[:, 2:3], float(ECAP), None,
                                    op0=ALU.is_lt),
                     reads=[("cA",)], writes=[("cA",)], tag="p_kp")
                P.op("vector", v_ts(cA[:, 4:5], icol, float(ECAPP), None, op0=ALU.mult),
                     reads=[ikey, ("cA",)], writes=[("cA",)], tag="p_s1")
                P.op("vector", v_tt(cA[:, 4:5], cA[:, 4:5], cA[:, 2:3], ALU.add),
                     reads=[("cA",)], writes=[("cA",)], tag="p_s2")
                P.op("vector", v_tt(cA[:, 4:5], cA[:, 4:5], cA[:, 3:4], ALU.mult),
                     reads=[("cA",)], writes=[("cA",)], tag="p_s3a")
                P.op("vector", v_ts(cA[:, 6:7], cA[:, 3:4], -1e9, 1e9,
                                    op0=ALU.mult, op1=ALU.add),
                     reads=[("cA",)], writes=[("cA",)], tag="p_s3b")
                P.op("vector", v_tt(cA[:, 4:5], cA[:, 4:5], cA[:, 6:7], ALU.add),
                     reads=[("cA",)], writes=[("cA",)], tag="p_s3")
                P.op("vector", v_tt(cA[:, 5:6], gt[:, tau:tau + 1], cA[:, 3:4], ALU.mult),
                     reads=[gkey, ("cA",)], writes=[("cA",)], tag="p_ge")
                P.op("vector", v_copy(esl[:, tau, k:k + 1], cA[:, 4:5]),
                     reads=[("cA",), ("esl",)], writes=[("esl",)], tag="p_se")
                P.op("vector", v_copy(gef[:, tau, k:k + 1], cA[:, 5:6]),
                     reads=[("cA",), ("gef",)], writes=[("gef",)], tag="p_sg")
                P.op("tensor", t_matmul(psT[0:8, 0:8], ohb[:], sc["ones8b"][:],
                                        start=True, stop=True),
                     reads=[("ohb",), ("c", "ones8b")], writes=[("psTp",)], tag="p_tt")
                P.op("vector", v_copy(st8[:], psT[0:8, 0:8]),
                     reads=[("psTp",), ("st8",)], writes=[("st8",)], tag="p_tte")
                P.op("tensor", t_transpose(psT[0:1, 0:8], st8[:, 0:1], consts["identf8"]),
                     reads=[("st8",), ("c", "identf8")], writes=[("psTp",)], tag="p_ttr")
                P.op("vector", v_tt(running[:], running[:], psT[0:1, 0:8], ALU.add),
                     reads=[("run",), ("psTp",)], writes=[("run",)], tag="p_ru")
        P.dma("gpsimd", eslot_o, esl[:], reads=[("esl",)], writes=[("eo",)], tag="st_es")
        P.dma("gpsimd", geff_o, gef[:], reads=[("gef",)], writes=[("go",)], tag="st_ge")
    return core


def build_moe_expert():
    core = Core("moeexp")
    P = core.P
    ht_d = core.inp("htf", [S, D], BF16)
    stok_d = core.inp("stoki", [ECAPP, 1], I32)
    w1e_d = core.inp("w1e", [D, 4 * D], F32)
    w3e_d = core.inp("w3e", [D, 4 * D], F32)
    w2e_d = core.inp("w2e", [4 * D, D], F32)
    w1s_d = core.inp("w1s", [D, 4 * D], F32)
    w3s_d = core.inp("w3s", [D, 4 * D], F32)
    w2s_d = core.inp("w2s", [4 * D, D], F32)
    hb_d = core.inp("hbi", [128, DC, TPC], BF16)
    ye_o = core.out("ye", [ECAPP, D], BF16)
    ysh_o = core.out("ysh", [D, TPC], F32)
    consts = dev_consts(core)

    def ffn_pass(tag, ngroups, tiles_of, w1d, w3d, w2d, fill_x, emit_out):
        with core.phase():
            wl = WeightLoader(core, stage_cols=1024, consts=consts)
            xf = core.sb(f"xf_{tag}", [128, DC, 512], BF16)
            tb = core.sb(f"tb_{tag}", [128, 32, 512], BF16)
            wsl = [core.sb(f"wsl_{tag}{j}", [128, DC, 128], BF16) for j in range(2)]
            w2sl = [core.sb(f"w2sl_{tag}{j}", [128, 32, 128], BF16) for j in range(2)]
            sg = core.sb(f"sg_{tag}", [128, 512], BF16)
            psM = [core.psum(f"psM_{tag}{j}") for j in range(2)]
            psW = core.psum(f"psW_{tag}")
            aux = dict(psT=core.psum(f"psT_{tag}", dtype=BF16),
                       gb=core.sb(f"gb_{tag}", [128, D], BF16),
                       tst=[core.sb(f"tst_{tag}{j}", [128, 512], BF16) for j in range(2)],
                       fst=[core.sb(f"fst_{tag}{j}", [128, 512], F32) for j in range(2)],
                       idx=core.sb(f"idx_{tag}", [128, 1], I32))
            for g in range(ngroups):
                nt = tiles_of[g]
                wid = nt * 128
                fill_x(g, nt, xf, ("xf", tag), aux)
                for oc in range(32):
                    j = oc % 2
                    st = wl.stage[j]
                    for wd, wnm in ((w1d, "w1"), (w3d, "w3")):
                        P.dma("sync", st[:, 0:1024].rearrange("p (a b) -> p a b", b=128),
                              wd[:, oc * 128:(oc + 1) * 128].rearrange("(a p) b -> p a b", p=128),
                              reads=[("wstage", j)], writes=[("wstage", j)],
                              tag=f"ld_{tag}w")
                        psx, pk = ((psM[j], ("psM", tag, j)) if wnm == "w1"
                                   else (psW, ("psW", tag)))
                        P.op("scalar", s_copy(wsl[j][:].rearrange("p c k -> p (c k)"),
                                              st[:, 0:1024]),
                             reads=[("wstage", j)], writes=[("wsl", tag, j)],
                             tag=f"cv_{tag}w")
                        for ki in range(DC):
                            P.op("tensor", t_matmul(psx[:, 0:wid], wsl[j][:, ki, :],
                                                    xf[:, ki, 0:wid],
                                                    start=ki == 0, stop=ki == DC - 1),
                                 reads=[("wsl", tag, j), ("xf", tag)], writes=[pk],
                                 tag=f"{tag}_mm")
                    P.op("scalar", s_act(sg[:, 0:wid], psM[j][:, 0:wid], AF.Silu),
                         reads=[("psM", tag, j), ("sg", tag)], writes=[("sg", tag)],
                         tag=f"{tag}_silu")
                    P.op("vector", v_tt(tb[:, oc, 0:wid], psW[:, 0:wid], sg[:, 0:wid],
                                        ALU.mult),
                         reads=[("psW", tag), ("sg", tag), ("tb", tag, oc)],
                         writes=[("tb", tag, oc)], tag=f"{tag}_t")
                for do in range(DC):
                    j = do % 2
                    st2 = wl.stage[j]
                    for rc in range(4):
                        P.dma("sync", st2[:, 0:1024].rearrange("p (a b) -> p a b", a=8),
                              w2d[rc * 1024:(rc + 1) * 1024, do * 128:(do + 1) * 128]
                              .rearrange("(a p) b -> p a b", p=128),
                              reads=[("wstage", j)], writes=[("wstage", j)],
                              tag=f"ld_{tag}w2")
                        P.op("scalar", s_copy(w2sl[j][:, rc * 8:(rc + 1) * 8, :],
                                              st2[:, 0:1024].rearrange("p (a b) -> p a b", a=8)),
                             reads=[("wstage", j)], writes=[("w2sl", tag, j)],
                             tag=f"cv_{tag}w2")
                    for hg in range(32):
                        P.op("tensor", t_matmul(psM[j][:, 0:wid], w2sl[j][:, hg, :],
                                                tb[:, hg, 0:wid], start=hg == 0,
                                                stop=hg == 31),
                             reads=[("w2sl", tag, j), ("tb", tag, hg)],
                             writes=[("psM", tag, j)], tag=f"{tag}_m2")
                    emit_out(g, nt, do, psM[j], ("psM", tag, j), aux, j)

    groups_rt = [4, 4, 4, 4, 4, 3]

    def fill_rt(g, nt, xf, xkey, aux):
        base = sum(groups_rt[:g])
        for t in range(nt):
            P.dma("sync", aux["idx"][:], stok_d[(base + t) * 128:(base + t + 1) * 128, :],
                  reads=[("idxk",)], writes=[("idxk",)], tag="ld_idx")
            P.op("vector", v_memset(aux["gb"][:], 0.0),
                 reads=[("gbk",)], writes=[("gbk",)], tag="gz")
            P.op("gpsimd", (lambda eng, o=aux["gb"][:], inap=ht_d,
                            ix=aux["idx"][:]:
                            eng.indirect_dma_start(o, None, inap,
                                                   IndirectOffsetOnAxis(ix, 0),
                                                   bounds_check=S - 1,
                                                   oob_is_err=False)),
                 reads=[("idxk",), ("gbk",)], writes=[("gbk",)], dma=True, tag="gth")
            for dc in range(DC):
                P.op("tensor", t_transpose(aux["psT"][:, 0:128],
                                           aux["gb"][:, dc * 128:(dc + 1) * 128],
                                           consts["identb"]),
                     reads=[("gbk",), ("c", "identb")], writes=[("psTk",)], tag="gtr")
                P.op("vector", v_copy(xf[:, dc, t * 128:(t + 1) * 128],
                                      aux["psT"][:, 0:128]),
                     reads=[("psTk",), xkey], writes=[xkey], tag="gte")

    def emit_rt(g, nt, do, ps, pskey, aux, j):
        base = sum(groups_rt[:g])
        P.op("scalar", s_copy(aux["tst"][j][:, 0:nt * 128], ps[:, 0:nt * 128]),
             reads=[pskey, ("tstk", j)], writes=[("tstk", j)], tag="ye_ev")
        for t in range(nt):
            P.op("tensor", t_transpose(aux["psT"][:, 0:128],
                                       aux["tst"][j][:, t * 128:(t + 1) * 128],
                                       consts["identb"]),
                 reads=[("tstk", j), ("c", "identb")], writes=[("psTk",)], tag="ye_tr")
            P.op("vector", v_copy(aux["gb"][:, 0:128], aux["psT"][:, 0:128]),
                 reads=[("psTk",), ("gbk",)], writes=[("gbk",)], tag="ye_te")
            P.dma("gpsimd", ye_o[(base + t) * 128:(base + t + 1) * 128,
                                 do * 128:(do + 1) * 128], aux["gb"][:, 0:128],
                  reads=[("gbk",)], writes=[("yeo", base + t, do)], tag="st_ye")

    def fill_sh(g, nt, xf, xkey, aux):
        for dc in range(DC):
            P.dma("sync", xf[:, dc, :], hb_d[:, dc, g * 512:(g + 1) * 512],
                  reads=[xkey], writes=[xkey], tag="ld_hb")

    def emit_sh(g, nt, do, ps, pskey, aux, j):
        P.op("scalar", s_copy(aux["fst"][j][:], ps[:, 0:512]),
             reads=[pskey, ("fstk", j)], writes=[("fstk", j)], tag="sh_ev")
        P.dma("gpsimd", ysh_o[do * 128:(do + 1) * 128, g * 512:(g + 1) * 512],
              aux["fst"][j][:], reads=[("fstk", j)], writes=[("ysho", do, g)],
              tag="st_sh")

    ffn_pass("rt", 6, groups_rt, w1e_d, w3e_d, w2e_d, fill_rt, emit_rt)
    ffn_pass("sh", 4, [4, 4, 4, 4], w1s_d, w3s_d, w2s_d, fill_sh, emit_sh)
    return core


def build_moe_fin():
    core = Core("moefin")
    P = core.P
    ye_d = core.inp("yef", [ESLOTS, D], BF16)
    es_d = core.inp("esi", [128, 16, 2], I32)
    ge_d = core.inp("gei", [128, 16, 2], F32)
    ysh_d = core.inp("yshi", [D, TPC], F32)
    x3p = core.inp("x3pf", [D, TPC], F32)
    identf_d = core.inp("identff", [128, 128], F32)
    x4f = core.out("x4f", [D, TPC], F32)
    consts = dev_consts(core)
    idf = core.sb_right("idf", [128, 128], F32)
    P.dma("sync", idf[:], identf_d, writes=[("c", "idf")], tag="ld_idf")
    ess = core.sb_right("ess", [128, 16, 2], I32)
    P.dma("sync", ess[:], es_d, writes=[("c", "es")], tag="ld_es")
    ges = core.sb_right("ges", [128, 16, 2], F32)
    P.dma("sync", ges[:], ge_d, writes=[("c", "ge")], tag="ld_ge")
    P.barrier()
    outf = core.sb_right("outfm", [128, DC, TPC], F32)
    with core.phase():
        gb = [core.sb(f"gbf{j}", [128, D], BF16) for j in range(2)]
        rsum = core.sb("rsum", [128, D], F32)
        yst = [core.sb(f"ystm{j}", [128, 512], F32) for j in range(2)]
        psT = core.psum("psTm")
        for qt in range(16):
            for k in range(2):
                P.op("vector", v_memset(gb[k][:], 0.0),
                     reads=[("gbm", k)], writes=[("gbm", k)], tag="fz")
                P.op("gpsimd", (lambda eng, o=gb[k][:], ix=ess[:, qt, k:k + 1]:
                                eng.indirect_dma_start(o, None, ye_d,
                                                       IndirectOffsetOnAxis(ix, 0),
                                                       bounds_check=ESLOTS - 1,
                                                       oob_is_err=False)),
                     reads=[("c", "es"), ("gbm", k)], writes=[("gbm", k)],
                     dma=True, tag="fgth")
            P.op("vector", v_ts(rsum[:], gb[0][:], ges[:, qt, 0:1], None, op0=ALU.mult),
                 reads=[("gbm", 0), ("c", "ge"), ("rsum",)], writes=[("rsum",)],
                 tag="fs1")
            P.op("vector", v_ts(gb[1][:], gb[1][:], ges[:, qt, 1:2], None,
                                op0=ALU.mult),
                 reads=[("gbm", 1), ("c", "ge")], writes=[("gbm", 1)], tag="fs2")
            P.op("vector", v_tt(rsum[:], rsum[:], gb[1][:], ALU.add),
                 reads=[("rsum",), ("gbm", 1)], writes=[("rsum",)], tag="fs3")
            for dc in range(DC):
                j = dc % 2
                P.op("tensor", t_transpose(psT[:, 0:128], rsum[:, dc * 128:(dc + 1) * 128],
                                           idf[:]),
                     reads=[("rsum",), ("c", "idf")], writes=[("psTm",)], tag="ftr")
                P.dma("scalar", yst[j][:, 0:128],
                      ysh_d[dc * 128:(dc + 1) * 128, qt * 128:(qt + 1) * 128],
                      reads=[("ystm", j)], writes=[("ystm", j)], tag="ld_ysh")
                P.op("vector", v_tt(outf[:, dc, qt * 128:(qt + 1) * 128], yst[j][:, 0:128],
                                    psT[:, 0:128], ALU.add),
                     reads=[("ystm", j), ("psTm",), ("outf", dc)],
                     writes=[("outf", dc)], tag="fadd")
    with core.phase():
        outb = core.sb("outbm", [128, DC, TPC], BF16)
        for dc in range(DC):
            P.op("vector", v_copy(outb[:, dc, :], outf[:, dc, :]),
                 reads=[("outf", dc)], writes=[("outbm", dc)], tag="fcv")
        with core.phase():
            rows = emit_ln_stats_dram(core, lambda d_, a, b: outb[:, d_, a:b],
                                      lambda d_: ("outbm", d_), TPC, consts, "lnm",
                                      xdt=BF16, compose=False)
        psA = core.psum("psAm")
        psB = core.psum("psBm")
        xst = [core.sb(f"xstm{j}", [128, 512], F32) for j in range(2)]
        ost = [core.sb(f"ostm{j}", [128, 512], F32) for j in range(2)]
        for g in range(4):
            t0 = g * 512
            rows.bcast(psA, ("psAm",), 0, t0, t0 + 512, consts)
            rows.bcast(psB, ("psBm",), 1, t0, t0 + 512, consts)
            for dc in range(DC):
                j = (dc + g) % 2
                P.dma("sync", xst[j][:], x3p[dc * 128:(dc + 1) * 128, t0:t0 + 512],
                      reads=[("xstm", j)], writes=[("xstm", j)], tag="ld_x3")
                P.op("vector", v_tt(ost[j][:], outf[:, dc, t0:t0 + 512], psA[:, 0:512],
                                    ALU.add),
                     reads=[("outf", dc), ("psAm",), ("ostm", j)], writes=[("ostm", j)],
                     tag="f_mu")
                P.op("vector", v_tt(ost[j][:], ost[j][:], psB[:, 0:512], ALU.mult),
                     reads=[("ostm", j), ("psBm",)], writes=[("ostm", j)], tag="f_rs")
                P.op("vector", v_tt(xst[j][:], xst[j][:], ost[j][:], ALU.add),
                     reads=[("xstm", j), ("ostm", j)], writes=[("xstm", j)], tag="f_x4")
                P.dma("gpsimd", x4f[dc * 128:(dc + 1) * 128, t0:t0 + 512], xst[j][:],
                      reads=[("xstm", j)], writes=[("x4o", dc, g)], tag="st_x4")
    return core


def moer_host_inputs(inputs, consts_h, x3_f):
    wr = np.zeros((D, 8), np.float32)
    wr[:, 0:7] = np.asarray(inputs["L1_ffn_Wr"])
    iota8 = np.tile(np.arange(8, dtype=np.float32), (128, 1))
    maps = []
    for c in range(NCORES):
        m = dict(consts_h)
        m.update({
            "x3p": np.ascontiguousarray(x3_f[:, c * TPC:(c + 1) * TPC]),
            "wr": wr,
            "lngc": np.ascontiguousarray(np.asarray(inputs["L1_ffn_ln_g"]).reshape(-1, 128).T),
            "lnbc": np.ascontiguousarray(np.asarray(inputs["L1_ffn_ln_b"]).reshape(-1, 128).T),
            "iota8": iota8,
            "iotaInv": 999.0 - iota8,
            "zero8": np.zeros((128, 8), np.float32),
            "cneg8": np.full((128, 8), -1e38, np.float32),
        })
        maps.append(m)
    return maps


def moep_host_inputs(consts_h, route_c):
    i1 = np.concatenate([r[:, :, 0].T.reshape(-1) for r in route_c])
    i2 = np.concatenate([r[:, :, 1].T.reshape(-1) for r in route_c])
    g1 = np.concatenate([r[:, :, 2].T.reshape(-1) for r in route_c])
    g2 = np.concatenate([r[:, :, 3].T.reshape(-1) for r in route_c])

    def tiled(v):
        return np.ascontiguousarray(v.reshape(128, 128).T.astype(np.float32))
    ii, jj = np.mgrid[0:128, 0:128]
    m0 = dict(consts_h)
    m0.update({
        "i1t": tiled(i1), "i2t": tiled(i2), "g1t": tiled(g1), "g2t": tiled(g2),
        "triu": np.where(ii <= jj, 1.0, 0.0).astype(BF),
        "iota8": np.tile(np.arange(8, dtype=np.float32), (128, 1)),
        "ones8b": np.ones((128, 8), BF),
        "iota128": np.arange(128, dtype=np.float32).reshape(128, 1),
        "cbig": np.full((128, 1), 1e9, np.float32),
    })
    return [m0] * NCORES


def moee_host_inputs(inputs, consts_h, ht_full, stok, hb_c):
    W1 = np.asarray(inputs["L1_ffn_W1"])
    W3 = np.asarray(inputs["L1_ffn_W3"])
    W2 = np.asarray(inputs["L1_ffn_W2"])
    maps = []
    for c in range(NCORES):
        if c < 7:
            w1e, w3e, w2e = W1[c], W3[c], W2[c]
        else:
            w1e = np.zeros((D, 4 * D), np.float32)
            w3e = np.zeros((D, 4 * D), np.float32)
            w2e = np.zeros((4 * D, D), np.float32)
        m = dict(consts_h)
        m.update({
            "htf": ht_full,
            "stoki": stok[c * ECAPP:(c + 1) * ECAPP].reshape(ECAPP, 1),
            "w1e": np.ascontiguousarray(w1e), "w3e": np.ascontiguousarray(w3e),
            "w2e": np.ascontiguousarray(w2e),
            "w1s": np.asarray(inputs["L1_ffn_W1s"]),
            "w3s": np.asarray(inputs["L1_ffn_W3s"]),
            "w2s": np.asarray(inputs["L1_ffn_W2s"]),
            "hbi": hb_c[c],
        })
        maps.append(m)
    return maps


def moef_host_inputs(consts_h, ye_full, es, ge, ysh_c, x3_f):
    maps = []
    for c in range(NCORES):
        m = dict(consts_h)
        m.update({
            "yef": ye_full,
            "esi": es[:, c * 16:(c + 1) * 16, :] if es.shape[1] == 128 else es[c],
            "gei": ge[:, c * 16:(c + 1) * 16, :] if ge.shape[1] == 128 else ge[c],
            "yshi": ysh_c[c],
            "x3pf": np.ascontiguousarray(x3_f[:, c * TPC:(c + 1) * TPC]),
            "identff": np.eye(128, dtype=np.float32),
        })
        maps.append(m)
    return maps


_BUILT = {}


def _get(name, builder, *args):
    if name not in _BUILT:
        c = builder(*args)
        _BUILT[name] = (c, c.finish())
    return _BUILT[name][1]


def kernel(**inputs):
    consts_h = host_consts()
    run = lambda nc, maps: run_bass_kernel_spmd(nc, maps, core_ids=list(range(NCORES)))

    nc = _get("l0a", build_l0attn)
    res = run(nc, l0a_host_inputs(inputs, consts_h))
    x1_f = np.zeros((D, S), np.float32)
    for c in range(NCORES):
        o = res.results[c]["x1f"]
        x1_f[:, c * TPC:(c + 1) * TPC] = o.transpose(1, 0, 2).reshape(D, TPC)

    state = x1_f
    for li in range(6):
        first = li == 0
        nc = _get(f"f0_{first}", build_f0pair, first)
        res = run(nc, f0_host_inputs(inputs, consts_h, li, state, first))
        state = np.concatenate([res.results[c]["yout"] for c in range(NCORES)], axis=1)

    nc = _get("f0fin", build_f0fin)
    res = run(nc, f0fin_host_inputs(inputs, consts_h, x1_f, state))
    x2_f = np.concatenate([res.results[c]["x2f"] for c in range(NCORES)], axis=1)
    qf_c = [res.results[c]["qf"] for c in range(NCORES)]
    kf_c = [res.results[c]["kf"] for c in range(NCORES)]
    vt_c = [res.results[c]["vt"] for c in range(NCORES)]
    qn_c = [res.results[c]["qn"] for c in range(NCORES)]
    kn_c = [res.results[c]["kn"] for c in range(NCORES)]

    nc = _get("a1", build_a1attn)
    res = run(nc, a1_host_inputs(inputs, consts_h, x2_f, qf_c, kf_c, vt_c, qn_c, kn_c))
    x3_f = a1_reassemble([res.results[c] for c in range(NCORES)])

    nc = _get("moer", build_moe_route)
    res = run(nc, moer_host_inputs(inputs, consts_h, x3_f))
    route_c = [res.results[c]["route"] for c in range(NCORES)]
    ht_full = np.concatenate([res.results[c]["ht"] for c in range(NCORES)], axis=0)
    hb_c = [res.results[c]["hb"] for c in range(NCORES)]

    nc = _get("moep", build_moe_plan)
    res = run(nc, moep_host_inputs(consts_h, route_c))
    r0 = res.results[0]
    eslot = r0["eslot"]
    geff = r0["geff"]
    nc = _get("moes", build_moe_scat)
    stoki = np.full((ESLOTS_PAD, 2), 2 ** 30, np.int32)
    for ph in range(8):
        maps = []
        tok = (np.arange(128)[:, None] + 128 * (16 * ph + np.arange(16))[None, :]
               ).astype(np.float32)
        m0 = dict(consts_h)
        m0.update({"stin": stoki,
                   "eslsl": np.ascontiguousarray(eslot[:, 16 * ph:16 * ph + 16, :]),
                   "tokb": tok})
        res = run(nc, [m0] * NCORES)
        stoki = res.results[0]["stok"]
    stok = np.ascontiguousarray(stoki[:, 0])

    nc = _get("moee", build_moe_expert)
    res = run(nc, moee_host_inputs(inputs, consts_h, ht_full, stok, hb_c))
    ye_full = np.concatenate([res.results[c]["ye"] for c in range(7)], axis=0)
    ysh_c = [res.results[c]["ysh"] for c in range(NCORES)]

    es_c = [np.ascontiguousarray(eslot[:, c * 16:(c + 1) * 16, :]) for c in range(NCORES)]
    ge_c = [np.ascontiguousarray(geff[:, c * 16:(c + 1) * 16, :]) for c in range(NCORES)]
    nc = _get("moef", build_moe_fin)
    maps = []
    for c in range(NCORES):
        m = dict(consts_h)
        m.update({"yef": ye_full, "esi": es_c[c], "gei": ge_c[c], "yshi": ysh_c[c],
                  "x3pf": np.ascontiguousarray(x3_f[:, c * TPC:(c + 1) * TPC]),
                  "identff": np.eye(128, dtype=np.float32)})
        maps.append(m)
    res = run(nc, maps)
    x4_f = np.concatenate([res.results[c]["x4f"] for c in range(NCORES)], axis=1)
    return np.ascontiguousarray(x4_f.T).reshape(1, S, D)


def build_moe_scat():
    core = Core("moescat")
    P = core.P
    stin_d = core.inp("stin", [ESLOTS_PAD, 2], I32)
    esl_d = core.inp("eslsl", [128, 16, 2], I32)
    tok_d = core.inp("tokb", [128, 16], F32)
    stok_o = core.out("stok", [ESLOTS_PAD, 2], I32)
    consts = dev_consts(core)
    with core.phase():
        srow = core.sb("srow", [1, 4096], I32)
        for cch in range(ESLOTS_PAD * 2 // 4096):
            P.dma("sync", srow[:],
                  stin_d.rearrange("a b -> (a b)")[cch * 4096:(cch + 1) * 4096].unsqueeze(0),
                  reads=[("srow",)], writes=[("srow",)], tag="ld_st")
            P.dma("gpsimd", stok_o.rearrange("a b -> (a b)")[cch * 4096:(cch + 1) * 4096]
                  .unsqueeze(0), srow[:], reads=[("srow",)],
                  writes=[("stok", cch)], tag="st_st")
        esl = core.sb("esl_sb", [128, 16, 2], I32)
        P.dma("sync", esl[:], esl_d, writes=[("esl",)], tag="ld_esl")
        tokb = core.sb("tokb_sb", [128, 16], F32)
        P.dma("sync", tokb[:], tok_d, writes=[("tokb",)], tag="ld_tok")
        tki = core.sb("tki2", [128, 2], I32)
        for tt in range(16):
            for k in range(2):
                P.op("vector", v_copy(tki[:, 0:1], tokb[:, tt:tt + 1]),
                     reads=[("tokb",), ("tki",)], writes=[("tki",)], tag="s_t0")
                P.op("vector", v_copy(tki[:, 1:2], tokb[:, tt:tt + 1]),
                     reads=[("tokb",), ("tki",)], writes=[("tki",)], tag="s_t1")
                P.op("gpsimd", (lambda eng, o=stok_o, idxap=esl[:, tt, k:k + 1],
                                inap=tki[:, 0:2]:
                                eng.indirect_dma_start(o, IndirectOffsetOnAxis(idxap, 0),
                                                       inap, None,
                                                       bounds_check=ESLOTS_PAD - 1,
                                                       oob_is_err=False)),
                     reads=[("tki",), ("esl",)] + [("stok", cc) for cc in
                                                   range(ESLOTS_PAD * 2 // 4096)],
                     writes=[("stok", 0)], dma=True, tag="s_sc")
    return core


# version 03
# speedup vs baseline: 1.0100x; 1.0100x over previous
import contextlib
import numpy as np
import ml_dtypes
import concourse.bass as bass
import concourse.mybir as mybir
from concourse.bass_utils import run_bass_kernel_spmd
from concourse.bass_core_rust import IndirectOffsetOnAxis

F32 = mybir.dt.float32
BF16 = mybir.dt.bfloat16
I32 = mybir.dt.int32
AF = mybir.ActivationFunctionType
ALU = mybir.AluOpType
AX = mybir.AxisListType
BF = ml_dtypes.bfloat16

S = 16384
D = 1024
NCORES = 8
TPC = S // NCORES
DC = D // 128
NEG = -30000.0


class Prog:

    def __init__(self, nc):
        self.nc = nc
        self.ops = []
        self.last_writer = {}
        self.readers = {}
        self.barrier_deps = set()

    NDMASEM = 12

    @staticmethod
    def semkey(o):
        if o["dma"]:
            import zlib
            h = zlib.adler32(repr(o["writes0"]).encode()) % Prog.NDMASEM
            return ("D", o["engine"], h)
        return ("C", o["engine"])

    def barrier(self):
        last = {}
        for o in self.ops:
            last[Prog.semkey(o)] = o["i"]
        self.barrier_deps = set(last.values())

    def op(self, engine, fn, reads=(), writes=(), dma=False, tag=""):
        deps = set(self.barrier_deps)
        for k in reads:
            w = self.last_writer.get(k)
            if w is not None:
                deps.add(w)
        for k in writes:
            w = self.last_writer.get(k)
            if w is not None:
                deps.add(w)
            for r in self.readers.get(k, ()):
                deps.add(r)
        i = len(self.ops)
        wl = list(writes)
        self.ops.append(dict(i=i, engine=engine, fn=fn, deps=deps, dma=dma,
                             tag=tag, signals=False,
                             writes0=(wl[0] if wl else tag)))
        for k in reads:
            self.readers.setdefault(k, []).append(i)
        for k in writes:
            self.last_writer[k] = i
            self.readers[k] = [i]
        return i

    def dma(self, engine, out, in_, tag="", reads=(), writes=(), **kw):
        def fn(eng, out=out, in_=in_, kw=kw):
            return eng.dma_start(out=out, in_=in_, **kw)
        return self.op(engine, fn, reads=reads, writes=writes, dma=True, tag=tag)

    def dma_tr(self, engine, out, in_, tag="", reads=(), writes=()):
        def fn(eng, out=out, in_=in_):
            return eng.dma_start_transpose(out, in_)
        return self.op(engine, fn, reads=reads, writes=writes, dma=True, tag=tag)

    def emit(self, stack):
        nc = self.nc

        for o in self.ops:
            o["sem"] = Prog.semkey(o)

        def needs_edge(dep, o):
            if dep["dma"]:
                return True
            if dep["engine"] != o["engine"] or o["dma"]:
                return True
            return dep["engine"] != "tensor"

        for o in self.ops:
            if o["dma"]:
                o["signals"] = True
            for d in o["deps"]:
                dep = self.ops[d]
                if needs_edge(dep, o):
                    dep["signals"] = True
        cum = {}
        for o in self.ops:
            s = o["sem"]
            if o["signals"]:
                cum[s] = cum.get(s, 0) + (16 if o["dma"] else 1)
            o["cum"] = cum.get(s, 0)
        sems = {}
        for s in cum:
            sems[s] = stack.enter_context(
                nc.semaphore(f"sem_{'_'.join(str(x) for x in s)}"))
        est = {}
        for o in self.ops:
            est.setdefault(o["engine"], []).append(o)
        for eng_name, ops in est.items():
            inherited = {}
            for o in ops:
                waits = {}

                def add_wait(s, v):
                    if v > inherited.get(s, 0):
                        waits[s] = max(waits.get(s, 0), v)

                for d in o["deps"]:
                    dep = self.ops[d]
                    if not needs_edge(dep, o):
                        continue
                    add_wait(dep["sem"], dep["cum"])
                if o["dma"]:
                    prev = o["cum"] - 16
                    add_wait(o["sem"], prev)
                o["waits"] = waits
                for s, v in waits.items():
                    inherited[s] = max(inherited.get(s, 0), v)

        totals = dict(cum)

        if getattr(self, "debug_dump", None):
            with open(self.debug_dump, "w") as f:
                for eng_name, ops in est.items():
                    f.write(f"==== engine {eng_name} ====\n")
                    for o in ops:
                        w = " ".join(f"wait({s},{v})" for s, v in
                                     sorted(o["waits"].items(), key=str))
                        sg = f" inc[{o['sem']}]->{o['cum']}" if o["signals"] else ""
                        f.write(f"  [{o['i']:5d}] {o['tag']:<16} {w}{sg}\n")

        with nc.Block() as block:
            def make_body(engine):
                def body(eng):
                    for o in est.get(engine, []):
                        for s, v in sorted(o["waits"].items(), key=str):
                            eng.wait_ge(sems[s], v)
                        inst = o["fn"](eng)
                        if o["signals"]:
                            if inst is None:
                                raise RuntimeError(f"op '{o['tag']}' returned no instruction")
                            inst.then_inc(sems[o["sem"]], 16 if o["dma"] else 1)
                    for s, t in totals.items():
                        if s[0] == "D" and s[1] == engine:
                            eng.wait_ge(sems[s], t)
                return body
            block.tensor(make_body("tensor"))
            block.vector(make_body("vector"))
            block.scalar(make_body("scalar"))
            block.gpsimd(make_body("gpsimd"))
            block.sync(make_body("sync"))


def ceil_div(a, b):
    return (a + b - 1) // b


def t_matmul(ps_ap, lhsT_ap, rhs_ap, start, stop, tile_position=None):
    def fn(eng, a=ps_ap, l=lhsT_ap, r=rhs_ap, st=start, sp=stop, tp=tile_position):
        return eng.matmul(a, l, r, start=st, stop=sp, tile_position=tp)
    return fn


def t_transpose(ps_ap, in_ap, ident_ap):
    def fn(eng, a=ps_ap, i=in_ap, d=ident_ap):
        return eng.transpose(a, i, d)
    return fn


def v_tt(out_ap, a_ap, b_ap, op):
    def fn(eng, o=out_ap, a=a_ap, b=b_ap, op=op):
        return eng.tensor_tensor(o, a, b, op)
    return fn


def v_ts(out_ap, a_ap, s1, s2=None, op0=ALU.mult, op1=ALU.add):
    def fn(eng, o=out_ap, a=a_ap, s1=s1, s2=s2, op0=op0, op1=op1):
        if s2 is None:
            return eng.tensor_single_scalar(o, a, s1, op0)
        return eng.tensor_scalar(o, a, s1, s2, op0, op1)
    return fn


def v_copy(out_ap, in_ap):
    def fn(eng, o=out_ap, a=in_ap):
        return eng.tensor_copy(o, a)
    return fn


def v_red(out_ap, in_ap, op="max"):
    def fn(eng, o=out_ap, a=in_ap, op=op):
        if op == "max":
            return eng.reduce_max(o, a, axis=AX.X)
        return eng.reduce_sum(o, a, axis=AX.X)
    return fn


def v_recip(out_ap, in_ap):
    def fn(eng, o=out_ap, a=in_ap):
        return eng.reciprocal(o, a)
    return fn


def v_memset(ap, val):
    def fn(eng, a=ap, v=val):
        return eng.memset(a, v)
    return fn


def v_select(out_ap, mask_ap, t_ap, f_ap):
    def fn(eng, o=out_ap, m=mask_ap, t=t_ap, f=f_ap):
        return eng.select(o, m, t, f)
    return fn


def s_copy(out_ap, in_ap):
    def fn(eng, o=out_ap, a=in_ap):
        return eng.copy(o, a)
    return fn


def s_act(out_ap, in_ap, func, bias=0.0, scale=1.0, accum=None):
    def fn(eng, o=out_ap, a=in_ap, f=func, b=bias, s=scale, acc=accum):
        return eng.activation(o, a, f, bias=b, scale=s, accum_out=acc)
    return fn


class Core:
    def __init__(self, name="prog"):
        self.nc = bass.Bass("TRN2", target_bir_lowering=False, debug=False)
        self.P = Prog(self.nc)
        self.stack = contextlib.ExitStack()
        self.stack_root = self.stack
        self.inputs = []

    def inp(self, name, shape, dtype=F32):
        self.inputs.append(name)
        return self.nc.dram_tensor(name, list(shape), dtype, kind="ExternalInput").ap()

    def out(self, name, shape, dtype=F32):
        return self.nc.dram_tensor(name, list(shape), dtype, kind="ExternalOutput").ap()

    def _nm(self, name):
        self._uid = getattr(self, "_uid", 0) + 1
        return f"{name}_u{self._uid}"

    def sb(self, name, shape, dtype=F32):
        return self.stack.enter_context(
            self.nc.sbuf_tensor(self._nm(name), list(shape), dtype))

    def sb_right(self, name, shape, dtype=F32):
        return self.stack_root.enter_context(
            self.nc.sbuf_tensor(self._nm(name), list(shape), dtype, side="right"))

    def psum(self, name, cols=512, dtype=F32):
        nbank = ceil_div(cols * (4 if dtype == F32 else 2), 2048)
        alloc_cols = nbank * (512 if dtype == F32 else 1024)
        t = self.stack.enter_context(
            self.nc.psum_tensor(self._nm(name), [128, alloc_cols], dtype))
        return t

    @contextlib.contextmanager
    def phase(self):
        st = contextlib.ExitStack()
        outer = self.stack
        self.stack = st
        try:
            yield
        finally:
            self.stack = outer
            st.close()
            self.P.barrier()

    def finish(self):
        self.P.emit(self.stack)
        self.stack.close()
        return self.nc


SCVALS = [1e-5, -0.5, 0.125, -0.125, 0.5, 1.0, -1.0, 2.0, -2.0, 8.0, -8.0,
          -30000.0, 0.03125, -0.03125, 1e-30, 1e-12]


def host_consts():
    c = {}
    c["scvals"] = np.tile(np.asarray(SCVALS, np.float32)[None, :], (128, 1))
    c["ones1"] = np.ones((1, 128), np.float32)
    c["negokm"] = np.full((128, 1), -1.0 / D, np.float32)
    c["onesokm"] = np.full((128, 1), 1.0 / D, np.float32)
    c["negokmb"] = np.full((128, 1), -1.0 / D, BF)
    c["onesokmb"] = np.full((128, 1), 1.0 / D, BF)
    c["identb"] = np.eye(128, dtype=BF)
    c["identf8"] = np.eye(8, dtype=np.float32)
    return c


_CONST_SPECS = {
    "scvals": ([128, len(SCVALS)], F32),
    "ones1": ([1, 128], F32),
    "negokm": ([128, 1], F32),
    "onesokm": ([128, 1], F32),
    "negokmb": ([128, 1], BF16),
    "onesokmb": ([128, 1], BF16),
    "identb": ([128, 128], BF16),
    "identf8": ([8, 8], F32),
}


def dev_consts(core: Core, extra=None):
    sb = {}
    specs = dict(_CONST_SPECS)
    if extra:
        specs.update(extra)
    for k, (shape, dt) in specs.items():
        d = core.inp(k, shape, dt)
        t = core.sb(f"c_{k}", shape, dt)
        sb[k] = t[:]
        core.P.dma("sync", sb[k], d, writes=[("c", k)], tag=f"c_{k}")
    for i, v in enumerate(SCVALS):
        core.nc.const_aps.register(F32, v, sb["scvals"][:, i:i + 1])
    core.P.barrier()
    return sb


class WeightLoader:

    def __init__(self, core: Core, stage_cols=5504, conv_engines=("vector", "scalar"),
                 consts=None, nbuf=2):
        self.c = core
        self.conv_engines = conv_engines
        self.consts = consts
        self._n = 0
        self.stage = [core.sb(f"wstage{i}", [128, stage_cols], F32) for i in range(nbuf)]

    def load(self, dram_ap, name, rows, cols, dma_engine="sync", dst=None):
        c = self.c
        assert rows % 128 == 0
        RC = rows // 128
        if dst is None:
            dst = c.sb(f"w_{name}", [128, RC, cols], BF16)
        for rc in range(RC):
            j = self._n % len(self.stage)
            st = self.stage[j]
            stkey = ("wstage", j)
            self._n += 1
            c.P.dma(dma_engine, st[:, 0:cols], dram_ap[rc * 128:(rc + 1) * 128, :],
                    reads=[stkey], writes=[stkey], tag=f"ldw_{name}_{rc}")
            eng = self.conv_engines[self._n % len(self.conv_engines)]
            mk = s_copy if eng == "scalar" else v_copy
            c.P.op(eng, mk(dst[:, rc, :], st[:, 0:cols]),
                   reads=[stkey], writes=[(name, rc)], tag=f"cvt_{name}_{rc}")
        return dst

    def load_bias_cols(self, dram_ap, name, n, dma_engine="sync"):
        c = self.c
        NCOL = ceil_div(n, 128)
        dst = c.sb(f"b_{name}", [128, NCOL], F32)
        c.P.dma(dma_engine, dst[:], dram_ap, writes=[(name, "bias")],
                tag=f"ldb_{name}")
        return dst


def emit_row_bcast(core, ps_ap, row_ap, consts, reads, writes, mult=False):
    core.P.op("tensor", t_matmul(ps_ap, consts["ones1"], row_ap, start=True, stop=True),
              reads=list(reads) + [("c", "ones1")], writes=list(writes), tag="bcast")


class LNState:
    def __init__(self, negmu, rstd, mukey, rkey, var=None, varkey=None):
        self.negmu = negmu
        self.rstd = rstd
        self.mukey = mukey
        self.rkey = rkey
        self.var = var
        self.varkey = varkey


def emit_ln_stats(core, x_ap_fn, xkeys, T, consts, name, xdt=BF16, tg=512,
                  root=False):
    P = core.P
    rsb = core.sb_right if root else core.sb
    negmu = rsb(f"ln_negmu_{name}", [1, T])
    msq = core.sb(f"ln_msq_{name}", [1, T], F32)
    sq = [core.sb(f"ln_sq_{name}{i}", [128, tg], xdt) for i in range(2)]
    ps = core.psum(f"ln_ps_{name}")
    pskey = f"ln_ps_{name}"
    negokm = consts["negokm"] if xdt == F32 else consts["negokmb"]
    onesokm = consts["onesokm"] if xdt == F32 else consts["onesokmb"]
    for g in range(ceil_div(T, tg)):
        t0, t1 = g * tg, min((g + 1) * tg, T)
        n = t1 - t0
        for dc in range(DC):
            P.op("tensor", t_matmul(ps[0:1, 0:n], negokm, x_ap_fn(dc, t0, t1),
                                    start=dc == 0, stop=dc == DC - 1),
                 reads=[xkeys(dc), ("c", "negokm")], writes=[pskey], tag=f"ln_mu_{name}")
        for dc in range(DC):
            j = dc % 2
            P.op("scalar", s_act(sq[j][:, 0:n], x_ap_fn(dc, t0, t1), AF.Square),
                 reads=[xkeys(dc), (name, "sq", j)], writes=[(name, "sq", j)],
                 tag=f"ln_sq_{name}")
            P.op("tensor", t_matmul(ps[32:33, 0:n], onesokm, sq[j][:, 0:n],
                                    start=dc == 0, stop=dc == DC - 1),
                 reads=[(name, "sq", j), ("c", "onesokm")], writes=[pskey],
                 tag=f"ln_ms_{name}")
        P.op("vector", v_copy(negmu[0:1, t0:t1], ps[0:1, 0:n]),
             reads=[pskey], writes=[(name, "negmu")], tag="ln_ev")
        P.op("vector", v_copy(msq[0:1, t0:t1], ps[32:33, 0:n]),
             reads=[pskey], writes=[(name, "msq")], tag="ln_ev2")
    var = rsb(f"ln_var_{name}", [1, T])
    P.op("vector", v_tt(var[:], negmu[:], negmu[:], ALU.mult),
         reads=[(name, "negmu")], writes=[(name, "var")], tag="ln_mu2")
    P.op("vector", v_tt(var[:], msq[:], var[:], ALU.subtract),
         reads=[(name, "msq"), (name, "var")], writes=[(name, "var")], tag="ln_var")
    lnv = core.sb(f"ln_lnv_{name}", [1, T], F32)
    P.op("scalar", s_act(lnv[:], var[:], AF.Ln, bias=1e-5),
         reads=[(name, "var")], writes=[(name, "lnv")], tag="ln_ln")
    rstd = rsb(f"ln_rstd_{name}", [1, T])
    P.op("scalar", s_act(rstd[:], lnv[:], AF.Exp, scale=-0.5),
         reads=[(name, "lnv")], writes=[(name, "rstd")], tag="ln_exp")
    return LNState(negmu[:], rstd[:], (name, "negmu"), (name, "rstd"),
                   var=var[:], varkey=(name, "var"))


def emit_ln_compose(core, lnst, name, T, root=False):
    P = core.P
    rsb = core.sb_right if root else core.sb
    vz = core.sb(f"lnc_vz_{name}", [1, T], F32)
    P.op("vector", v_tt(vz[:], lnst.rstd, lnst.rstd, ALU.mult),
         reads=[lnst.rkey], writes=[(name, "vz")], tag="lnc_r2")
    P.op("vector", v_tt(vz[:], vz[:], lnst.var, ALU.mult),
         reads=[(name, "vz"), lnst.varkey], writes=[(name, "vz")], tag="lnc_vz")
    P.op("scalar", s_act(vz[:], vz[:], AF.Ln, bias=1e-5),
         reads=[(name, "vz")], writes=[(name, "vz")], tag="lnc_ln")
    P.op("scalar", s_act(vz[:], vz[:], AF.Exp, scale=-0.5),
         reads=[(name, "vz")], writes=[(name, "vz")], tag="lnc_exp")
    R = rsb(f"lnc_R_{name}", [1, T])
    P.op("vector", v_tt(R[:], vz[:], lnst.rstd, ALU.mult),
         reads=[(name, "vz"), lnst.rkey], writes=[(name, "R")], tag="lnc_R")
    return LNState(lnst.negmu, R[:], lnst.mukey, (name, "R"))


class RowsDram:

    def __init__(self, core, name, T, nrows=3):
        self.core = core
        self.name = name
        self.T = T
        self.d = core.nc.dram_tensor(f"rows_{name}", [nrows, T], F32,
                                     kind="Internal").ap()
        self.tmp = core.sb_right(f"rowtmp_{name}", [1, 512], F32)
        self._key = (name, "rowtmp")

    def bcast(self, ps_ap, pskey, idx, t0, t1, consts):
        c = self.core
        n = t1 - t0
        c.P.dma("sync", self.tmp[0:1, 0:n], self.d[idx:idx + 1, t0:t1],
                reads=[self._key, (self.name, "row", idx)], writes=[self._key],
                tag=f"rowld_{self.name}")
        emit_row_bcast(c, ps_ap[:, 0:n], self.tmp[0:1, 0:n], consts,
                       reads=[self._key], writes=[pskey])


def emit_ln_stats_dram(core, x_ap_fn, xkeys, T, consts, name, xdt=BF16, tg=512,
                       compose=True):
    P = core.P
    rows = RowsDram(core, name, T, nrows=3 if compose else 2)
    ra = core.sb(f"ln_ra_{name}", [1, tg], F32)
    rb = core.sb(f"ln_rb_{name}", [1, tg], F32)
    rc = core.sb(f"ln_rc_{name}", [1, tg], F32)
    sq = [core.sb(f"ln_sq_{name}{i}", [128, tg], xdt) for i in range(2)]
    ps = core.psum(f"ln_ps_{name}")
    pskey = f"ln_ps_{name}"
    negokm = consts["negokm"] if xdt == F32 else consts["negokmb"]
    onesokm = consts["onesokm"] if xdt == F32 else consts["onesokmb"]
    ka, kb, kc2 = (name, "ra"), (name, "rb"), (name, "rc")
    for g in range(ceil_div(T, tg)):
        t0, t1 = g * tg, min((g + 1) * tg, T)
        n = t1 - t0
        for dc in range(DC):
            P.op("tensor", t_matmul(ps[0:1, 0:n], negokm, x_ap_fn(dc, t0, t1),
                                    start=dc == 0, stop=dc == DC - 1),
                 reads=[xkeys(dc), ("c", "negokm")], writes=[pskey], tag=f"ln_mu_{name}")
        for dc in range(DC):
            j = dc % 2
            P.op("scalar", s_act(sq[j][:, 0:n], x_ap_fn(dc, t0, t1), AF.Square),
                 reads=[xkeys(dc), (name, "sq", j)], writes=[(name, "sq", j)],
                 tag=f"ln_sq_{name}")
            P.op("tensor", t_matmul(ps[32:33, 0:n], onesokm, sq[j][:, 0:n],
                                    start=dc == 0, stop=dc == DC - 1),
                 reads=[(name, "sq", j), ("c", "onesokm")], writes=[pskey],
                 tag=f"ln_ms_{name}")
        P.op("vector", v_copy(ra[0:1, 0:n], ps[0:1, 0:n]),
             reads=[pskey, ka], writes=[ka], tag="ln_ev")
        P.op("vector", v_copy(rb[0:1, 0:n], ps[32:33, 0:n]),
             reads=[pskey, kb], writes=[kb], tag="ln_ev2")
        P.dma("gpsimd", rows.d[0:1, t0:t1], ra[0:1, 0:n],
              reads=[ka], writes=[(name, "row", 0)], tag="ln_st0")
        P.op("vector", v_tt(rc[0:1, 0:n], ra[0:1, 0:n], ra[0:1, 0:n], ALU.mult),
             reads=[ka, kc2], writes=[kc2], tag="ln_mu2")
        P.op("vector", v_tt(rb[0:1, 0:n], rb[0:1, 0:n], rc[0:1, 0:n], ALU.subtract),
             reads=[kb, kc2], writes=[kb], tag="ln_var")
        P.op("scalar", s_act(rc[0:1, 0:n], rb[0:1, 0:n], AF.Ln, bias=1e-5),
             reads=[kb, kc2], writes=[kc2], tag="ln_ln")
        P.op("scalar", s_act(rc[0:1, 0:n], rc[0:1, 0:n], AF.Exp, scale=-0.5),
             reads=[kc2], writes=[kc2], tag="ln_exp")
        P.dma("gpsimd", rows.d[1:2, t0:t1], rc[0:1, 0:n],
              reads=[kc2], writes=[(name, "row", 1)], tag="ln_st1")
        if compose:
            P.op("vector", v_tt(rb[0:1, 0:n], rb[0:1, 0:n], rc[0:1, 0:n], ALU.mult),
                 reads=[kb, kc2], writes=[kb], tag="lnc_1")
            P.op("vector", v_tt(rb[0:1, 0:n], rb[0:1, 0:n], rc[0:1, 0:n], ALU.mult),
                 reads=[kb, kc2], writes=[kb], tag="lnc_2")
            P.op("scalar", s_act(rb[0:1, 0:n], rb[0:1, 0:n], AF.Ln, bias=1e-5),
                 reads=[kb], writes=[kb], tag="lnc_3")
            P.op("scalar", s_act(rb[0:1, 0:n], rb[0:1, 0:n], AF.Exp, scale=-0.5),
                 reads=[kb], writes=[kb], tag="lnc_4")
            P.op("vector", v_tt(rb[0:1, 0:n], rb[0:1, 0:n], rc[0:1, 0:n], ALU.mult),
                 reads=[kb, kc2], writes=[kb], tag="lnc_5")
            P.dma("gpsimd", rows.d[2:3, t0:t1], rb[0:1, 0:n],
                  reads=[kb], writes=[(name, "row", 2)], tag="ln_st2")
    return rows


def emit_ln_apply_dram(core, x_ap_fn, xkeys, rows, muidx, ridx, consts,
                       out_ap_fn, outkey_fn, t0, t1, psA, psAkey, psB, psBkey,
                       gcol_fn=None, bcol_fn=None, gbkeys=()):
    P = core.P
    n = t1 - t0
    rows.bcast(psA, psAkey, muidx, t0, t1, consts)
    rows.bcast(psB, psBkey, ridx, t0, t1, consts)
    for dc in range(DC):
        dst = out_ap_fn(dc)
        ok = outkey_fn(dc)
        P.op("vector", v_tt(dst, x_ap_fn(dc, t0, t1), psA[:, 0:n], ALU.add),
             reads=[xkeys(dc), psAkey], writes=[ok], tag="ln_sub")
        P.op("vector", v_tt(dst, dst, psB[:, 0:n], ALU.mult),
             reads=[ok, psBkey], writes=[ok], tag="ln_mul")
        if gcol_fn is not None:
            P.op("vector", v_ts(dst, dst, gcol_fn(dc), bcol_fn(dc),
                                op0=ALU.mult, op1=ALU.add),
                 reads=[ok] + list(gbkeys), writes=[ok], tag="ln_gb")


def emit_ln_apply_group(core, x_ap_fn, xkeys, lnst, consts, out_ap_fn, outkey_fn,
                        t0, t1, psA, psAkey, psB, psBkey,
                        gcol_fn=None, bcol_fn=None, gbkeys=()):
    P = core.P
    n = t1 - t0
    emit_row_bcast(core, psA[:, 0:n], lnst.negmu[0:1, t0:t1], consts,
                   reads=[lnst.mukey], writes=[psAkey])
    emit_row_bcast(core, psB[:, 0:n], lnst.rstd[0:1, t0:t1], consts,
                   reads=[lnst.rkey], writes=[psBkey])
    for dc in range(DC):
        dst = out_ap_fn(dc)
        ok = outkey_fn(dc)
        P.op("vector", v_tt(dst, x_ap_fn(dc, t0, t1), psA[:, 0:n], ALU.add),
             reads=[xkeys(dc), psAkey], writes=[ok], tag="ln_sub")
        P.op("vector", v_tt(dst, dst, psB[:, 0:n], ALU.mult),
             reads=[ok, psBkey], writes=[ok], tag="ln_mul")
        if gcol_fn is not None:
            P.op("vector", v_ts(dst, dst, gcol_fn(dc), bcol_fn(dc),
                                op0=ALU.mult, op1=ALU.add),
                 reads=[ok] + list(gbkeys), writes=[ok], tag="ln_gb")


L0A_HALO = 512
L0A_TEXT = TPC + 2 * L0A_HALO

def l0a_slot_qtiles(k):
    i_off = -256 + 256 * k
    out = []
    for j in range(4):
        q = i_off - 128 + 128 * j
        if 0 <= q < TPC:
            out.append((q, j))
    return out


def build_l0attn(dbg=False):
    core = Core("l0attn")
    P = core.P
    nc = core.nc

    x0f = core.inp("x0f", [D, L0A_TEXT], F32)
    wq_d = core.inp("wq", [D, D], F32)
    wk_d = core.inp("wk", [D, D], F32)
    wv_d = core.inp("wv", [D, D], F32)
    wproj_d = core.inp("wproj", [D, D], F32)
    w1_d = core.inp("w1", [D, 4 * D], F32)
    w2_d = core.inp("w2", [4 * D, D], F32)
    bq_d = core.inp("bq", [128, DC], F32)
    bk_d = core.inp("bk", [128, DC], F32)
    bv_d = core.inp("bv", [1, D], F32)
    bproj_d = core.inp("bproj", [128, DC], F32)
    b1_d = core.inp("b1", [128, 32], F32)
    b2_d = core.inp("b2", [128, DC], F32)
    ln1g_d = core.inp("ln1g", [128, DC], F32)
    ln1b_d = core.inp("ln1b", [128, DC], F32)
    ln2g_d = core.inp("ln2g", [128, DC], F32)
    ln2b_d = core.inp("ln2b", [128, DC], F32)
    tents_d = core.inp("tents", [128, 40], F32)
    wmask_d = core.inp("wmask", [10, 512], F32)

    x1f = core.out("x1f", [128, DC, TPC], F32)
    if dbg:
        dbg_ctx = core.out("dbg_ctx", [128, DC, TPC], BF16)
        dbg_out = core.out("dbg_outf", [128, DC, TPC], F32)
        dbg_qf = core.out("dbg_qf", [128, DC, 512], BF16)
        dbg_kf = core.out("dbg_kf", [128, DC, 512], BF16)
        dbg_p = core.out("dbg_p", [128, 512], BF16)
        dbg_xn = core.out("dbg_xn", [128, DC, 512], BF16)
        dbg_rows = core.out("dbg_rows", [3, L0A_TEXT], F32)

    consts = dev_consts(core)
    tents = core.sb("tents_sb", [128, 40], F32)
    P.dma("sync", tents[:], tents_d, writes=[("tents",)], tag="ld_tents")
    wmtmp = core.sb("wmtmp", [1, 512], F32)

    ctxb = core.sb("ctxb", [128, DC, TPC], BF16)
    with core.phase():
        wl = WeightLoader(core, stage_cols=1024, consts=consts)
        wq = wl.load(wq_d, "wq", D, D)
        wk = wl.load(wk_d, "wk", D, D)
        wv = wl.load(wv_d, "wv", D, D)
        bqc = wl.load_bias_cols(bq_d, "bq", D)
        bkc = wl.load_bias_cols(bk_d, "bk", D)
        ln1gc = wl.load_bias_cols(ln1g_d, "ln1g", D)
        ln1bc = wl.load_bias_cols(ln1b_d, "ln1b", D)
        bvrow = core.sb("bvrow", [1, D], F32)
        P.dma("sync", bvrow[:], bv_d, writes=[("bvrow",)], tag="ld_bv")

        xb = core.sb("xb", [128, DC, L0A_TEXT], BF16)
        for dc in range(DC):
            for cc in range(3):
                j = (dc * 3 + cc) % 2
                st = wl.stage[j]
                P.dma("sync" if cc % 2 == 0 else "scalar", st[:, 0:1024],
                      x0f[dc * 128:(dc + 1) * 128, cc * 1024:(cc + 1) * 1024],
                      reads=[("wstage", j)], writes=[("wstage", j)], tag="ldx")
                P.op("vector", v_copy(xb[:, dc, cc * 1024:(cc + 1) * 1024],
                                      st[:, 0:1024]),
                     reads=[("wstage", j)], writes=[("xb", dc)], tag="cvt_x")

        with core.phase():
            rows1 = emit_ln_stats_dram(core, lambda d, a, b: xb[:, d, a:b],
                                       lambda d: ("xb", d), L0A_TEXT, consts, "ln1",
                                       xdt=BF16, compose=True)

        bvb = core.sb("bvb", [128, D], F32)
        psA = core.psum("psA")
        psB = core.psum("psB")
        for half in range(2):
            emit_row_bcast(core, psA[:, 0:512], bvrow[0:1, half * 512:(half + 1) * 512],
                           consts, reads=[("bvrow",)], writes=[("psA",)])
            P.op("vector", v_copy(bvb[:, half * 512:(half + 1) * 512], psA[:, 0:512]),
                 reads=[("psA",)], writes=[("bvb",)], tag="bvb")

        for dc in range(DC):
            P.op("vector", v_memset(ctxb[:, dc, :], 0.0),
                 writes=[("ctx", dc)], tag="ctx0")

        xnk = [core.sb(f"xnk{j}", [128, DC, 512], BF16) for j in range(2)]
        kfb = core.sb("kfb", [128, DC, 512], BF16)
        vtb = core.sb("vtb", [128, 4, D], BF16)
        qfb = core.sb("qfb", [128, DC, 512], BF16)
        pb = core.sb("pb", [128, 512], BF16)
        ptb = core.sb("ptb", [128, 2, 4, 512], BF16)
        mcol = core.sb("mcol", [128, 4], F32)
        psS = [core.psum(f"psS{j}") for j in range(2)]
        psK = core.psum("psK")
        psV = core.psum("psV", cols=1024)
        psO = core.psum("psO")

        for k in range(10):
            ke0 = 256 * k + 128
            qts = l0a_slot_qtiles(k)
            q0 = qts[0][0] + L0A_HALO
            qn = len(qts)
            wj = k % 2
            emit_ln_apply_dram(core, lambda d, a, b: xb[:, d, a:b], lambda d: ("xb", d),
                               rows1, 0, 2, consts,
                               lambda d, w=wj: xnk[w][:, d, 0:512],
                               lambda d, w=wj: ("xnk", w, d),
                               ke0, ke0 + 512, psA, ("psA",), psB, ("psB",),
                               gcol_fn=lambda d: ln1gc[:, d:d + 1],
                               bcol_fn=lambda d: ln1bc[:, d:d + 1],
                               gbkeys=[("ln1g", "bias"), ("ln1b", "bias")])
            P.dma("sync", wmtmp[0:1, :], wmask_d[k:k + 1, :],
                  reads=[("wm",)], writes=[("wm",)], tag="ld_wm")
            qrel = q0 - ke0
            for (w, x0_, dstb, dstkey, bc, bckey, wname, width) in (
                    (wq, qrel, qfb, ("qfb",), bqc, ("bq", "bias"), "wq", qn * 128),
                    (wk, 0, kfb, ("kfb",), bkc, ("bk", "bias"), "wk", 512)):
                for oc in range(DC):
                    for ki in range(DC):
                        P.op("tensor", t_matmul(psK[:, 0:width], w[:, ki, oc * 128:(oc + 1) * 128],
                                                xnk[wj][:, ki, x0_:x0_ + width],
                                                start=ki == 0, stop=ki == DC - 1),
                             reads=[(wname, ki), ("xnk", wj, ki)], writes=[("psK",)], tag="qk_mm")
                    P.op("vector", v_ts(dstb[:, oc, 0:width], psK[:, 0:width],
                                        bc[:, oc:oc + 1], None, op0=ALU.add),
                         reads=[("psK",), bckey], writes=[dstkey + (oc,)], tag="qk_ev")
            for tt in range(4):
                for half in range(2):
                    for ki in range(DC):
                        P.op("tensor", t_matmul(psV[:, half * 512:(half + 1) * 512],
                                                xnk[wj][:, ki, tt * 128:(tt + 1) * 128],
                                                wv[:, ki, half * 512:(half + 1) * 512],
                                                start=ki == 0, stop=ki == DC - 1),
                             reads=[("wv", ki), ("xnk", wj, ki)], writes=[("psV",)], tag="v_mm")
                for half in range(2):
                    P.op("vector", v_tt(vtb[:, tt, half * 512:(half + 1) * 512],
                                        psV[:, half * 512:(half + 1) * 512],
                                        bvb[:, half * 512:(half + 1) * 512], ALU.add),
                         reads=[("psV",), ("bvb",)], writes=[("vtb", tt)], tag="v_ev")
            for h in range(16):
                hp, hh = h // 2, (h % 2) * 64
                pt_j = h % 2
                for qi, (qoff, tcls) in enumerate(qts):
                    ps = psS[(h * 4 + qi) % 2]
                    pskey = ("psS", (h * 4 + qi) % 2)
                    P.op("tensor", t_matmul(ps[:, 0:512], consts["ones1"],
                                            wmtmp[0:1, :], start=True, stop=False),
                         reads=[("wm",), ("c", "ones1")], writes=[pskey], tag="s_mask")
                    P.op("tensor", t_matmul(ps[:, 0:512],
                                            qfb[hh:hh + 64, hp, qi * 128:(qi + 1) * 128],
                                            kfb[hh:hh + 64, hp, 0:512],
                                            start=False, stop=True),
                         reads=[("qfb", hp), ("kfb", hp)], writes=[pskey], tag="s_mm")
                    P.op("vector", v_red(mcol[:, 0:1], ps[:, 0:512], "max"),
                         reads=[pskey], writes=[("mcol",)], tag="s_max")
                    P.op("vector", v_ts(mcol[:, 1:2], mcol[:, 0:1], -0.125, None, op0=ALU.mult),
                         reads=[("mcol",)], writes=[("mcol",)], tag="s_neg")
                    P.op("scalar", s_act(pb[:, 0:512], ps[:, 0:512], AF.Exp,
                                         bias=mcol[:, 1:2], scale=0.125,
                                         accum=mcol[:, 2:3]),
                         reads=[pskey, ("mcol",)], writes=[("pb",), ("mcol",)], tag="s_exp")
                    P.op("vector", v_recip(mcol[:, 3:4], mcol[:, 2:3]),
                         reads=[("mcol",)], writes=[("mcol",)], tag="s_rec")
                    P.op("vector", v_tt(mcol[:, 3:4], mcol[:, 3:4],
                                        tents[:, k * 4 + tcls:k * 4 + tcls + 1], ALU.mult),
                         reads=[("mcol",), ("tents",)], writes=[("mcol",)], tag="s_tent")
                    P.op("vector", v_ts(pb[:, 0:512], pb[:, 0:512], mcol[:, 3:4], None,
                                        op0=ALU.mult),
                         reads=[("pb",), ("mcol",)], writes=[("pb",)], tag="s_scl")
                    for kc in range(4):
                        P.dma_tr("scalar" if kc % 2 else "sync",
                                 ptb[:, pt_j, kc, qi * 128:(qi + 1) * 128],
                                 pb[:, kc * 128:(kc + 1) * 128],
                                 reads=[("pb",)], writes=[("ptb", pt_j, kc)], tag="s_tr")
                for kc in range(4):
                    P.op("tensor", t_matmul(psO[hh:hh + 64, 0:qn * 128],
                                            vtb[:, kc, 64 * h:64 * h + 64],
                                            ptb[:, pt_j, kc, 0:qn * 128],
                                            start=kc == 0, stop=kc == 3,
                                            tile_position=(0, hh)),
                         reads=[("vtb", kc), ("ptb", pt_j, kc)], writes=[("psO",)], tag="pv_mm")
                if h % 2 == 1:
                    for qi, (qoff, tcls) in enumerate(qts):
                        P.op("vector", v_tt(ctxb[:, hp, qoff:qoff + 128],
                                            ctxb[:, hp, qoff:qoff + 128],
                                            psO[:, qi * 128:(qi + 1) * 128], ALU.add),
                             reads=[("ctx", hp), ("psO",)], writes=[("ctx", hp)], tag="pv_ev")

        if dbg:
            for dc in range(DC):
                P.dma("gpsimd", dbg_ctx[:, dc, :], ctxb[:, dc, :],
                      reads=[("ctx", dc)], writes=[("dbgc", dc)], tag="dbg")
            for dc in range(DC):
                P.dma("gpsimd", dbg_qf[:, dc, :], qfb[:, dc, :],
                      reads=[("qfb", dc)], writes=[("dbgq", dc)], tag="dbg")
                P.dma("gpsimd", dbg_kf[:, dc, :], kfb[:, dc, :],
                      reads=[("kfb", dc)], writes=[("dbgk", dc)], tag="dbg")
                P.dma("gpsimd", dbg_xn[:, dc, :], xnk[1][:, dc, :],
                      reads=[("xnk", 1, dc)], writes=[("dbgxn", dc)], tag="dbg")
            P.dma("gpsimd", dbg_p[:], pb[:], reads=[("pb",)], writes=[("dbgp",)], tag="dbg")
            P.dma("gpsimd", dbg_rows, rows1.d[:],
                  reads=[("ln1", "row", 0), ("ln1", "row", 1), ("ln1", "row", 2)],
                  writes=[("dbgr",)], tag="dbg")

    outf = core.sb("outf", [128, DC, TPC], F32)
    with core.phase():
        wl = WeightLoader(core, stage_cols=1024, consts=consts)
        wproj = wl.load(wproj_d, "wproj", D, D)
        bprojc = wl.load_bias_cols(bproj_d, "bproj", D)
        rst = [core.sb(f"rst{j}", [128, 512], F32) for j in range(2)]
        ost = [core.sb(f"ost{j}", [128, 512], F32) for j in range(2)]
        psP = [core.psum(f"psP{j}") for j in range(2)]
        psA = core.psum("psAr")
        psB = core.psum("psBr")
        for tg in range(4):
            e0 = L0A_HALO + tg * 512
            rows1.bcast(psA, ("psAr",), 0, e0, e0 + 512, consts)
            rows1.bcast(psB, ("psBr",), 1, e0, e0 + 512, consts)
            for oc in range(DC):
                j = (oc + tg) % 2
                P.dma("scalar", rst[j][:], x0f[oc * 128:(oc + 1) * 128, e0:e0 + 512],
                      reads=[("rst", j)], writes=[("rst", j)], tag="ld_res")
                P.op("vector", v_tt(rst[j][:], rst[j][:], psA[:, 0:512], ALU.add),
                     reads=[("rst", j), ("psAr",)], writes=[("rst", j)],
                     tag="z_sub")
                P.op("vector", v_tt(rst[j][:], rst[j][:], psB[:, 0:512], ALU.mult),
                     reads=[("rst", j), ("psBr",)], writes=[("rst", j)], tag="z_mul")
                for ki in range(DC):
                    P.op("tensor", t_matmul(psP[j][:, 0:512], wproj[:, ki, oc * 128:(oc + 1) * 128],
                                            ctxb[:, ki, tg * 512:(tg + 1) * 512],
                                            start=ki == 0, stop=ki == DC - 1),
                         reads=[("wproj", ki), ("ctx", ki)], writes=[("psP", j)], tag="pr_mm")
                P.op("scalar", s_act(ost[j][:], psP[j][:, 0:512], AF.Gelu_apprx_sigmoid
                                     if False else AF.Copy),
                     reads=[("psP", j), ("ost", j)],
                     writes=[("ost", j)], tag="pr_cp")
                P.op("vector", v_ts(ost[j][:], ost[j][:],
                                    bprojc[:, oc:oc + 1], None, op0=ALU.add),
                     reads=[("ost", j), ("bproj", "bias")],
                     writes=[("ost", j)], tag="pr_b")
                P.op("vector", v_tt(outf[:, oc, tg * 512:(tg + 1) * 512], ost[j][:],
                                    rst[j][:], ALU.add),
                     reads=[("ost", j), ("rst", j)], writes=[("outf", oc, tg)], tag="pr_res")

    if dbg:
        for dc in range(DC):
            core.P.dma("gpsimd", dbg_out[:, dc, :], outf[:, dc, :],
                       reads=[("outf", dc, 0), ("outf", dc, 1), ("outf", dc, 2),
                              ("outf", dc, 3)], writes=[("dbgo", dc)], tag="dbg")
    with core.phase():
        wl = WeightLoader(core, stage_cols=1024, consts=consts)
        b1c = wl.load_bias_cols(b1_d, "b1", 4 * D)
        b2c = wl.load_bias_cols(b2_d, "b2", D)
        ln2gc = wl.load_bias_cols(ln2g_d, "ln2g", D)
        ln2bc = wl.load_bias_cols(ln2b_d, "ln2b", D)
        outb = core.sb("outb", [128, DC, TPC], BF16)
        for dc in range(DC):
            P.op("vector", v_copy(outb[:, dc, :], outf[:, dc, :]),
                 reads=[("outf", dc, 0), ("outf", dc, 1), ("outf", dc, 2), ("outf", dc, 3)],
                 writes=[("outb", dc)], tag="cp_outb")
        with core.phase():
            rows2 = emit_ln_stats_dram(core, lambda d, a, b: outb[:, d, a:b],
                                       lambda d: ("outb", d), TPC, consts, "ln2",
                                       xdt=BF16, compose=False)
        psA = core.psum("psA2")
        psB = core.psum("psB2")
        for tg in range(4):
            emit_ln_apply_dram(core, lambda d, a, b: outb[:, d, a:b],
                               lambda d: ("outb", d), rows2, 0, 1, consts,
                               lambda d, tg=tg: outb[:, d, tg * 512:(tg + 1) * 512],
                               lambda d: ("outb", d),
                               tg * 512, (tg + 1) * 512, psA, ("psA2",), psB, ("psB2",),
                               gcol_fn=lambda d: ln2gc[:, d:d + 1],
                               bcol_fn=lambda d: ln2bc[:, d:d + 1],
                               gbkeys=[("ln2g", "bias"), ("ln2b", "bias")])
        h2 = core.sb("h2", [128, 32, 512], BF16)
        w1sl = [core.sb(f"w1sl{j}", [128, DC, 128], BF16) for j in range(2)]
        w2sl = [core.sb(f"w2sl{j}", [128, 32, 128], BF16) for j in range(2)]
        ost = [core.sb(f"ost{j}", [128, 512], F32) for j in range(2)]
        xst = [core.sb(f"xst{j}", [128, 512], F32) for j in range(2)]
        psM = [core.psum(f"psM{j}") for j in range(2)]
        for tg in range(4):
            for oc in range(32):
                j = oc % 2
                st = wl.stage[j]
                for rc in range(DC):
                    P.dma("sync" if rc % 2 == 0 else "scalar",
                          st[:, rc * 128:(rc + 1) * 128],
                          w1_d[rc * 128:(rc + 1) * 128, oc * 128:(oc + 1) * 128],
                          reads=[("wstage", j)], writes=[("wstage", j)], tag="ld_w1")
                P.op("scalar", s_copy(w1sl[j][:].rearrange("p c k -> p (c k)"), st[:, 0:1024]),
                     reads=[("wstage", j)], writes=[("w1sl", j)], tag="cvt_w1")
                for ki in range(DC):
                    P.op("tensor", t_matmul(psM[j][:, 0:512], w1sl[j][:, ki, :],
                                            outb[:, ki, tg * 512:(tg + 1) * 512],
                                            start=ki == 0, stop=ki == DC - 1),
                         reads=[("w1sl", j), ("outb", ki)], writes=[("psM", j)], tag="m1_mm")
                P.op("scalar", s_act(h2[:, oc, :], psM[j][:, 0:512], AF.Gelu,
                                     bias=b1c[:, oc:oc + 1]),
                     reads=[("psM", j), ("b1", "bias")], writes=[("h2", oc)], tag="m1_gelu")
            for do in range(DC):
                j = do % 2
                st2 = wl.stage[j]
                for rc in range(4):
                    P.dma("sync" if rc % 2 == 0 else "scalar",
                          st2[:, 0:1024].rearrange("p (a b) -> p a b", a=8)[:, 0:8, 0:128],
                          w2_d[rc * 1024:(rc + 1) * 1024, do * 128:(do + 1) * 128]
                          .rearrange("(a p) b -> p a b", p=128),
                          reads=[("wstage", j), ("w2sl", j)], writes=[("wstage", j)],
                          tag="ld_w2")
                    P.op("scalar", s_copy(w2sl[j][:, rc * 8:(rc + 1) * 8, :],
                                          st2[:, 0:1024].rearrange("p (a b) -> p a b", a=8)),
                         reads=[("wstage", j)], writes=[("w2sl", j)], tag="cvt_w2")
                for hg in range(32):
                    P.op("tensor", t_matmul(psM[j][:, 0:512], w2sl[j][:, hg, :],
                                            h2[:, hg, :], start=hg == 0, stop=hg == 31),
                         reads=[("w2sl", j), ("h2", hg)], writes=[("psM", j)], tag="m2_mm")
                P.op("vector", v_ts(ost[j][:], psM[j][:, 0:512],
                                    b2c[:, do:do + 1], None, op0=ALU.add),
                     reads=[("psM", j), ("b2", "bias"), ("ost", j)],
                     writes=[("ost", j)], tag="m2_b")
                P.op("vector", v_tt(outf[:, do, tg * 512:(tg + 1) * 512], ost[j][:],
                                    outf[:, do, tg * 512:(tg + 1) * 512], ALU.add),
                     reads=[("ost", j), ("outf", do, tg)],
                     writes=[("outf", do, tg)], tag="m2_res")

    with core.phase():
        outb = core.sb("outbC", [128, DC, TPC], BF16)
        for dc in range(DC):
            P.op("vector", v_copy(outb[:, dc, :], outf[:, dc, :]),
                 reads=[("outf", dc, 0), ("outf", dc, 1), ("outf", dc, 2),
                        ("outf", dc, 3)],
                 writes=[("outbC", dc)], tag="cp_outC")
        lnst3 = emit_ln_stats(core, lambda d, a, b: outb[:, d, a:b],
                              lambda d: ("outbC", d), TPC, consts, "ln3", xdt=BF16)
        psA = core.psum("psAc")
        psB = core.psum("psBc")
        rst = [core.sb(f"rstC{j}", [128, 512], F32) for j in range(2)]
        xst = [core.sb(f"xstC{j}", [128, 512], F32) for j in range(2)]
        for tg in range(4):
            emit_row_bcast(core, psA[:, 0:512], lnst3.negmu[0:1, tg * 512:(tg + 1) * 512],
                           consts, reads=[lnst3.mukey], writes=[("psAc",)])
            emit_row_bcast(core, psB[:, 0:512], lnst3.rstd[0:1, tg * 512:(tg + 1) * 512],
                           consts, reads=[lnst3.rkey], writes=[("psBc",)])
            for oc in range(DC):
                j = (oc * 4 + tg) % 2
                P.dma("sync", rst[j][:],
                      x0f[oc * 128:(oc + 1) * 128, L0A_HALO + tg * 512:L0A_HALO + (tg + 1) * 512],
                      reads=[("rstC", j)], writes=[("rstC", j)], tag="ld_x0")
                P.op("vector", v_tt(xst[j][:], outf[:, oc, tg * 512:(tg + 1) * 512],
                                    psA[:, 0:512], ALU.add),
                     reads=[("outf", oc, tg), ("psAc",), ("xstC", j)],
                     writes=[("xstC", j)], tag="c_sub")
                P.op("vector", v_tt(xst[j][:], xst[j][:], psB[:, 0:512], ALU.mult),
                     reads=[("xstC", j), ("psBc",)], writes=[("xstC", j)], tag="c_mul")
                P.op("vector", v_tt(xst[j][:], xst[j][:], rst[j][:], ALU.add),
                     reads=[("xstC", j), ("rstC", j)], writes=[("xstC", j)], tag="c_res")
                P.dma("gpsimd", x1f[:, oc, tg * 512:(tg + 1) * 512], xst[j][:],
                      reads=[("xstC", j)], writes=[("x1out", oc, tg)], tag="st_x1")
    return core


def l0a_host_inputs(inputs, consts_h):
    x0 = np.asarray(inputs["L0_attn_hidden_states"]).reshape(S, D)
    maps = []
    for c in range(NCORES):
        lo = 2048 * c - L0A_HALO
        xe = np.zeros((L0A_TEXT, D), np.float32)
        a, b = max(0, lo), min(S, lo + L0A_TEXT)
        xe[a - lo:b - lo] = x0[a:b]
        m = dict(consts_h)
        m["x0f"] = np.ascontiguousarray(xe.T)
        for nm in ("wq", "wk", "wv", "wproj", "w1", "w2"):
            m[nm] = np.asarray(inputs[f"L0_attn_{nm}"])
        def cols(v):
            return np.ascontiguousarray(np.asarray(v).reshape(-1, 128).T)
        for nm in ("bq", "bk", "bproj", "b1", "b2"):
            m[nm] = cols(inputs[f"L0_attn_{nm}"])
        m["bv"] = np.asarray(inputs["L0_attn_bv"]).reshape(1, -1)
        m["ln1g"] = cols(inputs["L0_attn_ln1_g"])
        m["ln1b"] = cols(inputs["L0_attn_ln1_b"])
        m["ln2g"] = cols(inputs["L0_attn_ln2_g"])
        m["ln2b"] = cols(inputs["L0_attn_ln2_b"])
        r = np.arange(128, dtype=np.float32)
        up_lo = r / 255.0
        up_hi = (r + 128.0) / 255.0
        dn_hi = 1.0 - r / 255.0
        dn_lo = 1.0 - (r + 128.0) / 255.0
        tents = np.zeros((128, 40), np.float32)
        for k in range(10):
            i_global = 2048 * c - 256 + 256 * k
            cls = [up_lo, up_hi, dn_hi, dn_lo]
            if i_global < 0 or i_global > S - 256:
                cls = [np.zeros(128, np.float32)] * 4
            else:
                if i_global == 0:
                    cls = [cls[0], np.ones(128, np.float32), cls[2], cls[3]]
                if i_global == S - 256:
                    cls = [cls[0], cls[1], np.ones(128, np.float32), cls[3]]
            for t in range(4):
                tents[:, k * 4 + t] = cls[t]
        m["tents"] = tents
        wm = np.zeros((10, 512), np.float32)
        for k in range(10):
            i_global = 2048 * c - 256 + 256 * k
            ks = i_global - 128
            for col in range(0, 512, 128):
                if ks + col < 0 or ks + col >= S:
                    wm[k, col:col + 128] = NEG
        m["wmask"] = wm
        maps.append(m)
    return maps


F0_HALO = 512
F0_TEXT = TPC + F0_HALO
F0_QE0 = 384
F0_OCW1 = 44
F0_HGW2 = 22


def build_f0pair(first, dbg=False):
    core = Core("f0pair")
    P = core.P

    yf = core.inp("yf", [D, F0_TEXT], F32)
    wqkv_d = core.inp("wqkv", [D, 3 * D], F32)
    wo_d = core.inp("wo", [D, D], F32)
    w1_d = core.inp("w1p", [D, F0_OCW1 * 128], F32)
    w2_d = core.inp("w2p", [F0_HGW2 * 128, D], F32)
    b1_d = core.inp("b1p", [128, F0_OCW1], F32)
    b2_d = core.inp("b2c", [128, DC], F32)
    lng_d = core.inp("lngc", [128, DC], F32)
    lnb_d = core.inp("lnbc", [128, DC], F32)
    qs_d = core.inp("qscol", [128, 1], F32)
    ks_d = core.inp("kscol", [128, 1], F32)
    qsr_d = core.inp("qsrcol", [128, 1], F32)
    ksr_d = core.inp("ksrcol", [128, 1], F32)
    rotq_d = core.inp("rotq", [128, 2, 5, 512], BF16)
    rotk_d = core.inp("rotk", [128, 2, 5, 768], BF16)
    rotm_d = core.inp("rotm", [128, 128], BF16)
    e2_d = core.inp("expand2", [128, 128], F32)
    o64_d = core.inp("ones64", [128, 2], BF16)
    mdiag_d = core.inp("mdiag", [128, 128], BF16)
    mfar_d = core.inp("mfar", [128, 128], BF16)
    kmask_d = core.inp("kmask", [1, F0_TEXT], F32)
    edge_d = core.inp("edgecol", [128, 1], F32)

    yout = core.out("yout", [D, TPC], F32)
    if dbg:
        dbg_y2 = core.out("dbg_y2", [D, TPC], F32)
        dbg_yb2 = core.out("dbg_yb2", [128, DC, TPC + 1], BF16)
        dbg_p = core.out("dbg_pf", [128, 384], BF16)
        dbg_q = core.out("dbg_qf", [128, DC, 512], BF16)
        dbg_k = core.out("dbg_kf", [128, DC, 768], BF16)

    consts = dev_consts(core)
    ydram = core.nc.dram_tensor("ydram", [D, TPC], F32, kind="Internal").ap()

    small = [("qsc", qs_d, [128, 1], F32), ("ksc", ks_d, [128, 1], F32),
             ("qsr", qsr_d, [128, 1], F32), ("ksr", ksr_d, [128, 1], F32),
             ("edg", edge_d, [128, 1], F32), ("rotm", rotm_d, [128, 128], BF16),
             ("e2", e2_d, [128, 128], F32), ("o64", o64_d, [128, 2], BF16),
             ("mdiag", mdiag_d, [128, 128], BF16), ("mfar", mfar_d, [128, 128], BF16),
             ("kmask", kmask_d, [1, F0_TEXT], F32)]
    sc = {}
    for k, dap, shape, dt in small:
        sc[k] = core.sb_right(f"sc_{k}", shape, dt)
        P.dma("sync", sc[k][:], dap, writes=[("c", k)], tag=f"ld_{k}")
    P.barrier()

    yb2 = core.sb_right("yb2", [128, DC, TPC + 1], BF16)

    with core.phase():
        wl = WeightLoader(core, stage_cols=3072, consts=consts, nbuf=1)
        wqkv = wl.load(wqkv_d, "wqkv", D, 3 * D)
        rows_state = None
        if first:
            with core.phase():
                xpan = core.sb("xpan", [128, DC, 512], BF16)
                xld = [core.sb(f"xld{j}", [128, 512], F32) for j in range(2)]
                rows_state = RowsDram(core, "lnt", F0_TEXT, nrows=2)
                ra = core.sb("st_ra", [1, 512], F32)
                rb = core.sb("st_rb", [1, 512], F32)
                rc_ = core.sb("st_rc", [1, 512], F32)
                sq0 = core.sb("st_sq", [128, 512], BF16)
                ps0 = core.psum("st_ps")
                ka, kb, kc2 = ("lnt", "ra"), ("lnt", "rb"), ("lnt", "rc")
                for g in range(5):
                    t0 = g * 512
                    for dc in range(DC):
                        j = dc % 2
                        P.dma("scalar", xld[j][:], yf[dc * 128:(dc + 1) * 128, t0:t0 + 512],
                              reads=[("xld", j)], writes=[("xld", j)], tag="ldst")
                        P.op("vector", v_copy(xpan[:, dc, :], xld[j][:]),
                             reads=[("xld", j), ("xpan", dc)], writes=[("xpan", dc)],
                             tag="cvst")
                    for dc in range(DC):
                        P.op("tensor", t_matmul(ps0[0:1, 0:512], consts["negokmb"],
                                                xpan[:, dc, :], start=dc == 0, stop=dc == DC - 1),
                             reads=[("xpan", dc), ("c", "negokmb")], writes=[("st_ps",)],
                             tag="st_mu")
                    for dc in range(DC):
                        P.op("scalar", s_act(sq0[:], xpan[:, dc, :], AF.Square),
                             reads=[("xpan", dc), ("st_sq",)], writes=[("st_sq",)],
                             tag="st_sq")
                        P.op("tensor", t_matmul(ps0[32:33, 0:512], consts["onesokmb"],
                                                sq0[:], start=dc == 0, stop=dc == DC - 1),
                             reads=[("st_sq",), ("c", "onesokmb")], writes=[("st_ps",)],
                             tag="st_ms")
                    P.op("vector", v_copy(ra[:], ps0[0:1, 0:512]),
                         reads=[("st_ps",), ka], writes=[ka], tag="st_ev")
                    P.op("vector", v_copy(rb[:], ps0[32:33, 0:512]),
                         reads=[("st_ps",), kb], writes=[kb], tag="st_ev2")
                    P.dma("gpsimd", rows_state.d[0:1, t0:t0 + 512], ra[:],
                          reads=[ka], writes=[("lnt", "row", 0)], tag="st_r0")
                    P.op("vector", v_tt(rc_[:], ra[:], ra[:], ALU.mult),
                         reads=[ka, kc2], writes=[kc2], tag="st_mu2")
                    P.op("vector", v_tt(rb[:], rb[:], rc_[:], ALU.subtract),
                         reads=[kb, kc2], writes=[kb], tag="st_var")
                    P.op("scalar", s_act(rb[:], rb[:], AF.Ln, bias=1e-5),
                         reads=[kb], writes=[kb], tag="st_ln")
                    P.op("scalar", s_act(rb[:], rb[:], AF.Exp, scale=-0.5),
                         reads=[kb], writes=[kb], tag="st_exp")
                    P.dma("gpsimd", rows_state.d[1:2, t0:t0 + 512], rb[:],
                          reads=[kb], writes=[("lnt", "row", 1)], tag="st_r1")

        KB = core.sb("kbuf", [128, DC, 768], BF16)
        VR = core.sb("vring", [128, 2, 4, D], BF16)
        QB = core.sb("qbuf", [128, DC, 512], BF16)
        ybg = [core.sb("ybg0", [128, DC, 769], BF16)]
        tabg = core.sb("tabg", [128, 2, 768], BF16)
        sqb = core.sb("sqs", [128, 512], BF16)
        f1b = core.sb("f1b", [128, 512], F32)
        f2b = core.sb("f2b", [128, 512], F32)
        rn2 = core.sb("rn2", [2, 512], F32)
        pb = core.sb("pbuf", [128, 384], BF16)
        ptb = core.sb("ptbuf", [128, 2, 3, 128], BF16)
        mcol = core.sb("mcolf", [128, 4], F32)
        ohb = core.sb("ohb", [128, DC, 512], BF16)
        yst = [core.sb(f"yst{j}", [128, 770], F32) for j in range(2)]
        wosl = [core.sb(f"wosl{j}", [128, DC, 128], BF16) for j in range(2)]
        psQK = core.psum("psQK")
        psS = [core.psum(f"psSf{j}") for j in range(2)]
        psO = core.psum("psOf")
        psR = core.psum("psR")
        psN = core.psum("psN")

        for p in range(5):
            g0 = p * 512
            gslot = p % 2
            gj = 0
            lo = max(g0 - 257, 0)
            off = lo - (g0 - 257)
            for dc in range(DC):
                j = dc % 2
                if off:
                    P.op("vector", v_memset(ybg[gj][:, dc, 0:off], 0.0),
                         reads=[("ybg", gj, dc)], writes=[("ybg", gj, dc)], tag="zb_pan")
                P.dma("scalar", yst[j][:, 0:769 - off],
                      yf[dc * 128:(dc + 1) * 128, lo:g0 + 512],
                      reads=[("yst", j)], writes=[("yst", j)], tag="ld_pan")
                P.op("vector", v_copy(ybg[gj][:, dc, off:769], yst[j][:, 0:769 - off]),
                     reads=[("yst", j), ("ybg", gj, dc)], writes=[("ybg", gj, dc)],
                     tag="cv_pan")
            if first:
                for (a0, a1) in ((off, off + 385), (off + 385, 769)):
                    if a1 <= a0:
                        continue
                    e0 = g0 - 257 + a0
                    n_ = a1 - a0
                    rows_state.bcast(psR, ("psR",), 0, e0, e0 + n_, consts)
                    for dc in range(DC):
                        P.op("vector", v_tt(ybg[gj][:, dc, a0:a1], ybg[gj][:, dc, a0:a1],
                                            psR[:, 0:n_], ALU.add),
                             reads=[("ybg", gj, dc), ("psR",)],
                             writes=[("ybg", gj, dc)], tag="pl_mu")
                    rows_state.bcast(psR, ("psR",), 1, e0, e0 + n_, consts)
                    for dc in range(DC):
                        P.op("vector", v_tt(ybg[gj][:, dc, a0:a1], ybg[gj][:, dc, a0:a1],
                                            psR[:, 0:n_], ALU.mult),
                             reads=[("ybg", gj, dc), ("psR",)],
                             writes=[("ybg", gj, dc)], tag="pl_rs")

            for kind, oc0 in (("q", 0), ("k", DC)):
                segs = ((257, 0, 512),) if kind == "q" else ((1, 0, 512), (513, 512, 256))
                for oc in range(DC):
                    for (ybase, dbase, wid) in segs:
                        for ki in range(DC):
                            soff = ybase if ki < 4 else ybase - 1
                            P.op("tensor", t_matmul(psQK[:, 0:wid],
                                                    wqkv[:, ki, (oc0 + oc) * 128:(oc0 + oc + 1) * 128],
                                                    ybg[gj][:, ki, soff:soff + wid],
                                                    start=ki == 0, stop=ki == DC - 1),
                                 reads=[("wqkv", ki), ("ybg", gj, ki)], writes=[("psQK",)],
                                 tag="qkv_mm")
                        if kind == "q":
                            P.op("vector", v_copy(QB[:, oc, dbase:dbase + wid], psQK[:, 0:wid]),
                                 reads=[("psQK",), ("qb0", oc)], writes=[("qb0", oc)],
                                 tag="qkv_ev")
                        else:
                            P.op("vector", v_copy(KB[:, oc, dbase:dbase + wid], psQK[:, 0:wid]),
                                 reads=[("psQK",), ("kr0", oc)],
                                 writes=[("kr0", oc)], tag="qkv_ev")
            for tt in range(4):
                for half in range(2):
                    for ki in range(DC):
                        soff = 257 if ki < 4 else 256
                        P.op("tensor", t_matmul(psQK[:, 0:512],
                                                ybg[gj][:, ki, soff + tt * 128:soff + (tt + 1) * 128],
                                                wqkv[:, ki, 2 * D + half * 512:2 * D + (half + 1) * 512],
                                                start=ki == 0, stop=ki == DC - 1),
                             reads=[("wqkv", ki), ("ybg", gj, ki)], writes=[("psQK",)],
                             tag="v_mm")
                    P.op("vector", v_copy(VR[:, gslot, tt, half * 512:(half + 1) * 512],
                                          psQK[:, 0:512]),
                         reads=[("psQK",), ("vr0", gslot, tt)],
                         writes=[("vr0", gslot, tt)], tag="v_ev")
            for kind in ("q", "k"):
                wid = 512 if kind == "q" else 768
                tabd = rotq_d if kind == "q" else rotk_d
                scol = "qsc" if kind == "q" else "ksc"
                srol = "qsr" if kind == "q" else "ksr"
                P.dma("sync", tabg[:, :, 0:wid], tabd[:, :, p, :],
                      reads=[("tabg",)], writes=[("tabg",)], tag="ld_tab")
                P.op("vector", v_ts(tabg[:, 0, 0:wid], tabg[:, 0, 0:wid],
                                    sc[scol][:], None, op0=ALU.mult),
                     reads=[("tabg",), ("c", scol)], writes=[("tabg",)], tag="tabq")
                P.op("vector", v_ts(tabg[:, 1, 0:wid], tabg[:, 1, 0:wid],
                                    sc[srol][:], None, op0=ALU.mult),
                     reads=[("tabg",), ("c", srol)], writes=[("tabg",)], tag="tabr")
                for hp in range(DC):
                    src = QB[:, hp, :] if kind == "q" else KB[:, hp, :]
                    skey = ("qb0", hp) if kind == "q" else ("kr0", hp)
                    for w0 in range(0, wid, 512):
                        w1 = min(w0 + 512, wid)
                        n_ = w1 - w0
                        sv = src[:, w0:w1] if kind == "q" else KB[:, hp, w0:w1]
                        P.op("scalar", s_act(sqb[:, 0:n_], sv, AF.Square),
                             reads=[skey, ("sqs",)], writes=[("sqs",)], tag="nsq")
                        P.op("tensor", t_matmul(psN[0:2, 0:n_], sc["o64"][:], sqb[:, 0:n_],
                                                start=True, stop=True),
                             reads=[("sqs",), ("c", "o64")], writes=[("psN",)], tag="n_mm")
                        P.op("vector", v_copy(rn2[:, 0:n_], psN[0:2, 0:n_]),
                             reads=[("psN",), ("rn2",)], writes=[("rn2",)], tag="n_ev")
                        P.op("scalar", s_act(rn2[:, 0:n_], rn2[:, 0:n_], AF.Ln, bias=1e-30),
                             reads=[("rn2",)], writes=[("rn2",)], tag="n_ln")
                        P.op("scalar", s_act(rn2[:, 0:n_], rn2[:, 0:n_], AF.Exp, scale=0.5),
                             reads=[("rn2",)], writes=[("rn2",)], tag="n_exp")
                        P.op("vector", v_ts(rn2[:, 0:n_], rn2[:, 0:n_], 1e-12, None,
                                            op0=ALU.max),
                             reads=[("rn2",)], writes=[("rn2",)], tag="n_max")
                        P.op("vector", v_recip(rn2[:, 0:n_], rn2[:, 0:n_]),
                             reads=[("rn2",)], writes=[("rn2",)], tag="n_rec")
                        P.op("tensor", t_matmul(psN[:, 0:n_], sc["e2"][0:2, :], rn2[:, 0:n_],
                                                start=True, stop=True),
                             reads=[("rn2",), ("c", "e2")], writes=[("psN",)], tag="nb_mm")
                        P.op("tensor", t_matmul(psR[:, 0:n_], sc["rotm"][:], sv,
                                                start=True, stop=True),
                             reads=[skey, ("c", "rotm")], writes=[("psR",)], tag="rot_mm")
                        P.op("vector", v_tt(f1b[:, 0:n_], sv, tabg[:, 0, w0:w1], ALU.mult),
                             reads=[skey, ("tabg",), ("f1b",)], writes=[("f1b",)],
                             tag="r_a")
                        P.op("vector", v_tt(f2b[:, 0:n_], psR[:, 0:n_], tabg[:, 1, w0:w1],
                                            ALU.mult),
                             reads=[("psR",), ("tabg",), ("f2b",)], writes=[("f2b",)],
                             tag="r_b")
                        P.op("vector", v_tt(f1b[:, 0:n_], f1b[:, 0:n_], f2b[:, 0:n_],
                                            ALU.add),
                             reads=[("f1b",), ("f2b",)], writes=[("f1b",)], tag="r_add")
                        P.op("vector", v_tt(sv, f1b[:, 0:n_], psN[:, 0:n_], ALU.mult),
                             reads=[("f1b",), ("psN",), skey], writes=[skey], tag="r_rn")
            if dbg and p == 1:
                for dc in range(DC):
                    P.dma("gpsimd", dbg_q[:, dc, :], QB[:, dc, :],
                          reads=[("qb0", dc)], writes=[("dbq", dc)], tag="dbgq")
                    P.dma("gpsimd", dbg_k[:, dc, :], KB[:, dc, :],
                          reads=[("kr0", dc)], writes=[("dbk", dc)], tag="dbgk")
            qts = [q for q in range(g0, g0 + 512, 128) if q >= F0_QE0]
            if len(qts) < 4:
                P.op("vector", v_memset(psO[:, 0:512 - len(qts) * 128], 0.0),
                     reads=[("psOf",)], writes=[("psOf",)], tag="pv_z")
            for h in range(16):
                hp, hh = h // 2, (h % 2) * 64
                ptj = h % 2
                for q0 in qts:
                    qi = (q0 - g0) // 128
                    ps = psS[(h + qi) % 2]
                    pskey = ("psSf", (h + qi) % 2)
                    P.op("tensor", t_matmul(ps[:, 0:384], consts["ones1"],
                                            sc["kmask"][0:1, q0 - 256:q0 + 128],
                                            start=True, stop=False),
                         reads=[("c", "kmask"), ("c", "ones1")], writes=[pskey],
                         tag="s_km")
                    P.op("tensor", t_matmul(ps[:, 0:384],
                                            QB[hh:hh + 64, hp, qi * 128:(qi + 1) * 128],
                                            KB[hh:hh + 64, hp, qi * 128:qi * 128 + 384],
                                            start=False, stop=False),
                         reads=[("qb0", hp), ("kr0", hp)], writes=[pskey],
                         tag="s_mm")
                    P.op("tensor", t_matmul(ps[:, 256:384], sc["mdiag"][:],
                                            consts["identb"], start=False, stop=False),
                         reads=[("c", "mdiag"), ("c", "identb")], writes=[pskey],
                         tag="s_md")
                    P.op("tensor", t_matmul(ps[:, 0:128], sc["mfar"][:],
                                            consts["identb"], start=False, stop=True),
                         reads=[("c", "mfar"), ("c", "identb")], writes=[pskey],
                         tag="s_mf")
                    P.op("vector", v_red(mcol[:, 0:1], ps[:, 0:384], "max"),
                         reads=[pskey, ("mcolf",)], writes=[("mcolf",)], tag="s_max")
                    P.op("vector", v_ts(mcol[:, 1:2], mcol[:, 0:1], -8.0, None,
                                        op0=ALU.mult),
                         reads=[("mcolf",)], writes=[("mcolf",)], tag="s_neg")
                    P.op("scalar", s_act(pb[:], ps[:, 0:384], AF.Exp,
                                         bias=mcol[:, 1:2], scale=8.0,
                                         accum=mcol[:, 2:3]),
                         reads=[pskey, ("mcolf",), ("pbuf",)],
                         writes=[("pbuf",), ("mcolf",)], tag="s_exp")
                    P.op("vector", v_recip(mcol[:, 3:4], mcol[:, 2:3]),
                         reads=[("mcolf",)], writes=[("mcolf",)], tag="s_rec")
                    P.op("vector", v_ts(pb[:], pb[:], mcol[:, 3:4], None, op0=ALU.mult),
                         reads=[("pbuf",), ("mcolf",)], writes=[("pbuf",)], tag="s_nrm")
                    if dbg and p == 1 and h == 15 and qi == 0:
                        P.dma("gpsimd", dbg_p, pb[:], reads=[("pbuf",)],
                              writes=[("dbp",)], tag="dbgp")
                    for kc in range(3):
                        P.dma_tr("scalar" if kc % 2 else "sync",
                                 ptb[:, ptj, kc, :], pb[:, kc * 128:(kc + 1) * 128],
                                 reads=[("pbuf",)], writes=[("ptbuf", ptj, kc)],
                                 tag="s_tr")
                    for kc in range(3):
                        kcol = q0 - 256 + kc * 128
                        slot = (kcol // 512) % 2
                        tt = (kcol % 512) // 128
                        P.op("tensor", t_matmul(psO[hh:hh + 64, qi * 128:(qi + 1) * 128],
                                                VR[:, slot, tt, 64 * h:64 * h + 64],
                                                ptb[:, ptj, kc, :],
                                                start=kc == 0, stop=kc == 2,
                                                tile_position=(0, hh)),
                             reads=[("vr0", slot, tt), ("ptbuf", ptj, kc)],
                             writes=[("psOf",)], tag="pv_mm")
                if h % 2 == 1:
                    P.op("vector", v_copy(ohb[:, hp, :], psO[:, 0:512]),
                         reads=[("psOf",), ("ohb", hp)], writes=[("ohb", hp)],
                         tag="pv_ev")
            c0 = g0 - 511
            s_ = max(0, -c0)
            for oc in range(DC):
                j = oc % 2
                st = wl.stage[0]
                P.dma("sync", st[:, 0:1024].rearrange("p (a b) -> p a b", b=128),
                      wo_d[:, oc * 128:(oc + 1) * 128].rearrange("(a p) b -> p a b", p=128),
                      reads=[("wstage", 0)], writes=[("wstage", 0)], tag="ld_wo")
                P.op("scalar", s_copy(wosl[j][:].rearrange("p c k -> p (c k)"),
                                      st[:, 0:1024]),
                     reads=[("wstage", 0)], writes=[("wosl", j)], tag="cvt_wo")
                for ki in range(DC):
                    P.op("tensor", t_matmul(psS[0][:, 0:512], wosl[j][:, ki, :],
                                            ohb[:, ki, :], start=ki == 0, stop=ki == DC - 1),
                         reads=[("wosl", j), ("ohb", ki)], writes=[("psSf", 0)],
                         tag="wo_mm")
                P.dma("scalar", yst[j][:, 0:512], yf[oc * 128:(oc + 1) * 128, g0:g0 + 512],
                      reads=[("yst", j)], writes=[("yst", j)], tag="ld_yold")
                if first:
                    rows_state.bcast(psR, ("psR",), 0, g0, g0 + 512, consts)
                    P.op("vector", v_tt(yst[j][:, 0:512], yst[j][:, 0:512], psR[:, 0:512], ALU.add),
                         reads=[("yst", j), ("psR",)], writes=[("yst", j)], tag="y_mu")
                    rows_state.bcast(psR, ("psR",), 1, g0, g0 + 512, consts)
                    P.op("vector", v_tt(yst[j][:, 0:512], yst[j][:, 0:512], psR[:, 0:512], ALU.mult),
                         reads=[("yst", j), ("psR",)], writes=[("yst", j)], tag="y_rs")
                P.op("vector", v_tt(yst[j][:, 0:512], yst[j][:, 0:512], psS[0][:, 0:512], ALU.add),
                     reads=[("yst", j), ("psSf", 0)], writes=[("yst", j)], tag="y_res")
                if p >= 1:
                    P.dma("gpsimd", ydram[oc * 128:(oc + 1) * 128, g0 - 512:g0],
                          yst[j][:, 0:512], reads=[("yst", j)],
                          writes=[("ydr", oc, p)], tag="st_y2")
                P.op("vector", v_copy(yb2[:, oc, c0 + s_:c0 + 512], yst[j][:, s_:512]),
                     reads=[("yst", j), ("yb2", oc)], writes=[("yb2", oc)], tag="yb2")
        for dc in range(DC):
            P.op("vector", v_ts(yb2[:, dc, 0:1], yb2[:, dc, 0:1], sc["edg"][:], None,
                                op0=ALU.mult),
                 reads=[("yb2", dc), ("c", "edg")], writes=[("yb2", dc)], tag="edge")

    if dbg:
        for dc in range(DC):
            P.dma("gpsimd", dbg_y2[dc * 128:(dc + 1) * 128, :], ydram[dc * 128:(dc + 1) * 128, :],
                  reads=[("ydr", dc, 1), ("ydr", dc, 2), ("ydr", dc, 3), ("ydr", dc, 4)],
                  writes=[("dby", dc)], tag="dbg")
            P.dma("gpsimd", dbg_yb2[:, dc, :], yb2[:, dc, :],
                  reads=[("yb2", dc)], writes=[("dbyb", dc)], tag="dbg")
    with core.phase():
        wl = WeightLoader(core, stage_cols=2048, consts=consts)
        b1c = core.sb("b1c", [128, F0_OCW1], F32)
        P.dma("sync", b1c[:], b1_d, writes=[("b1", "bias")], tag="ld_b1")
        b2c = core.sb("b2c_sb", [128, DC], F32)
        P.dma("sync", b2c[:], b2_d, writes=[("b2", "bias")], tag="ld_b2")
        lngc = core.sb("lngc_sb", [128, DC], F32)
        P.dma("sync", lngc[:], lng_d, writes=[("lng", "bias")], tag="ld_lng")
        lnbc = core.sb("lnbc_sb", [128, DC], F32)
        P.dma("sync", lnbc[:], lnb_d, writes=[("lnb", "bias")], tag="ld_lnb")
        P.barrier()
        w2f = wl.load(w2_d, "w2p", F0_HGW2 * 128, D)

        def xsh(dc, a, b):
            off = 1 if dc < 4 else 0
            return yb2[:, dc, a + off:b + off]

        def xshkeys(d_):
            return ("yb2", d_)

        with core.phase():
            rows2 = emit_ln_stats_dram(core, xsh, xshkeys, TPC, consts,
                                       "lnf", xdt=BF16, compose=False)
        xnb = core.sb("xnb", [128, DC, 512], BF16)
        tb = core.sb("tbuf", [128, F0_HGW2, 512], BF16)
        w1sl = [core.sb(f"w1slf{j}", [128, DC, 128], BF16) for j in range(2)]
        ast_ = [core.sb(f"astf{j}", [128, 512], F32) for j in range(2)]
        yst = [core.sb(f"ystf{j}", [128, 512], F32) for j in range(2)]
        psA = core.psum("psAf")
        psB = core.psum("psBf")
        psM = [core.psum(f"psMf{j}") for j in range(2)]
        for g in range(4):
            t0 = g * 512
            emit_ln_apply_dram(core, xsh, xshkeys, rows2, 0, 1, consts,
                               lambda d_: xnb[:, d_, :], lambda d_: ("xnb", d_),
                               t0, t0 + 512, psA, ("psAf",), psB, ("psBf",),
                               gcol_fn=lambda d_: lngc[:, d_:d_ + 1],
                               bcol_fn=lambda d_: lnbc[:, d_:d_ + 1],
                               gbkeys=[("lng", "bias"), ("lnb", "bias")])
            for oc in range(F0_OCW1):
                j = oc % 2
                st = wl.stage[j]
                P.dma("sync", st[:, 0:1024].rearrange("p (a b) -> p a b", b=128),
                      w1_d[:, oc * 128:(oc + 1) * 128].rearrange("(a p) b -> p a b", p=128),
                      reads=[("wstage", j)], writes=[("wstage", j)], tag="ld_w1f")
                P.op("scalar", s_copy(w1sl[j][:].rearrange("p c k -> p (c k)"),
                                      st[:, 0:1024]),
                     reads=[("wstage", j)], writes=[("w1slf", j)], tag="cvt_w1f")
                for ki in range(DC):
                    P.op("tensor", t_matmul(psM[j][:, 0:512], w1sl[j][:, ki, :],
                                            xnb[:, ki, :], start=ki == 0, stop=ki == DC - 1),
                         reads=[("w1slf", j), ("xnb", ki)], writes=[("psMf", j)],
                         tag="m1_mm")
                hh = 64 * (oc % 2)
                hgc = oc // 2
                P.op("scalar", s_act(tb[hh:hh + 64, hgc, :], psM[j][64:128, 0:512],
                                     AF.Gelu, bias=b1c[64:128, oc:oc + 1]),
                     reads=[("psMf", j), ("b1", "bias"), ("tbuf", hgc)],
                     writes=[("tbuf", hgc)], tag="m1_gelu")
                P.op("vector", v_ts(ast_[j][hh:hh + 64, :], psM[j][0:64, 0:512],
                                    b1c[0:64, oc:oc + 1], None, op0=ALU.add),
                     reads=[("psMf", j), ("b1", "bias"), ("astf", j)],
                     writes=[("astf", j)], tag="m1_a")
                P.op("vector", v_tt(tb[hh:hh + 64, hgc, :], tb[hh:hh + 64, hgc, :],
                                    ast_[j][hh:hh + 64, :], ALU.mult),
                     reads=[("tbuf", hgc), ("astf", j)],
                     writes=[("tbuf", hgc)], tag="m1_t")
            for do in range(DC):
                j = do % 2
                for hg in range(F0_HGW2):
                    P.op("tensor", t_matmul(psM[j][:, 0:512], w2f[:, hg, do * 128:(do + 1) * 128],
                                            tb[:, hg, :], start=hg == 0, stop=hg == F0_HGW2 - 1),
                         reads=[("w2p", hg), ("tbuf", hg)], writes=[("psMf", j)],
                         tag="m2_mm")
                P.dma("scalar", yst[j][:], ydram[do * 128:(do + 1) * 128, t0:t0 + 512],
                      reads=[("ystf", j)] + [("ydr", do, pp) for pp in range(1, 5)],
                      writes=[("ystf", j)], tag="ld_y2")
                P.op("vector", v_ts(yst[j][:], yst[j][:], b2c[:, do:do + 1], None,
                                    op0=ALU.add),
                     reads=[("ystf", j), ("b2", "bias")], writes=[("ystf", j)],
                     tag="m2_b")
                P.op("vector", v_tt(yst[j][:], yst[j][:], psM[j][:, 0:512], ALU.add),
                     reads=[("ystf", j), ("psMf", j)], writes=[("ystf", j)], tag="m2_res")
                P.dma("gpsimd", yout[do * 128:(do + 1) * 128, t0:t0 + 512], yst[j][:],
                      reads=[("ystf", j)], writes=[("yo", do, g)], tag="st_y3")
    return core


def f0_host_inputs(inputs, consts_h, layer, state_f, first):
    DH = 64
    freqs, scale = None, None
    half = np.arange(0, DH, 2, dtype=np.float32)
    inv_freq = 1.0 / 10000.0 ** (half / DH)
    t = np.arange(S, dtype=np.float32)
    fr = t[:, None] * inv_freq[None, :]
    fr = np.concatenate([fr, fr], -1)
    base_scale = (half + 0.4 * DH) / (1.4 * DH)
    power = (t - S // 2) / 128.0
    sc_ = base_scale[None, :] ** power[:, None]
    sc_ = np.concatenate([sc_, sc_], -1)
    cosf = np.cos(fr)
    sinf = np.sin(fr)
    logb = np.log(np.concatenate([base_scale, base_scale]))

    W1 = np.asarray(inputs["L0_ffn_W1"][layer])
    b1 = np.asarray(inputs["L0_ffn_b1"][layer])
    W2 = np.asarray(inputs["L0_ffn_W2"][layer])
    HALF = 2730
    W1p = np.zeros((D, F0_OCW1 * 128), np.float32)
    b1p = np.zeros((128, F0_OCW1), np.float32)
    for jj in range(F0_OCW1):
        a0, a1 = 64 * jj, min(64 * jj + 64, HALF)
        n = a1 - a0
        if n > 0:
            W1p[:, 128 * jj:128 * jj + n] = W1[:, a0:a1]
            W1p[:, 128 * jj + 64:128 * jj + 64 + n] = W1[:, HALF + a0:HALF + a1]
            b1p[0:n, jj] = b1[a0:a1]
            b1p[64:64 + n, jj] = b1[HALF + a0:HALF + a1]
    W2p = np.zeros((F0_HGW2 * 128, D), np.float32)
    W2p[0:HALF] = W2

    rotm = np.zeros((128, 128), np.float32)
    for base in (0, 64):
        for mm_ in range(32):
            rotm[base + mm_ + 32, base + mm_] = -1.0
            rotm[base + mm_, base + mm_ + 32] = 1.0
    e2 = np.zeros((128, 128), np.float32)
    e2[0, 0:64] = 1.0
    e2[1, 64:128] = 1.0
    o64 = np.zeros((128, 2), np.float32)
    o64[0:64, 0] = 1.0
    o64[64:128, 1] = 1.0
    ii, jj_ = np.mgrid[0:128, 0:128]
    mdiag = np.where(ii > jj_, NEG, 0.0).astype(np.float32)
    mfar = np.where(ii < jj_, NEG, 0.0).astype(np.float32)

    def cols(v):
        return np.ascontiguousarray(np.asarray(v).reshape(-1, 128).T)

    maps = []
    for c in range(NCORES):
        lo = 2048 * c - F0_HALO
        ye = np.zeros((D, F0_TEXT), np.float32)
        a, b = max(0, lo), min(S, lo + F0_TEXT)
        ye[:, a - lo:b - lo] = state_f[:, a:b]
        rotq = np.zeros((64, 2, 5, 512), np.float32)
        rotk = np.zeros((64, 2, 5, 768), np.float32)
        for pp in range(5):
            tref = lo + 512 * pp
            tq = np.arange(tref, tref + 512)
            tk = np.arange(tref - 256, tref + 512)
            for arr, tt_ in ((rotq, tq), (rotk, tk)):
                tcl = np.clip(tt_, 0, S - 1)
                e = (tt_ - tref)[:, None] / 128.0
                scl = np.exp(e * logb[None, :]) if arr is rotq else np.exp(-e * logb[None, :])
                arr[:, 0, pp, :] = (cosf[tcl] * scl).T
                arr[:, 1, pp, :] = (sinf[tcl] * scl).T
        rotq = np.concatenate([rotq, rotq], axis=0).astype(BF)
        rotk = np.concatenate([rotk, rotk], axis=0).astype(BF)
        km = np.zeros((1, F0_TEXT), np.float32)
        if a - lo > 0:
            km[0, 0:a - lo] = NEG
        edge = np.ones((128, 1), np.float32)
        if c == 0:
            edge[:] = 0.0
        m = dict(consts_h)
        m.update({
            "yf": ye, "wqkv": np.asarray(inputs["L0_ffn_Wqkv"][layer]),
            "wo": np.asarray(inputs["L0_ffn_Wo"][layer]),
            "w1p": W1p, "w2p": W2p, "b1p": b1p,
            "b2c": cols(inputs["L0_ffn_b2"][layer]),
            "lngc": cols(inputs["L0_ffn_ln_g"][layer]),
            "lnbc": cols(inputs["L0_ffn_ln_b"][layer]),
            "qscol": np.tile(np.asarray(inputs["L0_ffn_q_scale"][layer]), 2).reshape(128, 1).astype(np.float32),
            "kscol": np.tile(np.asarray(inputs["L0_ffn_k_scale"][layer]), 2).reshape(128, 1).astype(np.float32),
            "qsrcol": np.tile(np.roll(np.asarray(inputs["L0_ffn_q_scale"][layer]), 32), 2).reshape(128, 1).astype(np.float32),
            "ksrcol": np.tile(np.roll(np.asarray(inputs["L0_ffn_k_scale"][layer]), 32), 2).reshape(128, 1).astype(np.float32),
            "rotq": rotq, "rotk": rotk, "rotm": rotm.astype(BF), "expand2": e2,
            "ones64": o64.astype(BF), "mdiag": mdiag.astype(BF),
            "mfar": mfar.astype(BF), "kmask": km, "edgecol": edge,
        })
        maps.append(m)
    return maps


def build_f0fin():
    core = Core("f0fin")
    P = core.P
    x1p = core.inp("x1p", [D, TPC], F32)
    y6p = core.inp("y6p", [D, TPC], F32)
    wqT_d = core.inp("wqT", [D, D], F32)
    wkT_d = core.inp("wkT", [D, D], F32)
    wvT_d = core.inp("wvT", [D, D], F32)
    bqc_d = core.inp("bqc", [128, DC], F32)
    bkc_d = core.inp("bkc", [128, DC], F32)
    bvr_d = core.inp("bvr", [1, D], F32)
    x2f = core.out("x2f", [D, TPC], F32)
    qf_o = core.out("qf", [128, DC, TPC], BF16)
    kf_o = core.out("kf", [128, DC, TPC], BF16)
    vt_o = core.out("vt", [TPC, D], BF16)
    qn_o = core.out("qn", [1, TPC], F32)
    kn_o = core.out("kn", [1, TPC], F32)
    consts = dev_consts(core)

    xb2 = core.sb_right("xb2", [128, DC, TPC], BF16)
    with core.phase():
        yb6 = core.sb("yb6", [128, DC, TPC], BF16)
        st2 = [core.sb(f"st{j}", [128, 512], F32) for j in range(2)]
        for dc in range(DC):
            for g in range(4):
                j = (dc * 4 + g) % 2
                P.dma("scalar", st2[j][:], y6p[dc * 128:(dc + 1) * 128, g * 512:(g + 1) * 512],
                      reads=[("st", j)], writes=[("st", j)], tag="ldy6")
                P.op("vector", v_copy(yb6[:, dc, g * 512:(g + 1) * 512], st2[j][:]),
                     reads=[("st", j), ("yb6", dc)], writes=[("yb6", dc)], tag="cvy6")
        with core.phase():
            rows6 = emit_ln_stats_dram(core, lambda d_, a, b: yb6[:, d_, a:b],
                                       lambda d_: ("yb6", d_), TPC, consts, "ln6",
                                       xdt=BF16, compose=False)
        psA = core.psum("psA6")
        psB = core.psum("psB6")
        xst = [core.sb(f"xst{j}", [128, 512], F32) for j in range(2)]
        for g in range(4):
            t0 = g * 512
            rows6.bcast(psA, ("psA6",), 0, t0, t0 + 512, consts)
            rows6.bcast(psB, ("psB6",), 1, t0, t0 + 512, consts)
            for dc in range(DC):
                j = (dc + g) % 2
                P.dma("sync", xst[j][:], x1p[dc * 128:(dc + 1) * 128, t0:t0 + 512],
                      reads=[("xst", j)], writes=[("xst", j)], tag="ldx1")
                P.op("vector", v_tt(st2[j][:], yb6[:, dc, t0:t0 + 512], psA[:, 0:512],
                                    ALU.add),
                     reads=[("yb6", dc), ("psA6",), ("st", j)], writes=[("st", j)],
                     tag="f_mu")
                P.op("vector", v_tt(st2[j][:], st2[j][:], psB[:, 0:512], ALU.mult),
                     reads=[("st", j), ("psB6",)], writes=[("st", j)], tag="f_rs")
                P.op("vector", v_tt(xst[j][:], xst[j][:], st2[j][:], ALU.add),
                     reads=[("xst", j), ("st", j)], writes=[("xst", j)], tag="f_res")
                P.dma("gpsimd", x2f[dc * 128:(dc + 1) * 128, t0:t0 + 512], xst[j][:],
                      reads=[("xst", j)], writes=[("x2o", dc, g)], tag="st_x2")
                P.op("vector", v_copy(xb2[:, dc, t0:t0 + 512], xst[j][:]),
                     reads=[("xst", j), ("xb2", dc)], writes=[("xb2", dc)], tag="cvx2")
    with core.phase():
        wl = WeightLoader(core, stage_cols=1024, consts=consts)
        wq = wl.load(wqT_d, "wq", D, D)
        wk = wl.load(wkT_d, "wk", D, D)
        wv = wl.load(wvT_d, "wv", D, D)
        bqc = core.sb("bqc_sb", [128, DC], F32)
        P.dma("sync", bqc[:], bqc_d, writes=[("bq", "bias")], tag="ldbq")
        bkc = core.sb("bkc_sb", [128, DC], F32)
        P.dma("sync", bkc[:], bkc_d, writes=[("bk", "bias")], tag="ldbk")
        bvr = core.sb("bvr_sb", [1, D], F32)
        P.dma("sync", bvr[:], bvr_d, writes=[("bvr",)], tag="ldbv")
        with core.phase():
            rows2 = emit_ln_stats_dram(core, lambda d_, a, b: xb2[:, d_, a:b],
                                       lambda d_: ("xb2", d_), TPC, consts, "lnx2",
                                       xdt=BF16, compose=False)
        bvb = core.sb("bvb", [128, D], F32)
        psA = core.psum("psA7")
        psB = core.psum("psB7")
        for half in range(2):
            emit_row_bcast(core, psA[:, 0:512], bvr[0:1, half * 512:(half + 1) * 512],
                           consts, reads=[("bvr",)], writes=[("psA7",)])
            P.op("vector", v_copy(bvb[:, half * 512:(half + 1) * 512], psA[:, 0:512]),
                 reads=[("psA7",)], writes=[("bvb",)], tag="bvb")
        xnb = core.sb("xnb2", [128, DC, 512], BF16)
        sqb = core.sb("sqb2", [128, 512], BF16)
        rowb = core.sb("rowb2", [1, 512], F32)
        vst = [core.sb(f"vst{j}", [128, D], BF16) for j in range(2)]
        psQ = [core.psum(f"psQ7{j}") for j in range(2)]
        psV = core.psum("psV7", cols=1024)
        psN = core.psum("psN7")
        for g in range(4):
            t0 = g * 512
            emit_ln_apply_dram(core, lambda d_, a, b: xb2[:, d_, a:b],
                               lambda d_: ("xb2", d_), rows2, 0, 1, consts,
                               lambda d_: xnb[:, d_, :], lambda d_: ("xnb", d_),
                               t0, t0 + 512, psA, ("psA7",), psB, ("psB7",))
            for (w, wname, bc, bkey, dst, dkey, nout) in (
                    (wq, "wq", bqc, ("bq", "bias"), qf_o, "qn", qn_o),
                    (wk, "wk", bkc, ("bk", "bias"), kf_o, "kn", kn_o)):
                for oc in range(DC):
                    j = oc % 2
                    for ki in range(DC):
                        P.op("tensor", t_matmul(psQ[j][:, 0:512], w[:, ki, oc * 128:(oc + 1) * 128],
                                                xnb[:, ki, :], start=ki == 0, stop=ki == DC - 1),
                             reads=[(wname, ki), ("xnb", ki)], writes=[("psQ7", j)],
                             tag="qk_mm")
                    P.op("vector", v_ts(sqb[:] if False else xnb[:, 0, 0:512] if False else sqb[:],
                                        psQ[j][:, 0:512], bc[:, oc:oc + 1], None,
                                        op0=ALU.add),
                         reads=[("psQ7", j), bkey, ("sqb2",)], writes=[("sqb2",)],
                         tag="qk_b")
                    P.dma("gpsimd", dst[:, oc, t0:t0 + 512], sqb[:],
                          reads=[("sqb2",)], writes=[(wname + "o", oc, g)], tag="st_qk")
                    P.op("scalar", s_act(sqb[:], sqb[:], AF.Square),
                         reads=[("sqb2",)], writes=[("sqb2",)], tag="qk_sq")
                    P.op("tensor", t_matmul(psN[0:1, 0:512], consts["onesokmb"], sqb[:],
                                            start=oc == 0, stop=oc == DC - 1),
                         reads=[("sqb2",), ("c", "onesokm")], writes=[("psN7",)],
                         tag="qk_nm")
                P.op("vector", v_copy(rowb[:], psN[0:1, 0:512]),
                     reads=[("psN7",), ("rowb2",)], writes=[("rowb2",)], tag="qk_nev")
                P.dma("gpsimd", nout[0:1, t0:t0 + 512], rowb[:],
                      reads=[("rowb2",)], writes=[(wname + "n", g)], tag="st_n")
            for tt in range(4):
                j = tt % 2
                for half in range(2):
                    for ki in range(DC):
                        P.op("tensor", t_matmul(psV[:, half * 512:(half + 1) * 512],
                                                xnb[:, ki, tt * 128:(tt + 1) * 128],
                                                wv[:, ki, half * 512:(half + 1) * 512],
                                                start=ki == 0, stop=ki == DC - 1),
                             reads=[("wv", ki), ("xnb", ki)], writes=[("psV7",)],
                             tag="v_mm")
                for half in range(2):
                    P.op("vector", v_tt(vst[j][:, half * 512:(half + 1) * 512],
                                        psV[:, half * 512:(half + 1) * 512],
                                        bvb[:, half * 512:(half + 1) * 512], ALU.add),
                         reads=[("psV7",), ("bvb",), ("vst", j)], writes=[("vst", j)],
                         tag="v_ev")
                P.dma("gpsimd", vt_o[t0 + tt * 128:t0 + (tt + 1) * 128, :], vst[j][:],
                      reads=[("vst", j)], writes=[("vto", g, tt)], tag="st_v")
    return core


def f0fin_host_inputs(inputs, consts_h, x1_f, y6_f):
    maps = []
    for c in range(NCORES):
        m = dict(consts_h)
        m.update({
            "x1p": np.ascontiguousarray(x1_f[:, c * TPC:(c + 1) * TPC]),
            "y6p": np.ascontiguousarray(y6_f[:, c * TPC:(c + 1) * TPC]),
            "wqT": np.ascontiguousarray(np.asarray(inputs["L1_attn_Wq"]).T),
            "wkT": np.ascontiguousarray(np.asarray(inputs["L1_attn_Wk"]).T),
            "wvT": np.ascontiguousarray(np.asarray(inputs["L1_attn_Wv"]).T),
            "bqc": np.ascontiguousarray(np.asarray(inputs["L1_attn_bq"]).reshape(-1, 128).T),
            "bkc": np.ascontiguousarray(np.asarray(inputs["L1_attn_bk"]).reshape(-1, 128).T),
            "bvr": np.asarray(inputs["L1_attn_bv"]).reshape(1, -1),
        })
        maps.append(m)
    return maps


def build_a1attn():
    core = Core("a1attn")
    P = core.P
    qf = core.inp("qfi", [128, DC, TPC], BF16)
    kfull_d = core.inp("kfull", [128, DC, S], BF16)
    vfull_d = core.inp("vfull", [S, D], BF16)
    qn_d = core.inp("qni", [1, TPC], F32)
    knfull_d = core.inp("knfull", [1, S], F32)
    x2p = core.inp("x2p", [D, TPC], F32)
    identf_d = core.inp("identf", [128, 128], F32)
    mdiag2_d = core.inp("mdiag2", [128, DC, 128], BF16)
    x3f = core.out("x3f", [D, TPC], F32)
    consts = dev_consts(core)
    sc = {}
    for k, dd, shape, dt in (("identf", identf_d, [128, 128], F32),
                             ("mdiag2", mdiag2_d, [128, DC, 128], BF16)):
        sc[k] = core.sb_right(f"sc_{k}", shape, dt)
        P.dma("sync", sc[k][:], dd, writes=[("c", k)], tag=f"ld_{k}")
    P.barrier()

    with core.phase():
        qb = core.sb("qbl", [128, DC, TPC], BF16)
        for dc in range(DC):
            P.dma("sync", qb[:, dc, :], qf[:, dc, :], writes=[("qb", dc)], tag="ldq")
        qnrow = core.sb("qnrow", [1, TPC], F32)
        P.dma("sync", qnrow[:], qn_d, writes=[("qnr",)], tag="ldqn")
        knrow = core.sb("knrow", [1, 1024], F32)
        kmax = core.sb("kmax", [1, 4], F32)
        brow = core.sb("brow", [1, 128], F32)
        bcol = core.sb("bcol", [128, 1], F32)
        lacc = core.sb("lacc", [128, 2], F32)
        pb = core.sb("pbA", [128, 512], BF16)
        ptb = core.sb("ptA", [128, 4, 128], BF16)
        ktb = [core.sb(f"ktb{j}", [128, DC, 512], BF16) for j in range(2)]
        vtb = [core.sb(f"vtb{j}", [128, 4, D], BF16) for j in range(2)]
        ot = core.sb("otA", [128, D], F32)
        sqA = core.sb("sqA", [128, 512], F32)
        mc = core.sb("mcA", [128, 6], F32)
        x2st = [core.sb(f"x2stA{j}", [128, 128], F32) for j in range(2)]
        x3st = [core.sb(f"x3stA{j}", [128, 128], F32) for j in range(2)]
        psS = [core.psum(f"psSA{j}") for j in range(2)]
        psO = core.psum("psOA", cols=1024)
        psB1 = core.psum("psB1")
        P.op("vector", v_memset(kmax[:], 1e-30), writes=[("kmax",)], tag="km0")
        for i in range(16):
            nkb = 2 * i + 2
            P.dma("sync", knrow[:], knfull_d[0:1, (nkb - 2) * 512:nkb * 512],
                  reads=[("knrow",)], writes=[("knrow",)], tag="ldkn")
            P.op("vector", v_red(kmax[:, 1:2], knrow[:], "max"),
                 reads=[("knrow",), ("kmax",)], writes=[("kmax",)], tag="km1")
            P.op("vector", v_tt(kmax[:, 0:1], kmax[:, 0:1], kmax[:, 1:2], ALU.max),
                 reads=[("kmax",)], writes=[("kmax",)], tag="km2")
            P.op("vector", v_ts(brow[:], qnrow[0:1, i * 128:(i + 1) * 128],
                                kmax[0:1, 0:1], None, op0=ALU.mult),
                 reads=[("qnr",), ("kmax",), ("brow",)], writes=[("brow",)], tag="b1")
            P.op("scalar", s_act(brow[:], brow[:], AF.Ln, bias=1e-30),
                 reads=[("brow",)], writes=[("brow",)], tag="b2")
            P.op("scalar", s_act(brow[:], brow[:], AF.Exp, scale=0.5),
                 reads=[("brow",)], writes=[("brow",)], tag="b3")
            P.op("vector", v_ts(brow[:], brow[:], -32.0, None, op0=ALU.mult),
                 reads=[("brow",)], writes=[("brow",)], tag="b4")
            P.op("tensor", t_transpose(psB1[:, 0:1], brow[:],
                                       consts["ones1"][0:1, 0:1]),
                 reads=[("brow",), ("c", "ones1")], writes=[("psB1",)], tag="b5")
            P.op("vector", v_copy(bcol[:], psB1[:, 0:1]),
                 reads=[("psB1",), ("bcol",)], writes=[("bcol",)], tag="b6")
            P.op("vector", v_memset(lacc[:, 0:1], 0.0),
                 reads=[("lacc",)], writes=[("lacc",)], tag="l0")
            for kt in range(nkb):
                j = kt % 2
                P.dma("sync", ktb[j][:], kfull_d[:, :, kt * 512:(kt + 1) * 512],
                      reads=[("ktb", j)], writes=[("ktb", j)], tag="ldk")
                for tt in range(4):
                    P.dma("scalar", vtb[j][:, tt, :],
                          vfull_d[kt * 512 + tt * 128:kt * 512 + (tt + 1) * 128, :],
                          reads=[("vtb", j)], writes=[("vtb", j)], tag="ldv")
                ps = psS[j]
                pskey = ("psSA", j)
                for dc in range(DC):
                    P.op("tensor", t_matmul(ps[:, 0:512], qb[:, dc, i * 128:(i + 1) * 128],
                                            ktb[j][:, dc, :], start=dc == 0,
                                            stop=dc == DC - 1 and kt < nkb - 2),
                         reads=[("qb", dc), ("ktb", j)], writes=[pskey], tag="s_mm")
                if kt >= nkb - 2:
                    moff = 4 * (kt - (nkb - 2))
                    for mm_ in range(4):
                        P.op("tensor", t_matmul(ps[:, mm_ * 128:(mm_ + 1) * 128],
                                                sc["mdiag2"][:, moff + mm_, :],
                                                consts["identb"],
                                                start=False,
                                                stop=mm_ == 3),
                             reads=[("c", "mdiag2"), ("c", "identb")], writes=[pskey],
                             tag="s_md")
                P.op("scalar", s_act(pb[:], ps[:, 0:512], AF.Exp,
                                     bias=bcol[:], scale=0.03125,
                                     accum=mc[:, 0:1]),
                     reads=[pskey, ("bcol",), ("pbA",), ("mcA",)],
                     writes=[("pbA",), ("mcA",)], tag="s_exp")
                P.op("vector", v_tt(lacc[:, 0:1], lacc[:, 0:1], mc[:, 0:1], ALU.add),
                     reads=[("lacc",), ("mcA",)], writes=[("lacc",)], tag="s_l")
                for kc in range(4):
                    P.dma_tr("scalar" if kc % 2 else "sync",
                             ptb[:, kc, :], pb[:, kc * 128:(kc + 1) * 128],
                             reads=[("pbA",)], writes=[("ptA", kc)], tag="s_tr")
                for kc in range(4):
                    for half in range(2):
                        P.op("tensor", t_matmul(psO[:, half * 512:(half + 1) * 512],
                                                ptb[:, kc, :],
                                                vtb[j][:, kc, half * 512:(half + 1) * 512],
                                                start=(kt == 0 and kc == 0),
                                                stop=(kt == nkb - 1 and kc == 3)),
                             reads=[("ptA", kc), ("vtb", j)], writes=[("psOA",)],
                             tag="pv_mm")
            P.op("vector", v_recip(lacc[:, 1:2], lacc[:, 0:1]),
                 reads=[("lacc",)], writes=[("lacc",)], tag="o_rec")
            for half in range(2):
                P.op("vector", v_ts(ot[:, half * 512:(half + 1) * 512],
                                    psO[:, half * 512:(half + 1) * 512],
                                    lacc[:, 1:2], None, op0=ALU.mult),
                     reads=[("psOA",), ("lacc",), ("ot",)], writes=[("ot",)],
                     tag="o_nrm")
            P.op("vector", v_red(mc[:, 1:2], ot[:], "sum"),
                 reads=[("ot",), ("mcA",)], writes=[("mcA",)], tag="ln_s")
            P.op("vector", v_ts(mc[:, 2:3], mc[:, 1:2], -1.0 / D, None, op0=ALU.mult),
                 reads=[("mcA",)], writes=[("mcA",)], tag="ln_m")
            P.op("scalar", s_act(pb.bitcast(F32)[:, 0:256] if False else ot[:, 0:1],
                                 ot[:, 0:1], AF.Square) if False else
                 s_act(mc[:, 3:4], mc[:, 3:4], AF.Square) if False else
                 s_act(mc[:, 3:4], mc[:, 2:3], AF.Square),
                 reads=[("mcA",)], writes=[("mcA",)], tag="ln_m2")
            P.op("scalar", s_act(sqA[:], ot[:, 0:512], AF.Square,
                                 accum=mc[:, 4:5]),
                 reads=[("ot",), ("sqA",), ("mcA",)], writes=[("sqA",), ("mcA",)],
                 tag="ln_q1")
            P.op("scalar", s_act(sqA[:], ot[:, 512:1024], AF.Square,
                                 accum=mc[:, 5:6]),
                 reads=[("ot",), ("sqA",), ("mcA",)], writes=[("sqA",), ("mcA",)],
                 tag="ln_q2")
            P.op("vector", v_tt(mc[:, 4:5], mc[:, 4:5], mc[:, 5:6], ALU.add),
                 reads=[("mcA",)], writes=[("mcA",)], tag="ln_q3")
            P.op("vector", v_ts(mc[:, 4:5], mc[:, 4:5], 1.0 / D, None, op0=ALU.mult),
                 reads=[("mcA",)], writes=[("mcA",)], tag="ln_q4")
            P.op("vector", v_tt(mc[:, 4:5], mc[:, 4:5], mc[:, 3:4], ALU.subtract),
                 reads=[("mcA",)], writes=[("mcA",)], tag="ln_v")
            P.op("scalar", s_act(mc[:, 4:5], mc[:, 4:5], AF.Ln, bias=1e-5),
                 reads=[("mcA",)], writes=[("mcA",)], tag="ln_l")
            P.op("scalar", s_act(mc[:, 4:5], mc[:, 4:5], AF.Exp, scale=-0.5),
                 reads=[("mcA",)], writes=[("mcA",)], tag="ln_e")
            P.op("vector", v_ts(ot[:], ot[:], mc[:, 2:3], None, op0=ALU.add),
                 reads=[("ot",), ("mcA",)], writes=[("ot",)], tag="ln_a1")
            P.op("vector", v_ts(ot[:], ot[:], mc[:, 4:5], None, op0=ALU.mult),
                 reads=[("ot",), ("mcA",)], writes=[("ot",)], tag="ln_a2")
            for dc in range(DC):
                j = dc % 2
                P.op("tensor", t_transpose(psB1[:, 0:128], ot[:, dc * 128:(dc + 1) * 128],
                                           sc["identf"][:]),
                     reads=[("ot",), ("c", "identf")], writes=[("psB1",)], tag="tr")
                P.dma("sync", x2st[j][:], x2p[dc * 128:(dc + 1) * 128, i * 128:(i + 1) * 128],
                      reads=[("x2stA", j)], writes=[("x2stA", j)], tag="ldx2")
                P.op("vector", v_tt(x3st[j][:], x2st[j][:], psB1[:, 0:128], ALU.add),
                     reads=[("x2stA", j), ("psB1",), ("x3stA", j)],
                     writes=[("x3stA", j)], tag="x3")
                P.dma("gpsimd", x3f[dc * 128:(dc + 1) * 128, i * 128:(i + 1) * 128],
                      x3st[j][:], reads=[("x3stA", j)], writes=[("x3o", dc, i)],
                      tag="st_x3")
    return core


def a1_host_inputs(inputs, consts_h, x2_f, qf_c, kf_c, vt_c, qn_c, kn_c):
    kfull = np.concatenate(kf_c, axis=2)
    vfull = np.concatenate(vt_c, axis=0)
    knfull = np.concatenate(kn_c, axis=1)
    qfull = np.concatenate(qf_c, axis=2)
    qnfull = np.concatenate(qn_c, axis=1)
    maps = []
    for c in range(NCORES):
        tiles = [8 * i + c for i in range(16)]
        cols = np.concatenate([np.arange(t * 128, (t + 1) * 128) for t in tiles])
        md = np.zeros((128, D, 1), np.float32)
        md2 = np.zeros((128, DC, 128), np.float32)
        for m in range(DC):
            for pp in range(128):
                for q in []:
                    pass
        pidx = np.arange(128)
        for m in range(DC):
            mask = (128 * m + pidx[:, None]) > (128 * c + pidx[None, :])
            md2[:, m, :] = np.where(mask, NEG, 0.0)
        m = dict(consts_h)
        m.update({
            "qfi": np.ascontiguousarray(qfull[:, :, cols]),
            "kfull": kfull, "vfull": vfull,
            "qni": np.ascontiguousarray(qnfull[:, cols]),
            "knfull": knfull,
            "x2p": np.ascontiguousarray(x2_f[:, cols]),
            "identf": np.eye(128, dtype=np.float32),
            "mdiag2": md2.astype(BF),
        })
        maps.append(m)
    return maps


def a1_reassemble(results):
    x3 = np.zeros((D, S), np.float32)
    for c in range(NCORES):
        tiles = [8 * i + c for i in range(16)]
        cols = np.concatenate([np.arange(t * 128, (t + 1) * 128) for t in tiles])
        x3[:, cols] = results[c]["x3f"]
    return x3


ECAP = 2926
ECAPP = 2944
ESLOTS = 7 * ECAPP
ESLOTS_PAD = 24 * 128 * 8


def build_moe_route():
    core = Core("moeroute")
    P = core.P
    x3p = core.inp("x3p", [D, TPC], F32)
    wr_d = core.inp("wr", [D, 8], F32)
    lng_d = core.inp("lngc", [128, DC], F32)
    lnb_d = core.inp("lnbc", [128, DC], F32)
    iota8_d = core.inp("iota8", [128, 8], F32)
    iotaInv_d = core.inp("iotaInv", [128, 8], F32)
    zero8_d = core.inp("zero8", [128, 8], F32)
    cneg8_d = core.inp("cneg8", [128, 8], F32)
    ht_o = core.out("ht", [TPC, D], BF16)
    hb_o = core.out("hb", [128, DC, TPC], BF16)
    route_o = core.out("route", [128, 16, 4], F32)
    consts = dev_consts(core)
    sc = {}
    for k, dd in (("iota8", iota8_d), ("iotaInv", iotaInv_d), ("zero8", zero8_d),
                  ("cneg8", cneg8_d)):
        sc[k] = core.sb_right(f"sc_{k}", [128, 8], F32)
        P.dma("sync", sc[k][:], dd, writes=[("c", k)], tag=f"ld_{k}")
    lngc = core.sb_right("lngc_sb", [128, DC], F32)
    P.dma("sync", lngc[:], lng_d, writes=[("lng",)], tag="ld_lng")
    lnbc = core.sb_right("lnbc_sb", [128, DC], F32)
    P.dma("sync", lnbc[:], lnb_d, writes=[("lnb",)], tag="ld_lnb")
    wrs = core.sb_right("wr_sb", [128, DC, 8], F32)
    P.dma("sync", wrs[:], wr_d.rearrange("(c p) e -> p c e", p=128),
          writes=[("wr",)], tag="ld_wr")
    P.barrier()
    with core.phase():
        xf3 = core.sb("xf3", [128, DC, TPC], F32)
        xb3 = core.sb("xb3", [128, DC, TPC], BF16)
        for dc in range(DC):
            for g in range(4):
                P.dma("scalar", xf3[:, dc, g * 512:(g + 1) * 512],
                      x3p[dc * 128:(dc + 1) * 128, g * 512:(g + 1) * 512],
                      reads=[("xf3", dc)], writes=[("xf3", dc)], tag="ldx3")
            P.op("vector", v_copy(xb3[:, dc, :], xf3[:, dc, :]),
                 reads=[("xf3", dc), ("xb3", dc)], writes=[("xb3", dc)], tag="cvx3")
        with core.phase():
            rows = emit_ln_stats_dram(core, lambda d_, a, b: xb3[:, d_, a:b],
                                      lambda d_: ("xb3", d_), TPC, consts, "lnr",
                                      xdt=BF16, compose=True)
        psA = core.psum("psAr")
        psB = core.psum("psBr")
        psL = core.psum("psLr")
        psT = core.psum("psTr")
        psH = core.psum("psHr", dtype=BF16, cols=1024)
        h32 = core.sb("h32", [128, DC, 512], F32)
        lgs = core.sb("lgs", [8, 512], F32)
        lgt = core.sb("lgt", [128, 8], F32)
        colA = core.sb("colA", [128, 8], F32)
        colB = core.sb("colB", [128, 8], F32)
        rsb = core.sb("rsb", [128, 16, 4], F32)
        hts = [core.sb(f"hts{j}", [128, D], BF16) for j in range(2)]
        for g in range(4):
            t0 = g * 512
            emit_ln_apply_dram(core, lambda d_, a, b: xf3[:, d_, a:b],
                               lambda d_: ("xf3", d_), rows, 0, 2, consts,
                               lambda d_: h32[:, d_, :], lambda d_: ("h32", d_),
                               t0, t0 + 512, psA, ("psAr",), psB, ("psBr",),
                               gcol_fn=lambda d_: lngc[:, d_:d_ + 1],
                               bcol_fn=lambda d_: lnbc[:, d_:d_ + 1],
                               gbkeys=[("lng",), ("lnb",)])
            for dc in range(DC):
                P.op("scalar", s_copy(xb3[:, dc, t0:t0 + 512], h32[:, dc, :]),
                     reads=[("h32", dc), ("xb3", dc)], writes=[("xb3", dc)], tag="hb_cv")
                P.dma("gpsimd", hb_o[:, dc, t0:t0 + 512], xb3[:, dc, t0:t0 + 512],
                      reads=[("xb3", dc)], writes=[("hbo", dc, g)], tag="st_hb")
                P.op("tensor", t_matmul(psL[0:8, 0:512], wrs[:, dc, :], h32[:, dc, :],
                                        start=dc == 0, stop=dc == DC - 1),
                     reads=[("wr",), ("h32", dc)], writes=[("psLr",)], tag="lg_mm")
            P.op("scalar", s_copy(lgs[:], psL[0:8, 0:512]),
                 reads=[("psLr",), ("lgs",)], writes=[("lgs",)], tag="lg_ev")
            for tt in range(4):
                j = tt % 2
                for dc in range(DC):
                    P.op("tensor", t_transpose(psH[:, dc * 128:(dc + 1) * 128],
                                               xb3[:, dc, t0 + tt * 128:t0 + (tt + 1) * 128],
                                               consts["identb"]),
                         reads=[("xb3", dc), ("c", "identb")], writes=[("psHr",)],
                         tag="ht_tr")
                P.op("vector", v_copy(hts[j][:], psH[:, 0:1024]),
                     reads=[("psHr",), ("hts", j)], writes=[("hts", j)], tag="ht_ev")
                P.dma("gpsimd", ht_o[t0 + tt * 128:t0 + (tt + 1) * 128, :], hts[j][:],
                      reads=[("hts", j)], writes=[("hto", g, tt)], tag="st_ht")
            for tt in range(4):
                qt = g * 4 + tt
                P.op("tensor", t_transpose(psT[:, 0:8], lgs[:, tt * 128:(tt + 1) * 128],
                                           consts["identf8"]),
                     reads=[("lgs",), ("c", "identf8")], writes=[("psTr",)], tag="tp_tr")
                P.op("vector", v_copy(lgt[:], psT[:, 0:8]),
                     reads=[("psTr",), ("lgt",)], writes=[("lgt",)], tag="tp_ev")
                P.op("vector", v_memset(lgt[:, 7:8], -3e38),
                     reads=[("lgt",)], writes=[("lgt",)], tag="tp_p7")
                P.op("vector", v_red(rsb[:, qt, 0:1] if False else colA[:, 0:1], lgt[:], "max"),
                     reads=[("lgt",), ("colA",)], writes=[("colA",)], tag="t_m1")
                P.op("vector", v_ts(colB[:], lgt[:], colA[:, 0:1], None, op0=ALU.is_equal),
                     reads=[("lgt",), ("colA",), ("colB",)], writes=[("colB",)], tag="t_e1")
                P.op("vector", v_tt(colB[:], colB[:], sc["iotaInv"][:], ALU.mult),
                     reads=[("colB",), ("c", "iotaInv")],
                     writes=[("colB",)], tag="t_c1")
                P.op("vector", v_red(colA[:, 1:2], colB[:], "max"),
                     reads=[("colB",), ("colA",)], writes=[("colA",)], tag="t_i1m")
                P.op("vector", v_ts(colA[:, 2:3], colA[:, 1:2], -1.0, 999.0,
                                    op0=ALU.mult, op1=ALU.add),
                     reads=[("colA",)], writes=[("colA",)], tag="t_i1")
                P.op("vector", v_ts(colB[:], sc["iota8"][:], colA[:, 2:3], None,
                                    op0=ALU.is_equal),
                     reads=[("c", "iota8"), ("colA",), ("colB",)], writes=[("colB",)],
                     tag="t_e2")
                P.op("vector", v_tt(colB[:], colB[:], sc["cneg8"][:], ALU.mult),
                     reads=[("colB",), ("c", "cneg8")], writes=[("colB",)],
                     tag="t_mskm")
                P.op("vector", v_tt(lgt[:], lgt[:], colB[:], ALU.add),
                     reads=[("colB",), ("lgt",)], writes=[("lgt",)],
                     tag="t_msk")
                P.op("vector", v_red(colA[:, 3:4], lgt[:], "max"),
                     reads=[("lgt",), ("colA",)], writes=[("colA",)], tag="t_m2")
                P.op("vector", v_ts(colB[:], lgt[:], colA[:, 3:4], None, op0=ALU.is_equal),
                     reads=[("lgt",), ("colA",), ("colB",)], writes=[("colB",)], tag="t_e3")
                P.op("vector", v_tt(colB[:], colB[:], sc["iotaInv"][:], ALU.mult),
                     reads=[("colB",), ("c", "iotaInv")],
                     writes=[("colB",)], tag="t_c2")
                P.op("vector", v_red(colA[:, 4:5], colB[:], "max"),
                     reads=[("colB",), ("colA",)], writes=[("colA",)], tag="t_i2m")
                P.op("vector", v_ts(colA[:, 5:6], colA[:, 4:5], -1.0, 999.0,
                                    op0=ALU.mult, op1=ALU.add),
                     reads=[("colA",)], writes=[("colA",)], tag="t_i2")
                P.op("vector", v_tt(colA[:, 6:7], colA[:, 3:4], colA[:, 0:1],
                                    ALU.subtract),
                     reads=[("colA",)], writes=[("colA",)], tag="t_d")
                P.op("scalar", s_act(colA[:, 6:7], colA[:, 6:7], AF.Exp),
                     reads=[("colA",)], writes=[("colA",)], tag="t_ex")
                P.op("vector", v_ts(colA[:, 7:8], colA[:, 6:7], 1.0, None, op0=ALU.add),
                     reads=[("colA",)], writes=[("colA",)], tag="t_d1")
                P.op("vector", v_recip(colA[:, 7:8], colA[:, 7:8]),
                     reads=[("colA",)], writes=[("colA",)], tag="t_g1")
                P.op("vector", v_tt(colA[:, 6:7], colA[:, 6:7], colA[:, 7:8], ALU.mult),
                     reads=[("colA",)], writes=[("colA",)], tag="t_g2")
                for (srci, dsti) in ((2, 0), (5, 1), (7, 2), (6, 3)):
                    P.op("vector", v_copy(rsb[:, qt, dsti:dsti + 1],
                                          colA[:, srci:srci + 1]),
                         reads=[("colA",), ("rsb",)], writes=[("rsb",)], tag="t_st")
        P.dma("gpsimd", route_o, rsb[:], reads=[("rsb",)], writes=[("rto",)], tag="st_rt")
    return core


def build_moe_plan():
    core = Core("moeplan")
    P = core.P
    i1t_d = core.inp("i1t", [128, 128], F32)
    i2t_d = core.inp("i2t", [128, 128], F32)
    g1t_d = core.inp("g1t", [128, 128], F32)
    g2t_d = core.inp("g2t", [128, 128], F32)
    triu_d = core.inp("triu", [128, 128], BF16)
    iota8_d = core.inp("iota8", [128, 8], F32)
    ones8b_d = core.inp("ones8b", [128, 8], BF16)
    iota128_d = core.inp("iota128", [128, 1], F32)
    cbig_d = core.inp("cbig", [128, 1], F32)
    eslot_o = core.out("eslot", [128, 128, 2], I32)
    geff_o = core.out("geff", [128, 128, 2], F32)
    consts = dev_consts(core)
    sc = {}
    for k, dd, shape, dt in (("i1t", i1t_d, [128, 128], F32), ("i2t", i2t_d, [128, 128], F32),
                             ("g1t", g1t_d, [128, 128], F32), ("g2t", g2t_d, [128, 128], F32),
                             ("triu", triu_d, [128, 128], BF16),
                             ("iota8", iota8_d, [128, 8], F32),
                             ("ones8b", ones8b_d, [128, 8], BF16),
                             ("iota128", iota128_d, [128, 1], F32),
                             ("cbig", cbig_d, [128, 1], F32)):
        sc[k] = core.sb_right(f"sc_{k}", shape, dt)
        P.dma("sync", sc[k][:], dd, writes=[("c", k)], tag=f"ld_{k}")
    P.barrier()
    with core.phase():
        running = core.sb("running", [1, 8], F32)
        P.op("vector", v_memset(running[:], 0.0), writes=[("run",)], tag="r0")
        oh = core.sb("oh", [128, 8], F32)
        ohb = core.sb("ohb", [128, 8], BF16)
        e1 = core.sb("e1", [128, 8], F32)
        cA = core.sb("cA", [128, 8], F32)
        st8 = core.sb("st8", [8, 8], F32)
        esl = core.sb("esl", [128, 128, 2], I32)
        gef = core.sb("gef", [128, 128, 2], F32)
        tki = core.sb("tki", [128, 2], I32)
        sli = core.sb("sli", [128, 1], I32)
        tkf = core.sb("tkf", [128, 1], F32)
        psC = core.psum("psC")
        psB2 = core.psum("psB2")
        psT = core.psum("psTp")
        for k in range(2):
            it = sc["i1t"] if k == 0 else sc["i2t"]
            gt = sc["g1t"] if k == 0 else sc["g2t"]
            ikey = ("c", "i1t" if k == 0 else "i2t")
            gkey = ("c", "g1t" if k == 0 else "g2t")
            for tau in range(128):
                icol = it[:, tau:tau + 1]
                P.op("vector", v_ts(oh[:], sc["iota8"][:], icol, None, op0=ALU.is_equal),
                     reads=[("c", "iota8"), ikey, ("oh",)], writes=[("oh",)], tag="p_oh")
                P.op("vector", v_copy(ohb[:], oh[:]),
                     reads=[("oh",), ("ohb",)], writes=[("ohb",)], tag="p_ohb")
                P.op("tensor", t_matmul(psC[:, 0:8], sc["triu"][:], ohb[:],
                                        start=True, stop=True),
                     reads=[("ohb",), ("c", "triu")], writes=[("psC",)], tag="p_cum")
                P.op("tensor", t_matmul(psB2[:, 0:8], consts["ones1"], running[:],
                                        start=True, stop=True),
                     reads=[("run",), ("c", "ones1")], writes=[("psB2",)], tag="p_bc")
                P.op("vector", v_tt(e1[:], psC[:, 0:8], oh[:], ALU.mult),
                     reads=[("psC",), ("oh",), ("e1",)], writes=[("e1",)], tag="p_x1")
                P.op("vector", v_red(cA[:, 0:1], e1[:], "sum"),
                     reads=[("e1",), ("cA",)], writes=[("cA",)], tag="p_cm")
                P.op("vector", v_tt(e1[:], psB2[:, 0:8], oh[:], ALU.mult),
                     reads=[("psB2",), ("oh",), ("e1",)], writes=[("e1",)], tag="p_x2")
                P.op("vector", v_red(cA[:, 1:2], e1[:], "sum"),
                     reads=[("e1",), ("cA",)], writes=[("cA",)], tag="p_bs")
                P.op("vector", v_tt(cA[:, 2:3], cA[:, 0:1], cA[:, 1:2], ALU.add),
                     reads=[("cA",)], writes=[("cA",)], tag="p_p1")
                P.op("vector", v_ts(cA[:, 2:3], cA[:, 2:3], -1.0, None, op0=ALU.add),
                     reads=[("cA",)], writes=[("cA",)], tag="p_p2")
                P.op("vector", v_ts(cA[:, 3:4], cA[:, 2:3], float(ECAP), None,
                                    op0=ALU.is_lt),
                     reads=[("cA",)], writes=[("cA",)], tag="p_kp")
                P.op("vector", v_ts(cA[:, 4:5], icol, float(ECAPP), None, op0=ALU.mult),
                     reads=[ikey, ("cA",)], writes=[("cA",)], tag="p_s1")
                P.op("vector", v_tt(cA[:, 4:5], cA[:, 4:5], cA[:, 2:3], ALU.add),
                     reads=[("cA",)], writes=[("cA",)], tag="p_s2")
                P.op("vector", v_tt(cA[:, 4:5], cA[:, 4:5], cA[:, 3:4], ALU.mult),
                     reads=[("cA",)], writes=[("cA",)], tag="p_s3a")
                P.op("vector", v_ts(cA[:, 6:7], cA[:, 3:4], -1e9, 1e9,
                                    op0=ALU.mult, op1=ALU.add),
                     reads=[("cA",)], writes=[("cA",)], tag="p_s3b")
                P.op("vector", v_tt(cA[:, 4:5], cA[:, 4:5], cA[:, 6:7], ALU.add),
                     reads=[("cA",)], writes=[("cA",)], tag="p_s3")
                P.op("vector", v_tt(cA[:, 5:6], gt[:, tau:tau + 1], cA[:, 3:4], ALU.mult),
                     reads=[gkey, ("cA",)], writes=[("cA",)], tag="p_ge")
                P.op("vector", v_copy(esl[:, tau, k:k + 1], cA[:, 4:5]),
                     reads=[("cA",), ("esl",)], writes=[("esl",)], tag="p_se")
                P.op("vector", v_copy(gef[:, tau, k:k + 1], cA[:, 5:6]),
                     reads=[("cA",), ("gef",)], writes=[("gef",)], tag="p_sg")
                P.op("tensor", t_matmul(psT[0:8, 0:8], ohb[:], sc["ones8b"][:],
                                        start=True, stop=True),
                     reads=[("ohb",), ("c", "ones8b")], writes=[("psTp",)], tag="p_tt")
                P.op("vector", v_copy(st8[:], psT[0:8, 0:8]),
                     reads=[("psTp",), ("st8",)], writes=[("st8",)], tag="p_tte")
                P.op("tensor", t_transpose(psT[0:1, 0:8], st8[:, 0:1], consts["identf8"]),
                     reads=[("st8",), ("c", "identf8")], writes=[("psTp",)], tag="p_ttr")
                P.op("vector", v_tt(running[:], running[:], psT[0:1, 0:8], ALU.add),
                     reads=[("run",), ("psTp",)], writes=[("run",)], tag="p_ru")
        P.dma("gpsimd", eslot_o, esl[:], reads=[("esl",)], writes=[("eo",)], tag="st_es")
        P.dma("gpsimd", geff_o, gef[:], reads=[("gef",)], writes=[("go",)], tag="st_ge")
    return core


def build_moe_expert():
    core = Core("moeexp")
    P = core.P
    ht_d = core.inp("htf", [S, D], BF16)
    stok_d = core.inp("stoki", [ECAPP, 1], I32)
    w1e_d = core.inp("w1e", [D, 4 * D], F32)
    w3e_d = core.inp("w3e", [D, 4 * D], F32)
    w2e_d = core.inp("w2e", [4 * D, D], F32)
    w1s_d = core.inp("w1s", [D, 4 * D], F32)
    w3s_d = core.inp("w3s", [D, 4 * D], F32)
    w2s_d = core.inp("w2s", [4 * D, D], F32)
    hb_d = core.inp("hbi", [128, DC, TPC], BF16)
    ye_o = core.out("ye", [ECAPP, D], BF16)
    ysh_o = core.out("ysh", [D, TPC], F32)
    consts = dev_consts(core)

    def ffn_pass(tag, ngroups, tiles_of, w1d, w3d, w2d, fill_x, emit_out):
        with core.phase():
            wl = WeightLoader(core, stage_cols=1024, consts=consts)
            xf = core.sb(f"xf_{tag}", [128, DC, 512], BF16)
            tb = core.sb(f"tb_{tag}", [128, 32, 512], BF16)
            wsl = [core.sb(f"wsl_{tag}{j}", [128, DC, 128], BF16) for j in range(2)]
            w2sl = [core.sb(f"w2sl_{tag}{j}", [128, 32, 128], BF16) for j in range(2)]
            sg = core.sb(f"sg_{tag}", [128, 512], BF16)
            psM = [core.psum(f"psM_{tag}{j}") for j in range(2)]
            psW = core.psum(f"psW_{tag}")
            aux = dict(psT=core.psum(f"psT_{tag}", dtype=BF16),
                       gb=core.sb(f"gb_{tag}", [128, D], BF16),
                       tst=[core.sb(f"tst_{tag}{j}", [128, 512], BF16) for j in range(2)],
                       fst=[core.sb(f"fst_{tag}{j}", [128, 512], F32) for j in range(2)],
                       idx=core.sb(f"idx_{tag}", [128, 1], I32))
            for g in range(ngroups):
                nt = tiles_of[g]
                wid = nt * 128
                fill_x(g, nt, xf, ("xf", tag), aux)
                for oc in range(32):
                    j = oc % 2
                    st = wl.stage[j]
                    for wd, wnm in ((w1d, "w1"), (w3d, "w3")):
                        P.dma("sync", st[:, 0:1024].rearrange("p (a b) -> p a b", b=128),
                              wd[:, oc * 128:(oc + 1) * 128].rearrange("(a p) b -> p a b", p=128),
                              reads=[("wstage", j)], writes=[("wstage", j)],
                              tag=f"ld_{tag}w")
                        psx, pk = ((psM[j], ("psM", tag, j)) if wnm == "w1"
                                   else (psW, ("psW", tag)))
                        P.op("scalar", s_copy(wsl[j][:].rearrange("p c k -> p (c k)"),
                                              st[:, 0:1024]),
                             reads=[("wstage", j)], writes=[("wsl", tag, j)],
                             tag=f"cv_{tag}w")
                        for ki in range(DC):
                            P.op("tensor", t_matmul(psx[:, 0:wid], wsl[j][:, ki, :],
                                                    xf[:, ki, 0:wid],
                                                    start=ki == 0, stop=ki == DC - 1),
                                 reads=[("wsl", tag, j), ("xf", tag)], writes=[pk],
                                 tag=f"{tag}_mm")
                    P.op("scalar", s_act(sg[:, 0:wid], psM[j][:, 0:wid], AF.Silu),
                         reads=[("psM", tag, j), ("sg", tag)], writes=[("sg", tag)],
                         tag=f"{tag}_silu")
                    P.op("vector", v_tt(tb[:, oc, 0:wid], psW[:, 0:wid], sg[:, 0:wid],
                                        ALU.mult),
                         reads=[("psW", tag), ("sg", tag), ("tb", tag, oc)],
                         writes=[("tb", tag, oc)], tag=f"{tag}_t")
                for do in range(DC):
                    j = do % 2
                    st2 = wl.stage[j]
                    for rc in range(4):
                        P.dma("sync", st2[:, 0:1024].rearrange("p (a b) -> p a b", a=8),
                              w2d[rc * 1024:(rc + 1) * 1024, do * 128:(do + 1) * 128]
                              .rearrange("(a p) b -> p a b", p=128),
                              reads=[("wstage", j)], writes=[("wstage", j)],
                              tag=f"ld_{tag}w2")
                        P.op("scalar", s_copy(w2sl[j][:, rc * 8:(rc + 1) * 8, :],
                                              st2[:, 0:1024].rearrange("p (a b) -> p a b", a=8)),
                             reads=[("wstage", j)], writes=[("w2sl", tag, j)],
                             tag=f"cv_{tag}w2")
                    for hg in range(32):
                        P.op("tensor", t_matmul(psM[j][:, 0:wid], w2sl[j][:, hg, :],
                                                tb[:, hg, 0:wid], start=hg == 0,
                                                stop=hg == 31),
                             reads=[("w2sl", tag, j), ("tb", tag, hg)],
                             writes=[("psM", tag, j)], tag=f"{tag}_m2")
                    emit_out(g, nt, do, psM[j], ("psM", tag, j), aux, j)

    groups_rt = [4, 4, 4, 4, 4, 3]

    def fill_rt(g, nt, xf, xkey, aux):
        base = sum(groups_rt[:g])
        for t in range(nt):
            P.dma("sync", aux["idx"][:], stok_d[(base + t) * 128:(base + t + 1) * 128, :],
                  reads=[("idxk",)], writes=[("idxk",)], tag="ld_idx")
            P.op("vector", v_memset(aux["gb"][:], 0.0),
                 reads=[("gbk",)], writes=[("gbk",)], tag="gz")
            P.op("gpsimd", (lambda eng, o=aux["gb"][:], inap=ht_d,
                            ix=aux["idx"][:]:
                            eng.indirect_dma_start(o, None, inap,
                                                   IndirectOffsetOnAxis(ix, 0),
                                                   bounds_check=S - 1,
                                                   oob_is_err=False)),
                 reads=[("idxk",), ("gbk",)], writes=[("gbk",)], dma=True, tag="gth")
            for dc in range(DC):
                P.op("tensor", t_transpose(aux["psT"][:, 0:128],
                                           aux["gb"][:, dc * 128:(dc + 1) * 128],
                                           consts["identb"]),
                     reads=[("gbk",), ("c", "identb")], writes=[("psTk",)], tag="gtr")
                P.op("vector", v_copy(xf[:, dc, t * 128:(t + 1) * 128],
                                      aux["psT"][:, 0:128]),
                     reads=[("psTk",), xkey], writes=[xkey], tag="gte")

    def emit_rt(g, nt, do, ps, pskey, aux, j):
        base = sum(groups_rt[:g])
        P.op("scalar", s_copy(aux["tst"][j][:, 0:nt * 128], ps[:, 0:nt * 128]),
             reads=[pskey, ("tstk", j)], writes=[("tstk", j)], tag="ye_ev")
        for t in range(nt):
            P.op("tensor", t_transpose(aux["psT"][:, 0:128],
                                       aux["tst"][j][:, t * 128:(t + 1) * 128],
                                       consts["identb"]),
                 reads=[("tstk", j), ("c", "identb")], writes=[("psTk",)], tag="ye_tr")
            P.op("vector", v_copy(aux["gb"][:, 0:128], aux["psT"][:, 0:128]),
                 reads=[("psTk",), ("gbk",)], writes=[("gbk",)], tag="ye_te")
            P.dma("gpsimd", ye_o[(base + t) * 128:(base + t + 1) * 128,
                                 do * 128:(do + 1) * 128], aux["gb"][:, 0:128],
                  reads=[("gbk",)], writes=[("yeo", base + t, do)], tag="st_ye")

    def fill_sh(g, nt, xf, xkey, aux):
        for dc in range(DC):
            P.dma("sync", xf[:, dc, :], hb_d[:, dc, g * 512:(g + 1) * 512],
                  reads=[xkey], writes=[xkey], tag="ld_hb")

    def emit_sh(g, nt, do, ps, pskey, aux, j):
        P.op("scalar", s_copy(aux["fst"][j][:], ps[:, 0:512]),
             reads=[pskey, ("fstk", j)], writes=[("fstk", j)], tag="sh_ev")
        P.dma("gpsimd", ysh_o[do * 128:(do + 1) * 128, g * 512:(g + 1) * 512],
              aux["fst"][j][:], reads=[("fstk", j)], writes=[("ysho", do, g)],
              tag="st_sh")

    ffn_pass("rt", 6, groups_rt, w1e_d, w3e_d, w2e_d, fill_rt, emit_rt)
    ffn_pass("sh", 4, [4, 4, 4, 4], w1s_d, w3s_d, w2s_d, fill_sh, emit_sh)
    return core


def build_moe_fin():
    core = Core("moefin")
    P = core.P
    ye_d = core.inp("yef", [ESLOTS, D], BF16)
    es_d = core.inp("esi", [128, 16, 2], I32)
    ge_d = core.inp("gei", [128, 16, 2], F32)
    ysh_d = core.inp("yshi", [D, TPC], F32)
    x3p = core.inp("x3pf", [D, TPC], F32)
    identf_d = core.inp("identff", [128, 128], F32)
    x4f = core.out("x4f", [D, TPC], F32)
    consts = dev_consts(core)
    idf = core.sb_right("idf", [128, 128], F32)
    P.dma("sync", idf[:], identf_d, writes=[("c", "idf")], tag="ld_idf")
    ess = core.sb_right("ess", [128, 16, 2], I32)
    P.dma("sync", ess[:], es_d, writes=[("c", "es")], tag="ld_es")
    ges = core.sb_right("ges", [128, 16, 2], F32)
    P.dma("sync", ges[:], ge_d, writes=[("c", "ge")], tag="ld_ge")
    P.barrier()
    outf = core.sb_right("outfm", [128, DC, TPC], F32)
    with core.phase():
        gb = [core.sb(f"gbf{j}", [128, D], BF16) for j in range(2)]
        rsum = core.sb("rsum", [128, D], F32)
        yst = [core.sb(f"ystm{j}", [128, 512], F32) for j in range(2)]
        psT = core.psum("psTm")
        for qt in range(16):
            for k in range(2):
                P.op("vector", v_memset(gb[k][:], 0.0),
                     reads=[("gbm", k)], writes=[("gbm", k)], tag="fz")
                P.op("gpsimd", (lambda eng, o=gb[k][:], ix=ess[:, qt, k:k + 1]:
                                eng.indirect_dma_start(o, None, ye_d,
                                                       IndirectOffsetOnAxis(ix, 0),
                                                       bounds_check=ESLOTS - 1,
                                                       oob_is_err=False)),
                     reads=[("c", "es"), ("gbm", k)], writes=[("gbm", k)],
                     dma=True, tag="fgth")
            P.op("vector", v_ts(rsum[:], gb[0][:], ges[:, qt, 0:1], None, op0=ALU.mult),
                 reads=[("gbm", 0), ("c", "ge"), ("rsum",)], writes=[("rsum",)],
                 tag="fs1")
            P.op("vector", v_ts(gb[1][:], gb[1][:], ges[:, qt, 1:2], None,
                                op0=ALU.mult),
                 reads=[("gbm", 1), ("c", "ge")], writes=[("gbm", 1)], tag="fs2")
            P.op("vector", v_tt(rsum[:], rsum[:], gb[1][:], ALU.add),
                 reads=[("rsum",), ("gbm", 1)], writes=[("rsum",)], tag="fs3")
            for dc in range(DC):
                j = dc % 2
                P.op("tensor", t_transpose(psT[:, 0:128], rsum[:, dc * 128:(dc + 1) * 128],
                                           idf[:]),
                     reads=[("rsum",), ("c", "idf")], writes=[("psTm",)], tag="ftr")
                P.dma("scalar", yst[j][:, 0:128],
                      ysh_d[dc * 128:(dc + 1) * 128, qt * 128:(qt + 1) * 128],
                      reads=[("ystm", j)], writes=[("ystm", j)], tag="ld_ysh")
                P.op("vector", v_tt(outf[:, dc, qt * 128:(qt + 1) * 128], yst[j][:, 0:128],
                                    psT[:, 0:128], ALU.add),
                     reads=[("ystm", j), ("psTm",), ("outf", dc)],
                     writes=[("outf", dc)], tag="fadd")
    with core.phase():
        outb = core.sb("outbm", [128, DC, TPC], BF16)
        for dc in range(DC):
            P.op("vector", v_copy(outb[:, dc, :], outf[:, dc, :]),
                 reads=[("outf", dc)], writes=[("outbm", dc)], tag="fcv")
        with core.phase():
            rows = emit_ln_stats_dram(core, lambda d_, a, b: outb[:, d_, a:b],
                                      lambda d_: ("outbm", d_), TPC, consts, "lnm",
                                      xdt=BF16, compose=False)
        psA = core.psum("psAm")
        psB = core.psum("psBm")
        xst = [core.sb(f"xstm{j}", [128, 512], F32) for j in range(2)]
        ost = [core.sb(f"ostm{j}", [128, 512], F32) for j in range(2)]
        for g in range(4):
            t0 = g * 512
            rows.bcast(psA, ("psAm",), 0, t0, t0 + 512, consts)
            rows.bcast(psB, ("psBm",), 1, t0, t0 + 512, consts)
            for dc in range(DC):
                j = (dc + g) % 2
                P.dma("sync", xst[j][:], x3p[dc * 128:(dc + 1) * 128, t0:t0 + 512],
                      reads=[("xstm", j)], writes=[("xstm", j)], tag="ld_x3")
                P.op("vector", v_tt(ost[j][:], outf[:, dc, t0:t0 + 512], psA[:, 0:512],
                                    ALU.add),
                     reads=[("outf", dc), ("psAm",), ("ostm", j)], writes=[("ostm", j)],
                     tag="f_mu")
                P.op("vector", v_tt(ost[j][:], ost[j][:], psB[:, 0:512], ALU.mult),
                     reads=[("ostm", j), ("psBm",)], writes=[("ostm", j)], tag="f_rs")
                P.op("vector", v_tt(xst[j][:], xst[j][:], ost[j][:], ALU.add),
                     reads=[("xstm", j), ("ostm", j)], writes=[("xstm", j)], tag="f_x4")
                P.dma("gpsimd", x4f[dc * 128:(dc + 1) * 128, t0:t0 + 512], xst[j][:],
                      reads=[("xstm", j)], writes=[("x4o", dc, g)], tag="st_x4")
    return core


def moer_host_inputs(inputs, consts_h, x3_f):
    wr = np.zeros((D, 8), np.float32)
    wr[:, 0:7] = np.asarray(inputs["L1_ffn_Wr"])
    iota8 = np.tile(np.arange(8, dtype=np.float32), (128, 1))
    maps = []
    for c in range(NCORES):
        m = dict(consts_h)
        m.update({
            "x3p": np.ascontiguousarray(x3_f[:, c * TPC:(c + 1) * TPC]),
            "wr": wr,
            "lngc": np.ascontiguousarray(np.asarray(inputs["L1_ffn_ln_g"]).reshape(-1, 128).T),
            "lnbc": np.ascontiguousarray(np.asarray(inputs["L1_ffn_ln_b"]).reshape(-1, 128).T),
            "iota8": iota8,
            "iotaInv": 999.0 - iota8,
            "zero8": np.zeros((128, 8), np.float32),
            "cneg8": np.full((128, 8), -1e38, np.float32),
        })
        maps.append(m)
    return maps


def moep_host_inputs(consts_h, route_c):
    i1 = np.concatenate([r[:, :, 0].T.reshape(-1) for r in route_c])
    i2 = np.concatenate([r[:, :, 1].T.reshape(-1) for r in route_c])
    g1 = np.concatenate([r[:, :, 2].T.reshape(-1) for r in route_c])
    g2 = np.concatenate([r[:, :, 3].T.reshape(-1) for r in route_c])

    def tiled(v):
        return np.ascontiguousarray(v.reshape(128, 128).T.astype(np.float32))
    ii, jj = np.mgrid[0:128, 0:128]
    m0 = dict(consts_h)
    m0.update({
        "i1t": tiled(i1), "i2t": tiled(i2), "g1t": tiled(g1), "g2t": tiled(g2),
        "triu": np.where(ii <= jj, 1.0, 0.0).astype(BF),
        "iota8": np.tile(np.arange(8, dtype=np.float32), (128, 1)),
        "ones8b": np.ones((128, 8), BF),
        "iota128": np.arange(128, dtype=np.float32).reshape(128, 1),
        "cbig": np.full((128, 1), 1e9, np.float32),
    })
    return [m0] * NCORES


def moee_host_inputs(inputs, consts_h, ht_full, stok, hb_c):
    W1 = np.asarray(inputs["L1_ffn_W1"])
    W3 = np.asarray(inputs["L1_ffn_W3"])
    W2 = np.asarray(inputs["L1_ffn_W2"])
    maps = []
    for c in range(NCORES):
        if c < 7:
            w1e, w3e, w2e = W1[c], W3[c], W2[c]
        else:
            w1e = np.zeros((D, 4 * D), np.float32)
            w3e = np.zeros((D, 4 * D), np.float32)
            w2e = np.zeros((4 * D, D), np.float32)
        m = dict(consts_h)
        m.update({
            "htf": ht_full,
            "stoki": stok[c * ECAPP:(c + 1) * ECAPP].reshape(ECAPP, 1),
            "w1e": np.ascontiguousarray(w1e), "w3e": np.ascontiguousarray(w3e),
            "w2e": np.ascontiguousarray(w2e),
            "w1s": np.asarray(inputs["L1_ffn_W1s"]),
            "w3s": np.asarray(inputs["L1_ffn_W3s"]),
            "w2s": np.asarray(inputs["L1_ffn_W2s"]),
            "hbi": hb_c[c],
        })
        maps.append(m)
    return maps


def moef_host_inputs(consts_h, ye_full, es, ge, ysh_c, x3_f):
    maps = []
    for c in range(NCORES):
        m = dict(consts_h)
        m.update({
            "yef": ye_full,
            "esi": es[:, c * 16:(c + 1) * 16, :] if es.shape[1] == 128 else es[c],
            "gei": ge[:, c * 16:(c + 1) * 16, :] if ge.shape[1] == 128 else ge[c],
            "yshi": ysh_c[c],
            "x3pf": np.ascontiguousarray(x3_f[:, c * TPC:(c + 1) * TPC]),
            "identff": np.eye(128, dtype=np.float32),
        })
        maps.append(m)
    return maps


_BUILT = {}


def _get(name, builder, *args):
    if name not in _BUILT:
        c = builder(*args)
        _BUILT[name] = (c, c.finish())
    return _BUILT[name][1]


def kernel(**inputs):
    consts_h = host_consts()
    run = lambda nc, maps: run_bass_kernel_spmd(nc, maps, core_ids=list(range(NCORES)))

    nc = _get("l0a", build_l0attn)
    res = run(nc, l0a_host_inputs(inputs, consts_h))
    x1_f = np.zeros((D, S), np.float32)
    for c in range(NCORES):
        o = res.results[c]["x1f"]
        x1_f[:, c * TPC:(c + 1) * TPC] = o.transpose(1, 0, 2).reshape(D, TPC)

    state = x1_f
    for li in range(6):
        first = li == 0
        nc = _get(f"f0_{first}", build_f0pair, first)
        res = run(nc, f0_host_inputs(inputs, consts_h, li, state, first))
        state = np.concatenate([res.results[c]["yout"] for c in range(NCORES)], axis=1)

    nc = _get("f0fin", build_f0fin)
    res = run(nc, f0fin_host_inputs(inputs, consts_h, x1_f, state))
    x2_f = np.concatenate([res.results[c]["x2f"] for c in range(NCORES)], axis=1)
    qf_c = [res.results[c]["qf"] for c in range(NCORES)]
    kf_c = [res.results[c]["kf"] for c in range(NCORES)]
    vt_c = [res.results[c]["vt"] for c in range(NCORES)]
    qn_c = [res.results[c]["qn"] for c in range(NCORES)]
    kn_c = [res.results[c]["kn"] for c in range(NCORES)]

    nc = _get("a1", build_a1attn)
    res = run(nc, a1_host_inputs(inputs, consts_h, x2_f, qf_c, kf_c, vt_c, qn_c, kn_c))
    x3_f = a1_reassemble([res.results[c] for c in range(NCORES)])

    nc = _get("moer", build_moe_route)
    res = run(nc, moer_host_inputs(inputs, consts_h, x3_f))
    route_c = [res.results[c]["route"] for c in range(NCORES)]
    ht_full = np.concatenate([res.results[c]["ht"] for c in range(NCORES)], axis=0)
    hb_c = [res.results[c]["hb"] for c in range(NCORES)]

    nc = _get("moep", build_moe_plan)
    res = run(nc, moep_host_inputs(consts_h, route_c))
    r0 = res.results[0]
    eslot = r0["eslot"]
    geff = r0["geff"]
    nc = _get("moes", build_moe_scat)
    stoki = np.full((ESLOTS_PAD, 2), 2 ** 30, np.int32)
    for ph in range(8):
        maps = []
        tok = (np.arange(128)[:, None] + 128 * (16 * ph + np.arange(16))[None, :]
               ).astype(np.float32)
        m0 = dict(consts_h)
        m0.update({"stin": stoki,
                   "eslsl": np.ascontiguousarray(eslot[:, 16 * ph:16 * ph + 16, :]),
                   "tokb": tok})
        res = run(nc, [m0] * NCORES)
        stoki = res.results[0]["stok"]
    stok = np.ascontiguousarray(stoki[:, 0])

    nc = _get("moee", build_moe_expert)
    res = run(nc, moee_host_inputs(inputs, consts_h, ht_full, stok, hb_c))
    ye_full = np.concatenate([res.results[c]["ye"] for c in range(7)], axis=0)
    ysh_c = [res.results[c]["ysh"] for c in range(NCORES)]

    es_c = [np.ascontiguousarray(eslot[:, c * 16:(c + 1) * 16, :]) for c in range(NCORES)]
    ge_c = [np.ascontiguousarray(geff[:, c * 16:(c + 1) * 16, :]) for c in range(NCORES)]
    nc = _get("moef", build_moe_fin)
    maps = []
    for c in range(NCORES):
        m = dict(consts_h)
        m.update({"yef": ye_full, "esi": es_c[c], "gei": ge_c[c], "yshi": ysh_c[c],
                  "x3pf": np.ascontiguousarray(x3_f[:, c * TPC:(c + 1) * TPC]),
                  "identff": np.eye(128, dtype=np.float32)})
        maps.append(m)
    res = run(nc, maps)
    x4_f = np.concatenate([res.results[c]["x4f"] for c in range(NCORES)], axis=1)
    return np.ascontiguousarray(x4_f.T).reshape(1, S, D)


def build_moe_scat():
    core = Core("moescat")
    P = core.P
    stin_d = core.inp("stin", [ESLOTS_PAD, 2], I32)
    esl_d = core.inp("eslsl", [128, 16, 2], I32)
    tok_d = core.inp("tokb", [128, 16], F32)
    stok_o = core.out("stok", [ESLOTS_PAD, 2], I32)
    consts = dev_consts(core)
    with core.phase():
        srow = core.sb("srow", [1, 4096], I32)
        for cch in range(ESLOTS_PAD * 2 // 4096):
            P.dma("sync", srow[:],
                  stin_d.rearrange("a b -> (a b)")[cch * 4096:(cch + 1) * 4096].unsqueeze(0),
                  reads=[("srow",)], writes=[("srow",)], tag="ld_st")
            P.dma("gpsimd", stok_o.rearrange("a b -> (a b)")[cch * 4096:(cch + 1) * 4096]
                  .unsqueeze(0), srow[:], reads=[("srow",)],
                  writes=[("stok", cch)], tag="st_st")
        esl = core.sb("esl_sb", [128, 16, 2], I32)
        P.dma("sync", esl[:], esl_d, writes=[("esl",)], tag="ld_esl")
        tokb = core.sb("tokb_sb", [128, 16], F32)
        P.dma("sync", tokb[:], tok_d, writes=[("tokb",)], tag="ld_tok")
        tki = core.sb("tki2", [128, 2], I32)
        for tt in range(16):
            for k in range(2):
                P.op("vector", v_copy(tki[:, 0:1], tokb[:, tt:tt + 1]),
                     reads=[("tokb",), ("tki",)], writes=[("tki",)], tag="s_t0")
                P.op("vector", v_copy(tki[:, 1:2], tokb[:, tt:tt + 1]),
                     reads=[("tokb",), ("tki",)], writes=[("tki",)], tag="s_t1")
                P.op("gpsimd", (lambda eng, o=stok_o, idxap=esl[:, tt, k:k + 1],
                                inap=tki[:, 0:2]:
                                eng.indirect_dma_start(o, IndirectOffsetOnAxis(idxap, 0),
                                                       inap, None,
                                                       bounds_check=ESLOTS_PAD - 1,
                                                       oob_is_err=False)),
                     reads=[("tki",), ("esl",)] + [("stok", cc) for cc in
                                                   range(ESLOTS_PAD * 2 // 4096)],
                     writes=[("stok", 0)], dma=True, tag="s_sc")
    return core
